# version 21
# baseline (speedup 1.0000x reference)
import math
import contextlib
import numpy as np
import ml_dtypes
import concourse.bass as bass
import concourse.mybir as mybir
from concourse.bass_utils import run_bass_kernel_spmd

F32 = mybir.dt.float32
BF16 = mybir.dt.bfloat16
AF = mybir.ActivationFunctionType
AX = mybir.AxisListType
ALU = mybir.AluOpType

B, T, D = 4, 2048, 2048
H, DH = 16, 128
HID = 8192
TO = 1024
NT = 8
GT = 16
DC = 16
QK_SCALE = 1.0 / math.sqrt(DH)
NEG = -1.0e30

ENGINES = ("sync", "gpsimd", "tensor", "scalar", "vector")


def build_nc():
    nc = bass.Bass("TRN2", target_bir_lowering=False)

    x_d = nc.dram_tensor("x", [T, D], F32, kind="ExternalInput").ap()
    x_own_d = nc.dram_tensor("x_own", [TO, D], F32, kind="ExternalInput").ap()
    wq_d = nc.dram_tensor("wq_l", [16, 128, 2048], F32, kind="ExternalInput").ap()
    wk_d = nc.dram_tensor("wk_l", [128, 2048], F32, kind="ExternalInput").ap()
    wv_d = nc.dram_tensor("wv_l", [128, 2048], F32, kind="ExternalInput").ap()
    wo_d = nc.dram_tensor("wo_l", [4, 128, 8192], F32, kind="ExternalInput").ap()
    wg_d = nc.dram_tensor("wg_l", [16, 128, 8192], F32, kind="ExternalInput").ap()
    wu_d = nc.dram_tensor("wu_l", [16, 128, 8192], F32, kind="ExternalInput").ap()
    wd_d = nc.dram_tensor("wd_l", [16, 128, 8192], F32, kind="ExternalInput").ap()
    bo_d = nc.dram_tensor("bo_bc", [128, 2048], F32, kind="ExternalInput").ap()
    mask_d = nc.dram_tensor("masks", [2, 128, 512], BF16, kind="ExternalInput").ap()
    id_d = nc.dram_tensor("ident", [128, 128], BF16, kind="ExternalInput").ap()
    y_d = nc.dram_tensor("y", [TO, D], F32, kind="ExternalOutput").ap()

    ctx = contextlib.ExitStack()
    with ctx:
        y_sb = ctx.enter_context(nc.sbuf_tensor("y_sb", [128, NT * 2048], F32))
        xtS = ctx.enter_context(nc.sbuf_tensor("xtS", [128, 2 * 2048], F32))
        misc = ctx.enter_context(nc.sbuf_tensor("misc", [128, 2048], F32))
        stats = ctx.enter_context(nc.sbuf_tensor("stats", [128, 72], F32))

        xnT = ctx.enter_context(nc.sbuf_tensor("xnT", [128, DC * TO], BF16))
        qT = ctx.enter_context(nc.sbuf_tensor("qT", [128, 16 * TO], BF16))
        pool = ctx.enter_context(nc.sbuf_tensor("pool", [128, 26624], BF16))
        id_sb = ctx.enter_context(nc.sbuf_tensor("id_sb", [128, 128], BF16))
        masks_sb = ctx.enter_context(nc.sbuf_tensor("masks_sb", [128, 1024], BF16))

        P_XNT_GRP, P_XN_NAT, P_WQ, P_WK, P_WV, P_SQ, P_KT, P_VSB = (
            0, 8192, 12288, 16384, 18432, 20480, 22528, 24576)
        P_PSCALED, P_PT = 0, 16384
        P_WO = 8192
        P_WG, P_WU, P_WD = 0, 8192, 16384

        scores4 = ctx.enter_context(nc.psum_tensor("scores4", [128, 2048], F32))
        psA = scores4[:, 0:512]
        psB = scores4[:, 512:1024]
        psC = scores4[:, 1024:1536]
        psD = scores4[:, 1536:2048]
        psE = ctx.enter_context(nc.psum_tensor("psE", [128, 512], F32))
        psF = ctx.enter_context(nc.psum_tensor("psF", [128, 512], F32))
        psT0 = ctx.enter_context(nc.psum_tensor("psT0", [128, 1024], BF16))
        psT1 = ctx.enter_context(nc.psum_tensor("psT1", [128, 1024], BF16))

        _sem_names = ["LCON", "LYI", "LMK", "LXT0", "LXT1", "LWK", "LWQ0", "LWQ1",
                      "LWO0", "LWO1", "LWG", "LWU", "LWD", "SO", "GPX",
                      "PE", "AC", "VE"]
        SEM = {nm: ctx.enter_context(nc.semaphore(nm)) for nm in _sem_names}

        prog = {e: [] for e in ENGINES}
        cnt = {k: 0 for k in SEM}

        def emit(eng, fn):
            prog[eng].append(fn)

        def wait(eng, sem, val):
            if val and val > 0:
                emit(eng, lambda E, s=SEM[sem], v=val: E.wait_ge(s, v))

        def emit_i(eng, fn, sem=None, by=1):
            if sem is None:
                emit(eng, fn)
                return None
            cnt[sem] += by
            v = cnt[sem]
            emit(eng, lambda E, f=fn, s=SEM[sem], b=by: f(E).then_inc(s, b))
            return v

        LAG = 0

        emit_i("sync", lambda E: E.dma_start(out=id_sb[:, :], in_=id_d), "LCON", 16)
        lx_const = cnt["LCON"]
        emit_i("gpsimd", lambda E: E.memset(stats[:, 64:65], 1.0e-5), "GPX")
        gp_eps = cnt["GPX"]
        eps_ap = stats[:, 64:65]

        wk_sb = pool[:, P_WK:P_WK + 2048]
        wv_sb = pool[:, P_WV:P_WV + 2048]
        emit_i("gpsimd", lambda E: E.dma_start(out=wk_sb, in_=wk_d), "LWK", 16)
        emit_i("gpsimd", lambda E: E.dma_start(out=wv_sb, in_=wv_d), "LWK", 16)
        lw_kv = cnt["LWK"]

        emit_i("scalar", lambda E: E.dma_start(out=misc[:, 0:2048], in_=bo_d), "LYI", 16)
        for tt in range(NT):
            emit_i("scalar", lambda E, tt=tt: E.dma_start(
                out=y_sb[:, tt * 2048:(tt + 1) * 2048],
                in_=x_own_d[tt * 128:(tt + 1) * 128, :]), "LYI", 16)

        emit_i("gpsimd", lambda E: E.dma_start(out=masks_sb[:, 0:512], in_=mask_d[0]), "LMK", 16)
        emit_i("gpsimd", lambda E: E.dma_start(out=masks_sb[:, 512:1024], in_=mask_d[1]), "LMK", 16)
        lx_masks = cnt["LMK"]
        mask_E = masks_sb[:, 0:512]
        mask_L = masks_sb[:, 512:1024]

        xt2 = [xtS[:, 0:2048], xtS[:, 2048:4096]]
        sq = pool[:, P_SQ:P_SQ + 2048]
        xn_nat = [pool[:, P_XN_NAT + i * 2048: P_XN_NAT + (i + 1) * 2048] for i in range(2)]
        xnT_grp = pool[:, P_XNT_GRP:P_XNT_GRP + 8192]
        kT = pool[:, P_KT:P_KT + 2048]
        v_sb = pool[:, P_VSB:P_VSB + 2048]

        def stat(slot, col):
            return stats[:, slot * 8 + col: slot * 8 + col + 1]

        ac_norm_done = 0
        ac_norm_hist = {}
        ac_norm_half = {}
        ac_sq_hist = {}
        pe_trans_tile = {}
        psT_war = {0: 0, 1: 0}
        ve_xnT_own_last = 0
        ve_kT_evac = 0
        ve_v_evac = {0: 0, 1: 0}
        ve_grp_evac_last = 0

        def ln_s1(gt):
            sl = gt % 2
            xt = xt2[gt % 2]
            wait("sync", "AC", ac_norm_hist.get(gt - 2, 0))
            lxs = "LXT0" if gt % 2 == 0 else "LXT1"
            lxv = emit_i("sync", lambda E, gt=gt, xt=xt: E.dma_start(
                out=xt, in_=x_d[gt * 128:(gt + 1) * 128, :]), lxs, 16)
            wait("vector", lxs, lxv)
            ve_sum = emit_i("vector", lambda E, sl=sl, xt=xt: E.reduce_sum(
                stat(sl, 0), xt, axis=AX.X), "VE")
            wait("scalar", lxs, lxv)
            wait("scalar", "AC", ac_sq_hist.get(gt - 1, 0))
            ac_sq = emit_i("scalar", lambda E, sl=sl, xt=xt: E.activation(
                sq, xt, AF.Square, accum_out=stat(sl, 1)), "AC")
            ac_sq_hist[gt] = ac_sq
            return dict(gt=gt, sl=sl, xt=xt, ve_sum=ve_sum, ac_sq=ac_sq)

        def ln_s2(st1):
            nonlocal ac_norm_done
            gt, sl, xt = st1["gt"], st1["sl"], st1["xt"]
            wait("vector", "VE", st1["ve_sum"])
            ve_m = emit_i("vector", lambda E, sl=sl: E.tensor_scalar_mul(
                stat(sl, 2), stat(sl, 0), 1.0 / D), "VE")
            wait("vector", "VE", ve_m)
            ve_msq = emit_i("vector", lambda E, sl=sl: E.tensor_mul(
                stat(sl, 3), stat(sl, 2), stat(sl, 2)), "VE")
            wait("vector", "AC", st1["ac_sq"])
            wait("vector", "VE", ve_msq)
            ve_var = emit_i("vector", lambda E, sl=sl: E.scalar_tensor_tensor(
                stat(sl, 4), stat(sl, 1), 1.0 / D, stat(sl, 3),
                ALU.mult, ALU.subtract), "VE")
            wait("scalar", "VE", ve_var)
            if gt == 0:
                wait("scalar", "GPX", gp_eps)
            ac_ln = emit_i("scalar", lambda E, sl=sl: E.activation(
                stat(sl, 5), stat(sl, 4), AF.Ln, bias=eps_ap), "AC")
            wait("scalar", "AC", ac_ln)
            ac_rstd = emit_i("scalar", lambda E, sl=sl: E.activation(
                stat(sl, 6), stat(sl, 5), AF.Exp, scale=-0.5), "AC")
            wait("vector", "AC", ac_rstd)
            ve_negmr = emit_i("vector", lambda E, sl=sl: E.tensor_scalar(
                stat(sl, 7), stat(sl, 2), stat(sl, 6), -1.0,
                ALU.mult, ALU.mult), "VE")
            wait("scalar", "PE", pe_trans_tile.get(gt - 2, 0))
            wait("scalar", "VE", ve_negmr)
            ac_norm_h0 = emit_i("scalar", lambda E, sl=sl, gt=gt, xt=xt: E.activation(
                xn_nat[gt % 2][:, 0:1024], xt[:, 0:1024], AF.Identity,
                bias=stat(sl, 7), scale=stat(sl, 6)), "AC")
            ac_norm_done = emit_i("scalar", lambda E, sl=sl, gt=gt, xt=xt: E.activation(
                xn_nat[gt % 2][:, 1024:2048], xt[:, 1024:2048], AF.Identity,
                bias=stat(sl, 7), scale=stat(sl, 6)), "AC")
            ac_norm_hist[gt] = ac_norm_done
            ac_norm_half[gt] = (ac_norm_h0, ac_norm_done)

        def ln_s3(gt):
            nonlocal ve_grp_evac_last, ve_xnT_own_last
            ti = gt % 4
            if gt == 0:
                wait("tensor", "LCON", lx_const)
            for half in range(2):
                wait("tensor", "AC", ac_norm_half[gt][half])
                pst = psT0 if half == 0 else psT1
                wait("tensor", "VE", psT_war[half])
                last = None
                for q8 in range(8):
                    dc = half * 8 + q8
                    def f(E, pst=pst, q8=q8, dc=dc, gt=gt):
                        return E.transpose(
                            pst[:, q8 * 128:(q8 + 1) * 128],
                            xn_nat[gt % 2][:, dc * 128:(dc + 1) * 128],
                            id_sb[:, :])
                    if q8 < 7:
                        emit_i("tensor", f)
                    else:
                        last = emit_i("tensor", f, "PE")
                pe_trans_tile[gt] = last
                wait("vector", "PE", last + LAG)
                dst_full = xnT_grp.rearrange("p (dc t) -> p dc t", dc=16)[
                    :, half * 8:(half + 1) * 8, ti * 128:(ti + 1) * 128]
                src_full = pst.rearrange("p (dc t) -> p dc t", dc=8)
                emit_i("vector", lambda E, d=dst_full, s=src_full: E.tensor_copy(d, s), "VE")
                dst_own = xnT.rearrange("p (dc t) -> p dc t", dc=16)[
                    :, half * 8:(half + 1) * 8, gt * 64:gt * 64 + 64]
                src_own = pst.rearrange("p (dc t two) -> p dc t two", dc=8, two=2)[:, :, :, 0]
                ev = emit_i("vector", lambda E, d=dst_own, s=src_own: E.tensor_copy(d, s), "VE")
                psT_war[half] = ev
                ve_grp_evac_last = ev
                if gt == 7:
                    ve_xnT_th0[0] = ev
                if gt == GT - 1:
                    ve_xnT_own_last = ev

        def ln_kv(g4):
            nonlocal ve_kT_evac
            wait("tensor", "LWK", lw_kv)
            wait("tensor", "VE", ve_grp_evac_last)
            wait("tensor", "VE", ve_kT_evac)
            last = None
            for dc in range(DC):
                def f(E, dc=dc):
                    return E.matmul(psC[:, :],
                                    wk_sb[:, dc * 128:(dc + 1) * 128],
                                    xnT_grp[:, dc * 512:(dc + 1) * 512],
                                    start=(dc == 0), stop=(dc == DC - 1))
                if dc < DC - 1:
                    emit_i("tensor", f)
                else:
                    last = emit_i("tensor", f, "PE")
            wait("vector", "PE", last + LAG)
            ve_kT_evac = emit_i("vector", lambda E, g4=g4: E.tensor_copy(
                kT[:, g4 * 512:(g4 + 1) * 512], psC[:, :]), "VE")
            for ti in range(4):
                jt = g4 * 4 + ti
                vps = psD if jt % 2 == 0 else psE
                wait("tensor", "VE", ve_v_evac[jt % 2])
                last = None
                for dc in range(DC):
                    def f(E, dc=dc, ti=ti, vps=vps):
                        return E.matmul(vps[:, 0:128],
                                        xnT_grp[:, dc * 512 + ti * 128: dc * 512 + (ti + 1) * 128],
                                        wv_sb[:, dc * 128:(dc + 1) * 128],
                                        start=(dc == 0), stop=(dc == DC - 1))
                    if dc < DC - 1:
                        emit_i("tensor", f)
                    else:
                        last = emit_i("tensor", f, "PE")
                wait("vector", "PE", last + LAG)
                ve_v_evac[jt % 2] = emit_i("vector", lambda E, jt=jt, vps=vps: E.tensor_copy(
                    v_sb[:, jt * 128:(jt + 1) * 128], vps[:, 0:128]), "VE")

        wq_sb = [pool[:, P_WQ + i * 2048: P_WQ + (i + 1) * 2048] for i in range(2)]
        pe_wq_done = {}
        pe_wq_buf_last = {0: 0, 1: 0}
        ve_q_evac = {}
        psF_war = [("VE", 0)]
        ve_xnT_th0 = [0]
        q1_first = [True]

        def q_proj_half(ec, th):
            lws = "LWQ0" if ec % 2 == 0 else "LWQ1"
            wait("gpsimd", "PE", pe_wq_buf_last[ec % 2])
            lw = emit_i("gpsimd", lambda E, ec=ec: E.dma_start(
                out=wq_sb[ec % 2], in_=wq_d[ec]), lws, 16)
            wait("tensor", lws, lw)
            if th == 0:
                wait("tensor", "VE", ve_xnT_th0[0])
            elif q1_first[0]:
                wait("tensor", "VE", ve_xnT_own_last)
                q1_first[0] = False
            weng, wv = psF_war[0]
            wait("tensor", weng, wv)
            last = None
            for dc in range(DC):
                def f(E, ec=ec, th=th, dc=dc):
                    return E.matmul(psF[:, :],
                                    wq_sb[ec % 2][:, dc * 128:(dc + 1) * 128],
                                    xnT[:, dc * 1024 + th * 512: dc * 1024 + (th + 1) * 512],
                                    start=(dc == 0), stop=(dc == DC - 1))
                if dc < DC - 1:
                    emit_i("tensor", f)
                else:
                    last = emit_i("tensor", f, "PE")
            pe_wq_done[ec] = last
            pe_wq_buf_last[ec % 2] = last
            wait("vector", "PE", last + LAG)
            ve = emit_i("vector", lambda E, ec=ec, th=th: E.tensor_scalar_mul(
                qT[:, ec * 1024 + th * 512: ec * 1024 + (th + 1) * 512],
                psF[:, :], QK_SCALE), "VE")
            psF_war[0] = ("VE", ve)
            ve_q_evac[ec] = ve

        st1s = {}
        for gt in range(GT + 1):
            if gt >= 1:
                ln_s2(st1s[gt - 1])
            if gt < GT:
                st1s[gt] = ln_s1(gt)
            if gt >= 1:
                ln_s3(gt - 1)
                if (gt - 1) % 4 == 3:
                    ln_kv((gt - 1) // 4)
            if gt >= 9:
                q_proj_half(2 * (gt - 9), 0)
                q_proj_half(2 * (gt - 9) + 1, 0)

        ac_norm_final = ac_norm_done

        wait("vector", "LYI", cnt["LYI"])
        ve_after_yinit = 0
        for tt in range(NT):
            ysl = y_sb[:, tt * 2048:(tt + 1) * 2048]
            ve_after_yinit = emit_i("vector", lambda E, ysl=ysl: E.tensor_add(
                ysl, ysl, misc[:, 0:2048]), "VE")

        probsT = pool[:, P_PT:P_PT + 4096]

        wait("tensor", "VE", ve_kT_evac)
        wait("tensor", "VE", max(ve_v_evac.values()))
        wait("tensor", "LMK", lx_masks)

        sc_banks = [psA, psB, psC, psD]
        sc_ring = [0, 1, 2, 3]
        sc_war = {0: ("VE", 0), 1: ("VE", 0), 2: ("VE", 0), 3: ("VE", 0)}
        cidx = 0
        pstW = {0: ("VE", psT_war[0]), 1: ("VE", psT_war[1])}
        av_banks = {0: psE, 1: psF}
        av_war0 = [("VE", max(ve_v_evac.values()))]

        pe_trans_slot = {}
        ve_avevac_last = 0
        ac_avevac_last = 0
        ac_exp_done = {}
        ve_partials_read = {}

        def p2_stat_mk(par, u):
            base = 16 + (par * 2 + u % 2) * 12
            def f(col):
                return stats[:, base + col: base + col + 1]
            return f, base

        probs_scaled6 = [pool[:, P_PSCALED + i * 2048: P_PSCALED + (i + 1) * 2048]
                         for i in range(6)]

        def stageA1(u, h, g):
            nonlocal cidx
            E_ext = 512 * (g + 1)
            C = g + 1
            out = dict(u=u, h=h, g=g, E_ext=E_ext, njt=E_ext // 128, deps=[],
                       exps=[])
            wait("tensor", "VE", ve_q_evac[h])
            for par, tk in enumerate((2 * g, 2 * g + 1)):
                slot = (u % 3) * 2 + par
                pslot = probs_scaled6[slot]
                ps, psbase = p2_stat_mk(par, u)
                ac_exp = 0
                slot0 = cidx % 4
                cpes = []
                for c in range(C):
                    bi = sc_ring[cidx % 4]
                    weng, wv = sc_war[bi]
                    wait("tensor", weng, wv)
                    diag = (c == C - 1)
                    def f(E, h=h, tk=tk, c=c, bank=sc_banks[bi], diag=diag):
                        return E.matmul(bank[:, :],
                                        qT[:, h * 1024 + tk * 128: h * 1024 + (tk + 1) * 128],
                                        kT[:, c * 512:(c + 1) * 512],
                                        start=True, stop=not diag)
                    if not diag:
                        cpe = emit_i("tensor", f, "PE")
                    else:
                        emit_i("tensor", f)
                        m = mask_E if (tk % 2 == 0) else mask_L
                        def fm(E, bank=sc_banks[bi], m=m):
                            return E.matmul(bank[:, :], id_sb[:, :], m,
                                            start=False, stop=True)
                        cpe = emit_i("tensor", fm, "PE")
                    cpes.append(cpe)
                    cidx += 1
                    if c == 0:
                        wait("vector", "PE", cpe + LAG)
                        wait("vector", "AC", ac_exp_done.get((par, u % 2), 0))
                        ve_negm = emit_i("vector", lambda E, ps=ps, b0=sc_banks[bi]: E.reduce_max(
                            ps(8), b0[:, :], axis=AX.X, negate=True), "VE")
                runs = []
                c = 0
                while c < C:
                    rlen = min(C - c, 4 - (slot0 + c) % 4)
                    runs.append((c, rlen))
                    c += rlen
                for ri, (c0, rlen) in enumerate(runs):
                    wait("scalar", "PE", cpes[c0 + rlen - 1] + LAG)
                    if ri == 0:
                        wait("scalar", "VE", ve_negm)
                        wait("scalar", "PE", pe_trans_slot.get(slot, 0))
                        wait("scalar", "VE", ve_partials_read.get((par, u % 2), 0))
                    sb = (slot0 + c0) % 4
                    def fx(E, c0=c0, rlen=rlen, sb=sb, ps=ps, pslot=pslot, ri=ri):
                        return E.activation(
                            pslot[:, c0 * 512:(c0 + rlen) * 512],
                            scores4[:, sb * 512:(sb + rlen) * 512], AF.Exp,
                            bias=ps(8), accum_out=ps(ri))
                    ac_exp = emit_i("scalar", fx, "AC")
                    for k in range(rlen):
                        sc_war[(sb + k)] = ("AC", ac_exp)
                ac_exp_done[(par, u % 2)] = ac_exp
                out["exps"].append((par, slot, pslot, ps, psbase, ac_exp, len(runs)))
            return out

        def stageA2(info):
            u = info["u"]
            E_ext, C = info["E_ext"], info["g"] + 1
            for (par, slot, pslot, ps, psbase, ac_exp, nruns) in info["exps"]:
                wait("vector", "AC", ac_exp)
                if nruns > 1:
                    ve_z = emit_i("vector", lambda E, ps=ps, psbase=psbase, nruns=nruns: E.reduce_sum(
                        ps(9), stats[:, psbase: psbase + nruns], axis=AX.X), "VE")
                    zcol = 9
                    wait("vector", "VE", ve_z)
                    ve_partials_read[(par, u % 2)] = ve_z
                else:
                    zcol = 0
                ve_zrec = emit_i("vector", lambda E, ps=ps, zcol=zcol: E.reciprocal(
                    ps(10), ps(zcol)), "VE")
                if nruns == 1:
                    ve_partials_read[(par, u % 2)] = ve_zrec
                wait("vector", "VE", ve_zrec)
                ve_scale = emit_i("vector", lambda E, E_ext=E_ext, pslot=pslot, ps=ps: E.tensor_scalar_mul(
                    pslot[:, 0:E_ext], pslot[:, 0:E_ext], ps(10)), "VE")
                info["deps"].append((slot, ve_scale))

        def stageB(info):
            nonlocal ve_avevac_last, ac_avevac_last
            u, h, g, njt = info["u"], info["h"], info["g"], info["njt"]
            ve_evac_u = 0
            for par in range(2):
                slot, ve_scale = info["deps"][par]
                wait("tensor", "VE", ve_scale)
                nbatches = (njt + 7) // 8
                tb = []
                for b in range(nbatches):
                    bank = ((u * 2 + par) % 2) if nbatches == 1 else b % 2
                    pst = psT0 if bank == 0 else psT1
                    weng, wv = pstW[bank]
                    wait("tensor", weng, wv)
                    n_in = min(8, njt - b * 8)
                    last = None
                    for q8 in range(n_in):
                        jt = b * 8 + q8
                        def f(E, pst=pst, q8=q8, jt=jt, slot=slot):
                            return E.transpose(
                                pst[:, q8 * 128:(q8 + 1) * 128],
                                probs_scaled6[slot][:, jt * 128:(jt + 1) * 128],
                                id_sb[:, :])
                        if q8 < n_in - 1:
                            emit_i("tensor", f)
                        else:
                            last = emit_i("tensor", f, "PE")
                    tb.append((last, bank, b, n_in))
                    pe_trans_slot[slot] = last
                for (last, bank, b, n_in) in tb:
                    pst = psT0 if bank == 0 else psT1
                    wait("vector", "PE", last + LAG)
                    dst = probsT.rearrange("p (jt i) -> p jt i", jt=16)[
                        :, b * 8: b * 8 + n_in, par * 128:(par + 1) * 128]
                    srcp = pst.rearrange("p (jt i) -> p jt i", jt=8)[:, 0:n_in, :]
                    ev = emit_i("vector", lambda E, d=dst, s=srcp: E.tensor_copy(d, s), "VE")
                    ve_evac_u = ev
                    pstW[bank] = ("VE", ev)
            bank_i = u % 2
            avp = av_banks[bank_i]
            weng, wv = av_war0[0] if bank_i == 0 else psF_war[0]
            wait("tensor", weng, wv)
            wait("tensor", "VE", ve_evac_u)
            last = None
            for jt in range(njt):
                def f(E, jt=jt, avp=avp, njt=njt):
                    return E.matmul(avp[:, 0:256],
                                    v_sb[:, jt * 128:(jt + 1) * 128],
                                    probsT[:, jt * 256:(jt + 1) * 256],
                                    start=(jt == 0), stop=(jt == njt - 1))
                if jt < njt - 1:
                    emit_i("tensor", f)
                else:
                    last = emit_i("tensor", f, "PE")
            wait("scalar", "PE", last + LAG)
            ac_avevac_last = emit_i("scalar", lambda E, h=h, g=g, avp=avp: E.copy(
                qT[:, h * 1024 + g * 256: h * 1024 + (g + 1) * 256], avp[:, 0:256]), "AC")
            if bank_i == 0:
                av_war0[0] = ("AC", ac_avevac_last)
            else:
                psF_war[0] = ("AC", ac_avevac_last)

        PIPE_LAG = 3
        q_proj_half(0, 1)
        q_proj_half(1, 1)
        infos = []
        units = [(h, g) for h in range(H) for g in range(4)]
        for u, (h, g) in enumerate(units):
            if h + 2 < H and g == 0:
                q_proj_half(h + 2, 1)
            if u >= PIPE_LAG:
                stageB(infos[u - PIPE_LAG])
            infos.append(stageA1(u, h, g))
            if u >= 1:
                stageA2(infos[u - 1])
        stageA2(infos[-1])
        for uu in range(len(units) - PIPE_LAG, len(units)):
            stageB(infos[uu])

        pe_p2_end = cnt["PE"]
        ve_p2_end = cnt["VE"]
        pe_wo0_ok = max(pe_trans_slot.get(4, 0), pe_trans_slot.get(5, 0),
                        max(pe_wq_done.values()))

        wo_sb = [pool[:, P_WO + i * 8192: P_WO + (i + 1) * 8192] for i in range(2)]
        wo_ps = [psA, psB]
        pe_wo_done = {}
        ve_wo_evac = {0: ve_p2_end, 1: ve_p2_end}
        first_wo = True
        for es in range(4):
            if es == 0:
                wait("gpsimd", "PE", pe_wo0_ok)
            elif es == 1:
                wait("gpsimd", "PE", pe_p2_end)
            else:
                wait("gpsimd", "PE", pe_wo_done[es - 2])
            lw = emit_i("gpsimd", lambda E, es=es: E.dma_start(
                out=wo_sb[es % 2], in_=wo_d[es]), "LWO0" if es % 2 == 0 else "LWO1", 16)
            wait("tensor", "LWO0" if es % 2 == 0 else "LWO1", lw)
            if first_wo:
                wait("tensor", "AC", ac_avevac_last)
                first_wo = False
            for tt in range(NT):
                ps = wo_ps[tt % 2]
                wait("tensor", "VE", ve_wo_evac[tt % 2])
                last = None
                for cc in range(16):
                    def f(E, es=es, tt=tt, cc=cc, ps=ps):
                        return E.matmul(ps[:, :],
                                        qT[:, cc * 1024 + tt * 128: cc * 1024 + (tt + 1) * 128],
                                        wo_sb[es % 2][:, cc * 512:(cc + 1) * 512],
                                        start=(cc == 0), stop=(cc == 15))
                    if cc < 15:
                        emit_i("tensor", f)
                    else:
                        last = emit_i("tensor", f, "PE")
                pe_wo_done[es] = last
                wait("vector", "PE", last + LAG)
                ysl = y_sb[:, tt * 2048 + es * 512: tt * 2048 + (es + 1) * 512]
                ve_wo_evac[tt % 2] = emit_i("vector", lambda E, ysl=ysl, ps=ps: E.tensor_add(
                    ysl, ysl, ps[:, :]), "VE")

        ve_p3_end = cnt["VE"]
        pe_p3_end = cnt["PE"]

        wg_sb = pool[:, P_WG:P_WG + 8192]
        wu_sb = pool[:, P_WU:P_WU + 8192]
        wd_sb = pool[:, P_WD:P_WD + 8192]
        hact = qT[:, 0:4096]
        gs = [qT[:, 4096 + i * 512: 4096 + (i + 1) * 512] for i in range(2)]

        pe_gate_done = {}
        pe_down_done = {}
        ve_down_evac = {0: ve_p3_end, 1: ve_p3_end}
        ve_final_tt = {}
        ac_silu_prev = {0: 0, 1: 0}
        ve_hact = {}
        GU_banks = [(psA, psB), (psC, psD)]
        D_banks2 = {0: (psE, psF), 1: (psA, psB)}
        first_gate = True
        first_down = True

        for s in range(16):
            if s == 0:
                wait("gpsimd", "PE", pe_p2_end)
            else:
                wait("gpsimd", "PE", pe_gate_done[s - 1])
            lw_gs = emit_i("gpsimd", lambda E, s=s: E.dma_start(out=wg_sb, in_=wg_d[s]), "LWG", 16)
            if s == 0:
                wait("gpsimd", "PE", pe_wo_done[2])
            lw_us = emit_i("gpsimd", lambda E, s=s: E.dma_start(out=wu_sb, in_=wu_d[s]), "LWU", 16)
            if s == 0:
                wait("gpsimd", "PE", pe_p3_end)
            else:
                wait("gpsimd", "PE", pe_down_done[s - 1])
            lw_ds = emit_i("gpsimd", lambda E, s=s: E.dma_start(out=wd_sb, in_=wd_d[s]), "LWD", 16)

            wait("tensor", "LWG", lw_gs)
            wait("tensor", "LWU", lw_us)
            for hq in range(4):
                for th in range(2):
                    pair = hq * 2 + th
                    gb, ub = GU_banks[pair % 2]
                    if first_gate:
                        wait("tensor", "VE", ve_p3_end)
                        first_gate = False
                    if pair < 2 and s > 0:
                        wait("tensor", "VE", ve_down_evac[1] if pair == 0
                             else ve_hact.get((s - 1, 7), 0))
                    wait("tensor", "AC", ac_silu_prev[pair % 2])
                    wait("tensor", "VE", ve_hact.get((s, pair - 2),
                                                     ve_hact.get((s - 1, 6 + pair), 0)))
                    lastg = None
                    for dc in range(DC):
                        def f(E, s=s, hq=hq, th=th, dc=dc, gb=gb):
                            return E.matmul(gb[:, :],
                                            wg_sb[:, dc * 512 + hq * 128: dc * 512 + (hq + 1) * 128],
                                            xnT[:, dc * 1024 + th * 512: dc * 1024 + (th + 1) * 512],
                                            start=(dc == 0), stop=(dc == DC - 1))
                        if dc < DC - 1:
                            emit_i("tensor", f)
                        else:
                            lastg = emit_i("tensor", f, "PE")
                    lastu = None
                    for dc in range(DC):
                        def f(E, s=s, hq=hq, th=th, dc=dc, ub=ub):
                            return E.matmul(ub[:, :],
                                            wu_sb[:, dc * 512 + hq * 128: dc * 512 + (hq + 1) * 128],
                                            xnT[:, dc * 1024 + th * 512: dc * 1024 + (th + 1) * 512],
                                            start=(dc == 0), stop=(dc == DC - 1))
                        if dc < DC - 1:
                            emit_i("tensor", f)
                        else:
                            lastu = emit_i("tensor", f, "PE")
                    pe_gate_done[s] = lastu
                    wait("scalar", "PE", lastg + LAG)
                    ac_silu = emit_i("scalar", lambda E, pair=pair, gb=gb: E.activation(
                        gs[pair % 2], gb[:, :], AF.Silu), "AC")
                    ac_silu_prev[pair % 2] = ac_silu
                    wait("vector", "AC", ac_silu)
                    wait("vector", "PE", lastu + LAG)
                    dsth = hact[:, hq * 1024 + th * 512: hq * 1024 + (th + 1) * 512]
                    ve_hact[(s, pair)] = emit_i("vector", lambda E, d=dsth, pair=pair, ub=ub: E.tensor_mul(
                        d, gs[pair % 2], ub[:, :]), "VE")

            wait("tensor", "LWD", lw_ds)
            wait("tensor", "VE", ve_hact[(s, 7)])
            for tt in range(NT):
                for e2 in range(2):
                    b0, b1 = D_banks2[e2]
                    if first_down:
                        wait("tensor", "VE", ve_p3_end)
                        first_down = False
                    if tt == 0 and e2 == 1:
                        wait("tensor", "VE", ve_hact[(s, 7)])
                        wait("tensor", "AC", ac_silu_prev[0])
                    wait("tensor", "VE", ve_down_evac[e2])
                    last = None
                    for hq in range(4):
                        for ei, bank in ((0, b0), (1, b1)):
                            es = e2 * 2 + ei
                            def f(E, s=s, tt=tt, hq=hq, es=es, bank=bank):
                                return E.matmul(bank[:, :],
                                                hact[:, hq * 1024 + tt * 128: hq * 1024 + (tt + 1) * 128],
                                                wd_sb[:, hq * 2048 + es * 512: hq * 2048 + (es + 1) * 512],
                                                start=(hq == 0), stop=(hq == 3))
                            if hq == 3 and ei == 1:
                                last = emit_i("tensor", f, "PE")
                            else:
                                emit_i("tensor", f)
                    pe_down_done[s] = last
                    wait("vector", "PE", last + LAG)
                    for ei, bank in ((0, b0), (1, b1)):
                        es = e2 * 2 + ei
                        ysl = y_sb[:, tt * 2048 + es * 512: tt * 2048 + (es + 1) * 512]
                        ve_down_evac[e2] = emit_i(
                            "vector",
                            lambda E, ysl=ysl, bank=bank: E.tensor_add(ysl, ysl, bank[:, :]),
                            "VE")
                        if s == 15:
                            ve_final_tt[tt] = ve_down_evac[e2]

        wait("tensor", "VE", cnt["VE"])
        wait("tensor", "AC", cnt["AC"])
        for _ in range(LAG + 1):
            emit_i("tensor", lambda E: E.matmul(
                psA[:, 0:128], id_sb[:, :], id_sb[:, :], start=True, stop=True), "PE")

        for tt in range(NT):
            wait("sync", "VE", ve_final_tt[tt])
            emit_i("sync", lambda E, tt=tt: E.dma_start(
                out=y_d[tt * 128:(tt + 1) * 128, :],
                in_=y_sb[:, tt * 2048:(tt + 1) * 2048]), "SO", 16)
        wait("sync", "SO", cnt["SO"])

        with nc.Block() as block:
            @block.sync
            def _(E):
                for f in prog["sync"]:
                    f(E)

            @block.gpsimd
            def _(E):
                for f in prog["gpsimd"]:
                    f(E)

            @block.tensor
            def _(E):
                for f in prog["tensor"]:
                    f(E)

            @block.scalar
            def _(E):
                for f in prog["scalar"]:
                    f(E)

            @block.vector
            def _(E):
                for f in prog["vector"]:
                    f(E)

    return nc


def _host_layouts(Wq, Wk, Wv, Wo, Wg, Wu, Wd, bo):
    wq_l = np.ascontiguousarray(
        Wq.reshape(16, 128, 16, 128).transpose(2, 1, 0, 3).reshape(16, 128, 2048))
    wk_l = np.ascontiguousarray(Wk.reshape(16, 128, 128).transpose(1, 0, 2).reshape(128, 2048))
    wv_l = np.ascontiguousarray(Wv.reshape(16, 128, 128).transpose(1, 0, 2).reshape(128, 2048))
    wo_l = np.ascontiguousarray(
        Wo.reshape(16, 128, 4, 512).transpose(2, 1, 0, 3).reshape(4, 128, 8192))
    WgT = np.ascontiguousarray(Wg.T)
    WuT = np.ascontiguousarray(Wu.T)
    WdT = np.ascontiguousarray(Wd.T)
    wg_l = np.ascontiguousarray(
        WgT.reshape(16, 128, 16, 512).transpose(2, 1, 0, 3).reshape(16, 128, 8192))
    wu_l = np.ascontiguousarray(
        WuT.reshape(16, 128, 16, 512).transpose(2, 1, 0, 3).reshape(16, 128, 8192))
    wd_l = np.ascontiguousarray(
        WdT.reshape(16, 4, 128, 4, 512).transpose(0, 2, 1, 3, 4).reshape(16, 128, 8192))
    bo_bc = np.ascontiguousarray(np.broadcast_to(bo.reshape(1, -1), (128, 2048)))
    return dict(wq_l=wq_l, wk_l=wk_l, wv_l=wv_l, wo_l=wo_l,
                wg_l=wg_l, wu_l=wu_l, wd_l=wd_l, bo_bc=bo_bc)


def _masks(parity):
    i = np.arange(128)[:, None]
    jc = np.arange(512)[None, :]
    if parity == 0:
        key_true = jc
    else:
        key_true = np.where(jc % 2 == 0, jc + 1, jc - 1)
    mE = np.where(key_true <= 2 * i + parity, 0.0, NEG).astype(np.float32)
    mL = np.where(key_true <= 256 + 2 * i + parity, 0.0, NEG).astype(np.float32)
    return np.stack([mE, mL]).astype(ml_dtypes.bfloat16)


def kernel(L0_attn_x, L0_attn_Wq, L0_attn_Wk, L0_attn_Wv, L0_attn_Wo, L0_attn_bo,
           L0_ffn_w_gate, L0_ffn_w_up, L0_ffn_w_down):
    x = np.ascontiguousarray(np.asarray(L0_attn_x, dtype=np.float32))
    shared = _host_layouts(
        np.asarray(L0_attn_Wq, np.float32), np.asarray(L0_attn_Wk, np.float32),
        np.asarray(L0_attn_Wv, np.float32), np.asarray(L0_attn_Wo, np.float32),
        np.asarray(L0_ffn_w_gate, np.float32), np.asarray(L0_ffn_w_up, np.float32),
        np.asarray(L0_ffn_w_down, np.float32), np.asarray(L0_attn_bo, np.float32))
    ident = np.eye(128, dtype=ml_dtypes.bfloat16)
    masks = [_masks(0), _masks(1)]

    nc = build_nc()
    in_maps = []
    for core in range(8):
        b, p = core // 2, core % 2
        xb = x[b]
        xperm = np.ascontiguousarray(
            xb.reshape(1024, 2, 2048)[:, [p, 1 - p], :].reshape(2048, 2048))
        in_map = dict(shared)
        in_map["x"] = xperm
        in_map["x_own"] = np.ascontiguousarray(xb[p::2, :])
        in_map["masks"] = masks[p]
        in_map["ident"] = ident
        in_maps.append(in_map)

    res = run_bass_kernel_spmd(nc, in_maps, core_ids=list(range(8)))
    out = np.empty((B, T, D), dtype=np.float32)
    for core in range(8):
        b, p = core // 2, core % 2
        out[b, p::2, :] = res.results[core]["y"]
    return out


# version 22
# speedup vs baseline: 1.0157x; 1.0157x over previous
import math
import contextlib
import numpy as np
import ml_dtypes
import concourse.bass as bass
import concourse.mybir as mybir
from concourse.bass_utils import run_bass_kernel_spmd

F32 = mybir.dt.float32
BF16 = mybir.dt.bfloat16
AF = mybir.ActivationFunctionType
AX = mybir.AxisListType
ALU = mybir.AluOpType

B, T, D = 4, 2048, 2048
H, DH = 16, 128
HID = 8192
TO = 1024
NT = 8
GT = 16
DC = 16
QK_SCALE = 1.0 / math.sqrt(DH)
NEG = -1.0e30

ENGINES = ("sync", "gpsimd", "tensor", "scalar", "vector")


def build_nc():
    nc = bass.Bass("TRN2", target_bir_lowering=False)

    x_d = nc.dram_tensor("x", [T, D], F32, kind="ExternalInput").ap()
    x_own_d = nc.dram_tensor("x_own", [TO, D], F32, kind="ExternalInput").ap()
    wq_d = nc.dram_tensor("wq_l", [16, 128, 2048], F32, kind="ExternalInput").ap()
    wk_d = nc.dram_tensor("wk_l", [128, 2048], F32, kind="ExternalInput").ap()
    wv_d = nc.dram_tensor("wv_l", [128, 2048], F32, kind="ExternalInput").ap()
    wo_d = nc.dram_tensor("wo_l", [4, 128, 8192], F32, kind="ExternalInput").ap()
    wg_d = nc.dram_tensor("wg_l", [16, 128, 8192], F32, kind="ExternalInput").ap()
    wu_d = nc.dram_tensor("wu_l", [16, 128, 8192], F32, kind="ExternalInput").ap()
    wd_d = nc.dram_tensor("wd_l", [16, 128, 8192], F32, kind="ExternalInput").ap()
    bo_d = nc.dram_tensor("bo_bc", [128, 2048], F32, kind="ExternalInput").ap()
    mask_d = nc.dram_tensor("masks", [2, 128, 512], BF16, kind="ExternalInput").ap()
    id_d = nc.dram_tensor("ident", [128, 128], BF16, kind="ExternalInput").ap()
    y_d = nc.dram_tensor("y", [TO, D], F32, kind="ExternalOutput").ap()

    ctx = contextlib.ExitStack()
    with ctx:
        y_sb = ctx.enter_context(nc.sbuf_tensor("y_sb", [128, NT * 2048], F32))
        xtS = ctx.enter_context(nc.sbuf_tensor("xtS", [128, 2 * 2048], F32))
        misc = ctx.enter_context(nc.sbuf_tensor("misc", [128, 2048], F32))
        stats = ctx.enter_context(nc.sbuf_tensor("stats", [128, 72], F32))

        xnT = ctx.enter_context(nc.sbuf_tensor("xnT", [128, DC * TO], BF16))
        qT = ctx.enter_context(nc.sbuf_tensor("qT", [128, 16 * TO], BF16))
        pool = ctx.enter_context(nc.sbuf_tensor("pool", [128, 26624], BF16))
        id_sb = ctx.enter_context(nc.sbuf_tensor("id_sb", [128, 128], BF16))
        masks_sb = ctx.enter_context(nc.sbuf_tensor("masks_sb", [128, 1024], BF16))

        P_XNT_GRP, P_XN_NAT, P_WQ, P_WK, P_WV, P_SQ, P_KT, P_VSB = (
            0, 8192, 12288, 16384, 18432, 20480, 22528, 24576)
        P_PSCALED, P_PT = 0, 16384
        P_WO = 8192
        P_WG, P_WU, P_WD = 0, 8192, 16384

        scores4 = ctx.enter_context(nc.psum_tensor("scores4", [128, 2048], F32))
        psA = scores4[:, 0:512]
        psB = scores4[:, 512:1024]
        psC = scores4[:, 1024:1536]
        psD = scores4[:, 1536:2048]
        psE = ctx.enter_context(nc.psum_tensor("psE", [128, 512], F32))
        psF = ctx.enter_context(nc.psum_tensor("psF", [128, 512], F32))
        psT0 = ctx.enter_context(nc.psum_tensor("psT0", [128, 1024], BF16))
        psT1 = ctx.enter_context(nc.psum_tensor("psT1", [128, 1024], BF16))

        _sem_names = ["LCON", "LYI", "LMK", "LXT0", "LXT1", "LWK", "LWQ0", "LWQ1",
                      "LWO0", "LWO1", "LWG", "LWU", "LWD", "SO", "GPX",
                      "PE", "AC", "VE"]
        SEM = {nm: ctx.enter_context(nc.semaphore(nm)) for nm in _sem_names}

        prog = {e: [] for e in ENGINES}
        cnt = {k: 0 for k in SEM}

        def emit(eng, fn):
            prog[eng].append(fn)

        def wait(eng, sem, val):
            if val and val > 0:
                emit(eng, lambda E, s=SEM[sem], v=val: E.wait_ge(s, v))

        def emit_i(eng, fn, sem=None, by=1):
            if sem is None:
                emit(eng, fn)
                return None
            cnt[sem] += by
            v = cnt[sem]
            emit(eng, lambda E, f=fn, s=SEM[sem], b=by: f(E).then_inc(s, b))
            return v

        LAG = 0

        emit_i("sync", lambda E: E.dma_start(out=id_sb[:, :], in_=id_d), "LCON", 16)
        lx_const = cnt["LCON"]
        emit_i("gpsimd", lambda E: E.memset(stats[:, 64:65], 1.0e-5), "GPX")
        gp_eps = cnt["GPX"]
        eps_ap = stats[:, 64:65]

        wk_sb = pool[:, P_WK:P_WK + 2048]
        wv_sb = pool[:, P_WV:P_WV + 2048]
        emit_i("gpsimd", lambda E: E.dma_start(out=wk_sb, in_=wk_d), "LWK", 16)
        emit_i("gpsimd", lambda E: E.dma_start(out=wv_sb, in_=wv_d), "LWK", 16)
        lw_kv = cnt["LWK"]

        emit_i("scalar", lambda E: E.dma_start(out=misc[:, 0:2048], in_=bo_d), "LYI", 16)
        for tt in range(NT):
            emit_i("scalar", lambda E, tt=tt: E.dma_start(
                out=y_sb[:, tt * 2048:(tt + 1) * 2048],
                in_=x_own_d[tt * 128:(tt + 1) * 128, :]), "LYI", 16)

        emit_i("gpsimd", lambda E: E.dma_start(out=masks_sb[:, 0:512], in_=mask_d[0]), "LMK", 16)
        emit_i("gpsimd", lambda E: E.dma_start(out=masks_sb[:, 512:1024], in_=mask_d[1]), "LMK", 16)
        lx_masks = cnt["LMK"]
        mask_E = masks_sb[:, 0:512]
        mask_L = masks_sb[:, 512:1024]

        xt2 = [xtS[:, 0:2048], xtS[:, 2048:4096]]
        sq = pool[:, P_SQ:P_SQ + 2048]
        xn_nat = [pool[:, P_XN_NAT + i * 2048: P_XN_NAT + (i + 1) * 2048] for i in range(2)]
        xnT_grp = pool[:, P_XNT_GRP:P_XNT_GRP + 8192]
        kT = pool[:, P_KT:P_KT + 2048]
        v_sb = pool[:, P_VSB:P_VSB + 2048]

        def stat(slot, col):
            return stats[:, slot * 8 + col: slot * 8 + col + 1]

        ac_norm_done = 0
        ac_norm_hist = {}
        ac_norm_half = {}
        ac_sq_hist = {}
        pe_trans_tile = {}
        psT_war = {0: 0, 1: 0}
        ve_xnT_own_last = 0
        ve_kT_evac = 0
        ve_v_evac = {0: 0, 1: 0}
        ve_grp_evac_last = 0

        def ln_s1(gt):
            sl = gt % 2
            xt = xt2[gt % 2]
            wait("sync", "AC", ac_norm_hist.get(gt - 2, 0))
            lxs = "LXT0" if gt % 2 == 0 else "LXT1"
            lxv = emit_i("sync", lambda E, gt=gt, xt=xt: E.dma_start(
                out=xt, in_=x_d[gt * 128:(gt + 1) * 128, :]), lxs, 16)
            wait("vector", lxs, lxv)
            ve_sum = emit_i("vector", lambda E, sl=sl, xt=xt: E.reduce_sum(
                stat(sl, 0), xt, axis=AX.X), "VE")
            wait("scalar", lxs, lxv)
            wait("scalar", "AC", ac_sq_hist.get(gt - 1, 0))
            ac_sq = emit_i("scalar", lambda E, sl=sl, xt=xt: E.activation(
                sq, xt, AF.Square, accum_out=stat(sl, 1)), "AC")
            ac_sq_hist[gt] = ac_sq
            return dict(gt=gt, sl=sl, xt=xt, ve_sum=ve_sum, ac_sq=ac_sq)

        def ln_s2(st1):
            nonlocal ac_norm_done
            gt, sl, xt = st1["gt"], st1["sl"], st1["xt"]
            wait("vector", "VE", st1["ve_sum"])
            ve_m = emit_i("vector", lambda E, sl=sl: E.tensor_scalar_mul(
                stat(sl, 2), stat(sl, 0), 1.0 / D), "VE")
            wait("vector", "VE", ve_m)
            ve_msq = emit_i("vector", lambda E, sl=sl: E.tensor_mul(
                stat(sl, 3), stat(sl, 2), stat(sl, 2)), "VE")
            wait("vector", "AC", st1["ac_sq"])
            wait("vector", "VE", ve_msq)
            ve_var = emit_i("vector", lambda E, sl=sl: E.scalar_tensor_tensor(
                stat(sl, 4), stat(sl, 1), 1.0 / D, stat(sl, 3),
                ALU.mult, ALU.subtract), "VE")
            wait("scalar", "VE", ve_var)
            if gt == 0:
                wait("scalar", "GPX", gp_eps)
            ac_ln = emit_i("scalar", lambda E, sl=sl: E.activation(
                stat(sl, 5), stat(sl, 4), AF.Ln, bias=eps_ap), "AC")
            wait("scalar", "AC", ac_ln)
            ac_rstd = emit_i("scalar", lambda E, sl=sl: E.activation(
                stat(sl, 6), stat(sl, 5), AF.Exp, scale=-0.5), "AC")
            wait("vector", "AC", ac_rstd)
            ve_negmr = emit_i("vector", lambda E, sl=sl: E.tensor_scalar(
                stat(sl, 7), stat(sl, 2), stat(sl, 6), -1.0,
                ALU.mult, ALU.mult), "VE")
            wait("scalar", "PE", pe_trans_tile.get(gt - 2, 0))
            wait("scalar", "VE", ve_negmr)
            ac_norm_h0 = emit_i("scalar", lambda E, sl=sl, gt=gt, xt=xt: E.activation(
                xn_nat[gt % 2][:, 0:1024], xt[:, 0:1024], AF.Identity,
                bias=stat(sl, 7), scale=stat(sl, 6)), "AC")
            ac_norm_done = emit_i("scalar", lambda E, sl=sl, gt=gt, xt=xt: E.activation(
                xn_nat[gt % 2][:, 1024:2048], xt[:, 1024:2048], AF.Identity,
                bias=stat(sl, 7), scale=stat(sl, 6)), "AC")
            ac_norm_hist[gt] = ac_norm_done
            ac_norm_half[gt] = (ac_norm_h0, ac_norm_done)

        def ln_s3(gt):
            nonlocal ve_grp_evac_last, ve_xnT_own_last
            ti = gt % 4
            if gt == 0:
                wait("tensor", "LCON", lx_const)
            for half in range(2):
                wait("tensor", "AC", ac_norm_half[gt][half])
                pst = psT0 if half == 0 else psT1
                wait("tensor", "VE", psT_war[half])
                last = None
                for q8 in range(8):
                    dc = half * 8 + q8
                    def f(E, pst=pst, q8=q8, dc=dc, gt=gt):
                        return E.transpose(
                            pst[:, q8 * 128:(q8 + 1) * 128],
                            xn_nat[gt % 2][:, dc * 128:(dc + 1) * 128],
                            id_sb[:, :])
                    if q8 < 7:
                        emit_i("tensor", f)
                    else:
                        last = emit_i("tensor", f, "PE")
                pe_trans_tile[gt] = last
                wait("vector", "PE", last + LAG)
                dst_full = xnT_grp.rearrange("p (dc t) -> p dc t", dc=16)[
                    :, half * 8:(half + 1) * 8, ti * 128:(ti + 1) * 128]
                src_full = pst.rearrange("p (dc t) -> p dc t", dc=8)
                emit_i("vector", lambda E, d=dst_full, s=src_full: E.tensor_copy(d, s), "VE")
                dst_own = xnT.rearrange("p (dc t) -> p dc t", dc=16)[
                    :, half * 8:(half + 1) * 8, gt * 64:gt * 64 + 64]
                src_own = pst.rearrange("p (dc t two) -> p dc t two", dc=8, two=2)[:, :, :, 0]
                ev = emit_i("vector", lambda E, d=dst_own, s=src_own: E.tensor_copy(d, s), "VE")
                psT_war[half] = ev
                ve_grp_evac_last = ev
                if gt == 7:
                    ve_xnT_th0[0] = ev
                if gt == GT - 1:
                    ve_xnT_own_last = ev

        def ln_kv(g4):
            nonlocal ve_kT_evac
            wait("tensor", "LWK", lw_kv)
            wait("tensor", "VE", ve_grp_evac_last)
            wait("tensor", "VE", ve_kT_evac)
            last = None
            for dc in range(DC):
                def f(E, dc=dc):
                    return E.matmul(psC[:, :],
                                    wk_sb[:, dc * 128:(dc + 1) * 128],
                                    xnT_grp[:, dc * 512:(dc + 1) * 512],
                                    start=(dc == 0), stop=(dc == DC - 1))
                if dc < DC - 1:
                    emit_i("tensor", f)
                else:
                    last = emit_i("tensor", f, "PE")
            wait("vector", "PE", last + LAG)
            ve_kT_evac = emit_i("vector", lambda E, g4=g4: E.tensor_copy(
                kT[:, g4 * 512:(g4 + 1) * 512], psC[:, :]), "VE")
            for ti in range(4):
                jt = g4 * 4 + ti
                vps = psD if jt % 2 == 0 else psE
                wait("tensor", "VE", ve_v_evac[jt % 2])
                last = None
                for dc in range(DC):
                    def f(E, dc=dc, ti=ti, vps=vps):
                        return E.matmul(vps[:, 0:128],
                                        xnT_grp[:, dc * 512 + ti * 128: dc * 512 + (ti + 1) * 128],
                                        wv_sb[:, dc * 128:(dc + 1) * 128],
                                        start=(dc == 0), stop=(dc == DC - 1))
                    if dc < DC - 1:
                        emit_i("tensor", f)
                    else:
                        last = emit_i("tensor", f, "PE")
                wait("vector", "PE", last + LAG)
                ve_v_evac[jt % 2] = emit_i("vector", lambda E, jt=jt, vps=vps: E.tensor_copy(
                    v_sb[:, jt * 128:(jt + 1) * 128], vps[:, 0:128]), "VE")

        wq_sb = [pool[:, P_WQ + i * 2048: P_WQ + (i + 1) * 2048] for i in range(2)]
        pe_wq_done = {}
        pe_wq_buf_last = {0: 0, 1: 0}
        ve_q_evac = {}
        psF_war = [("VE", 0)]
        ve_xnT_th0 = [0]
        q1_first = [True]

        def q_proj_half(ec, th):
            lws = "LWQ0" if ec % 2 == 0 else "LWQ1"
            wait("gpsimd", "PE", pe_wq_buf_last[ec % 2])
            lw = emit_i("gpsimd", lambda E, ec=ec: E.dma_start(
                out=wq_sb[ec % 2], in_=wq_d[ec]), lws, 16)
            wait("tensor", lws, lw)
            if th == 0:
                wait("tensor", "VE", ve_xnT_th0[0])
            elif q1_first[0]:
                wait("tensor", "VE", ve_xnT_own_last)
                q1_first[0] = False
            weng, wv = psF_war[0]
            wait("tensor", weng, wv)
            last = None
            for dc in range(DC):
                def f(E, ec=ec, th=th, dc=dc):
                    return E.matmul(psF[:, :],
                                    wq_sb[ec % 2][:, dc * 128:(dc + 1) * 128],
                                    xnT[:, dc * 1024 + th * 512: dc * 1024 + (th + 1) * 512],
                                    start=(dc == 0), stop=(dc == DC - 1))
                if dc < DC - 1:
                    emit_i("tensor", f)
                else:
                    last = emit_i("tensor", f, "PE")
            pe_wq_done[ec] = last
            pe_wq_buf_last[ec % 2] = last
            wait("vector", "PE", last + LAG)
            ve = emit_i("vector", lambda E, ec=ec, th=th: E.tensor_scalar_mul(
                qT[:, ec * 1024 + th * 512: ec * 1024 + (th + 1) * 512],
                psF[:, :], QK_SCALE), "VE")
            psF_war[0] = ("VE", ve)
            ve_q_evac[ec] = ve

        st1s = {}
        for gt in range(GT + 1):
            if gt >= 1:
                ln_s2(st1s[gt - 1])
            if gt < GT:
                st1s[gt] = ln_s1(gt)
            if gt >= 1:
                ln_s3(gt - 1)
                if (gt - 1) % 4 == 3:
                    ln_kv((gt - 1) // 4)
            if gt >= 9:
                q_proj_half(2 * (gt - 9), 0)
                q_proj_half(2 * (gt - 9) + 1, 0)

        ac_norm_final = ac_norm_done

        wait("vector", "LYI", cnt["LYI"])
        ve_after_yinit = 0
        for tt in range(NT):
            ysl = y_sb[:, tt * 2048:(tt + 1) * 2048]
            ve_after_yinit = emit_i("vector", lambda E, ysl=ysl: E.tensor_add(
                ysl, ysl, misc[:, 0:2048]), "VE")

        probsT = pool[:, P_PT:P_PT + 4096]

        wait("tensor", "VE", ve_kT_evac)
        wait("tensor", "VE", max(ve_v_evac.values()))
        wait("tensor", "LMK", lx_masks)

        sc_banks = [psA, psB, psC, psD]
        sc_ring = [0, 1, 2, 3]
        sc_war = {0: ("VE", 0), 1: ("VE", 0), 2: ("VE", 0), 3: ("VE", 0)}
        cidx = 0
        pstW = {0: ("VE", psT_war[0]), 1: ("VE", psT_war[1])}
        av_banks = {0: psE, 1: psF}
        av_war0 = [("VE", max(ve_v_evac.values()))]

        pe_trans_slot = {}
        ve_avevac_last = 0
        ac_avevac_last = 0
        ac_exp_done = {}
        ve_partials_read = {}

        def p2_stat_mk(par, u):
            base = 16 + (par * 2 + u % 2) * 12
            def f(col):
                return stats[:, base + col: base + col + 1]
            return f, base

        probs_scaled6 = [pool[:, P_PSCALED + i * 2048: P_PSCALED + (i + 1) * 2048]
                         for i in range(6)]

        def stageA1(u, h, g):
            nonlocal cidx
            E_ext = 512 * (g + 1)
            C = g + 1
            out = dict(u=u, h=h, g=g, E_ext=E_ext, njt=E_ext // 128, deps=[],
                       exps=[])
            wait("tensor", "VE", ve_q_evac[h])
            for par, tk in enumerate((2 * g, 2 * g + 1)):
                slot = (u % 3) * 2 + par
                pslot = probs_scaled6[slot]
                ps, psbase = p2_stat_mk(par, u)
                ac_exp = 0
                slot0 = cidx % 4
                cpes = []
                for c in range(C):
                    bi = sc_ring[cidx % 4]
                    weng, wv = sc_war[bi]
                    wait("tensor", weng, wv)
                    diag = (c == C - 1)
                    def f(E, h=h, tk=tk, c=c, bank=sc_banks[bi], diag=diag):
                        return E.matmul(bank[:, :],
                                        qT[:, h * 1024 + tk * 128: h * 1024 + (tk + 1) * 128],
                                        kT[:, c * 512:(c + 1) * 512],
                                        start=True, stop=not diag)
                    if not diag:
                        cpe = emit_i("tensor", f, "PE")
                    else:
                        emit_i("tensor", f)
                        m = mask_E if (tk % 2 == 0) else mask_L
                        def fm(E, bank=sc_banks[bi], m=m):
                            return E.matmul(bank[:, :], id_sb[:, :], m,
                                            start=False, stop=True)
                        cpe = emit_i("tensor", fm, "PE")
                    cpes.append(cpe)
                    cidx += 1
                    if c == 0:
                        wait("vector", "PE", cpe + LAG)
                        wait("vector", "AC", ac_exp_done.get((par, u % 2), 0))
                        ve_negm = emit_i("vector", lambda E, ps=ps, b0=sc_banks[bi]: E.reduce_max(
                            ps(8), b0[:, :], axis=AX.X, negate=True), "VE")
                runs = [(c, 1) for c in range(C)]
                for ri, (c0, rlen) in enumerate(runs):
                    wait("scalar", "PE", cpes[c0 + rlen - 1] + LAG)
                    if ri == 0:
                        wait("scalar", "VE", ve_negm)
                        wait("scalar", "PE", pe_trans_slot.get(slot, 0))
                        wait("scalar", "VE", ve_partials_read.get((par, u % 2), 0))
                    sb = (slot0 + c0) % 4
                    def fx(E, c0=c0, rlen=rlen, sb=sb, ps=ps, pslot=pslot, ri=ri):
                        return E.activation(
                            pslot[:, c0 * 512:(c0 + rlen) * 512],
                            scores4[:, sb * 512:(sb + rlen) * 512], AF.Exp,
                            bias=ps(8), accum_out=ps(ri))
                    ac_exp = emit_i("scalar", fx, "AC")
                    for k in range(rlen):
                        sc_war[(sb + k)] = ("AC", ac_exp)
                ac_exp_done[(par, u % 2)] = ac_exp
                out["exps"].append((par, slot, pslot, ps, psbase, ac_exp, len(runs)))
            return out

        def stageA2(info):
            u = info["u"]
            E_ext, C = info["E_ext"], info["g"] + 1
            for (par, slot, pslot, ps, psbase, ac_exp, nruns) in info["exps"]:
                wait("vector", "AC", ac_exp)
                if nruns > 1:
                    ve_z = emit_i("vector", lambda E, ps=ps, psbase=psbase, nruns=nruns: E.reduce_sum(
                        ps(9), stats[:, psbase: psbase + nruns], axis=AX.X), "VE")
                    zcol = 9
                    wait("vector", "VE", ve_z)
                    ve_partials_read[(par, u % 2)] = ve_z
                else:
                    zcol = 0
                ve_zrec = emit_i("vector", lambda E, ps=ps, zcol=zcol: E.reciprocal(
                    ps(10), ps(zcol)), "VE")
                if nruns == 1:
                    ve_partials_read[(par, u % 2)] = ve_zrec
                wait("vector", "VE", ve_zrec)
                ve_scale = emit_i("vector", lambda E, E_ext=E_ext, pslot=pslot, ps=ps: E.tensor_scalar_mul(
                    pslot[:, 0:E_ext], pslot[:, 0:E_ext], ps(10)), "VE")
                info["deps"].append((slot, ve_scale))

        def stageB(info):
            nonlocal ve_avevac_last, ac_avevac_last
            u, h, g, njt = info["u"], info["h"], info["g"], info["njt"]
            nbatches = (njt + 7) // 8
            ve_evac_b = []
            for b in range(nbatches):
                n_in = min(8, njt - b * 8)
                evs = 0
                for par in range(2):
                    slot, ve_scale = info["deps"][par]
                    if b == 0:
                        wait("tensor", "VE", ve_scale)
                    bank = (u + par) % 2
                    pst = psT0 if bank == 0 else psT1
                    weng, wv = pstW[bank]
                    wait("tensor", weng, wv)
                    last = None
                    for q8 in range(n_in):
                        jt = b * 8 + q8
                        def f(E, pst=pst, q8=q8, jt=jt, slot=slot):
                            return E.transpose(
                                pst[:, q8 * 128:(q8 + 1) * 128],
                                probs_scaled6[slot][:, jt * 128:(jt + 1) * 128],
                                id_sb[:, :])
                        if q8 < n_in - 1:
                            emit_i("tensor", f)
                        else:
                            last = emit_i("tensor", f, "PE")
                    pe_trans_slot[slot] = last
                    wait("vector", "PE", last + LAG)
                    dst = probsT.rearrange("p (jt i) -> p jt i", jt=16)[
                        :, b * 8: b * 8 + n_in, par * 128:(par + 1) * 128]
                    srcp = pst.rearrange("p (jt i) -> p jt i", jt=8)[:, 0:n_in, :]
                    ev = emit_i("vector", lambda E, d=dst, s=srcp: E.tensor_copy(d, s), "VE")
                    evs = ev
                    pstW[bank] = ("VE", ev)
                ve_evac_b.append(evs)
            bank_i = u % 2
            avp = av_banks[bank_i]
            weng, wv = av_war0[0] if bank_i == 0 else psF_war[0]
            wait("tensor", weng, wv)
            last = None
            for jt in range(njt):
                if jt % 8 == 0:
                    wait("tensor", "VE", ve_evac_b[jt // 8])
                def f(E, jt=jt, avp=avp, njt=njt):
                    return E.matmul(avp[:, 0:256],
                                    v_sb[:, jt * 128:(jt + 1) * 128],
                                    probsT[:, jt * 256:(jt + 1) * 256],
                                    start=(jt == 0), stop=(jt == njt - 1))
                if jt < njt - 1:
                    emit_i("tensor", f)
                else:
                    last = emit_i("tensor", f, "PE")
            wait("scalar", "PE", last + LAG)
            ac_avevac_last = emit_i("scalar", lambda E, h=h, g=g, avp=avp: E.copy(
                qT[:, h * 1024 + g * 256: h * 1024 + (g + 1) * 256], avp[:, 0:256]), "AC")
            if bank_i == 0:
                av_war0[0] = ("AC", ac_avevac_last)
            else:
                psF_war[0] = ("AC", ac_avevac_last)

        PIPE_LAG = 3
        q_proj_half(0, 1)
        q_proj_half(1, 1)
        infos = []
        units = [(h, g) for h in range(H) for g in range(4)]
        for u, (h, g) in enumerate(units):
            if h + 2 < H and g == 0:
                q_proj_half(h + 2, 1)
            if u >= PIPE_LAG:
                stageB(infos[u - PIPE_LAG])
            infos.append(stageA1(u, h, g))
            if u >= 1:
                stageA2(infos[u - 1])
        stageA2(infos[-1])
        for uu in range(len(units) - PIPE_LAG, len(units)):
            stageB(infos[uu])

        pe_p2_end = cnt["PE"]
        ve_p2_end = cnt["VE"]
        pe_wo0_ok = max(pe_trans_slot.get(4, 0), pe_trans_slot.get(5, 0),
                        max(pe_wq_done.values()))

        wo_sb = [pool[:, P_WO + i * 8192: P_WO + (i + 1) * 8192] for i in range(2)]
        wo_ps = [psA, psB]
        pe_wo_done = {}
        ve_wo_evac = {0: ve_p2_end, 1: ve_p2_end}
        first_wo = True
        for es in range(4):
            if es == 0:
                wait("gpsimd", "PE", pe_wo0_ok)
            elif es == 1:
                wait("gpsimd", "PE", pe_p2_end)
            else:
                wait("gpsimd", "PE", pe_wo_done[es - 2])
            lw = emit_i("gpsimd", lambda E, es=es: E.dma_start(
                out=wo_sb[es % 2], in_=wo_d[es]), "LWO0" if es % 2 == 0 else "LWO1", 16)
            wait("tensor", "LWO0" if es % 2 == 0 else "LWO1", lw)
            if first_wo:
                wait("tensor", "AC", ac_avevac_last)
                first_wo = False
            for tt in range(NT):
                ps = wo_ps[tt % 2]
                wait("tensor", "VE", ve_wo_evac[tt % 2])
                last = None
                for cc in range(16):
                    def f(E, es=es, tt=tt, cc=cc, ps=ps):
                        return E.matmul(ps[:, :],
                                        qT[:, cc * 1024 + tt * 128: cc * 1024 + (tt + 1) * 128],
                                        wo_sb[es % 2][:, cc * 512:(cc + 1) * 512],
                                        start=(cc == 0), stop=(cc == 15))
                    if cc < 15:
                        emit_i("tensor", f)
                    else:
                        last = emit_i("tensor", f, "PE")
                pe_wo_done[es] = last
                wait("vector", "PE", last + LAG)
                ysl = y_sb[:, tt * 2048 + es * 512: tt * 2048 + (es + 1) * 512]
                ve_wo_evac[tt % 2] = emit_i("vector", lambda E, ysl=ysl, ps=ps: E.tensor_add(
                    ysl, ysl, ps[:, :]), "VE")

        ve_p3_end = cnt["VE"]
        pe_p3_end = cnt["PE"]

        wg_sb = pool[:, P_WG:P_WG + 8192]
        wu_sb = pool[:, P_WU:P_WU + 8192]
        wd_sb = pool[:, P_WD:P_WD + 8192]
        hact = qT[:, 0:4096]
        gs = [qT[:, 4096 + i * 512: 4096 + (i + 1) * 512] for i in range(2)]

        pe_gate_done = {}
        pe_down_done = {}
        ve_down_evac = {0: ve_p3_end, 1: ve_p3_end}
        ve_final_tt = {}
        ac_silu_prev = {0: 0, 1: 0}
        ve_hact = {}
        GU_banks = [(psA, psB), (psC, psD)]
        D_banks2 = {0: (psE, psF), 1: (psA, psB)}
        first_gate = True
        first_down = True

        for s in range(16):
            if s == 0:
                wait("gpsimd", "PE", pe_p2_end)
            else:
                wait("gpsimd", "PE", pe_gate_done[s - 1])
            lw_gs = emit_i("gpsimd", lambda E, s=s: E.dma_start(out=wg_sb, in_=wg_d[s]), "LWG", 16)
            if s == 0:
                wait("gpsimd", "PE", pe_wo_done[2])
            lw_us = emit_i("gpsimd", lambda E, s=s: E.dma_start(out=wu_sb, in_=wu_d[s]), "LWU", 16)
            if s == 0:
                wait("gpsimd", "PE", pe_p3_end)
            else:
                wait("gpsimd", "PE", pe_down_done[s - 1])
            lw_ds = emit_i("gpsimd", lambda E, s=s: E.dma_start(out=wd_sb, in_=wd_d[s]), "LWD", 16)

            wait("tensor", "LWG", lw_gs)
            wait("tensor", "LWU", lw_us)
            for hq in range(4):
                for th in range(2):
                    pair = hq * 2 + th
                    gb, ub = GU_banks[pair % 2]
                    if first_gate:
                        wait("tensor", "VE", ve_p3_end)
                        first_gate = False
                    if pair < 2 and s > 0:
                        wait("tensor", "VE", ve_down_evac[1] if pair == 0
                             else ve_hact.get((s - 1, 7), 0))
                    wait("tensor", "AC", ac_silu_prev[pair % 2])
                    wait("tensor", "VE", ve_hact.get((s, pair - 2),
                                                     ve_hact.get((s - 1, 6 + pair), 0)))
                    lastg = None
                    for dc in range(DC):
                        def f(E, s=s, hq=hq, th=th, dc=dc, gb=gb):
                            return E.matmul(gb[:, :],
                                            wg_sb[:, dc * 512 + hq * 128: dc * 512 + (hq + 1) * 128],
                                            xnT[:, dc * 1024 + th * 512: dc * 1024 + (th + 1) * 512],
                                            start=(dc == 0), stop=(dc == DC - 1))
                        if dc < DC - 1:
                            emit_i("tensor", f)
                        else:
                            lastg = emit_i("tensor", f, "PE")
                    lastu = None
                    for dc in range(DC):
                        def f(E, s=s, hq=hq, th=th, dc=dc, ub=ub):
                            return E.matmul(ub[:, :],
                                            wu_sb[:, dc * 512 + hq * 128: dc * 512 + (hq + 1) * 128],
                                            xnT[:, dc * 1024 + th * 512: dc * 1024 + (th + 1) * 512],
                                            start=(dc == 0), stop=(dc == DC - 1))
                        if dc < DC - 1:
                            emit_i("tensor", f)
                        else:
                            lastu = emit_i("tensor", f, "PE")
                    pe_gate_done[s] = lastu
                    wait("scalar", "PE", lastg + LAG)
                    ac_silu = emit_i("scalar", lambda E, pair=pair, gb=gb: E.activation(
                        gs[pair % 2], gb[:, :], AF.Silu), "AC")
                    ac_silu_prev[pair % 2] = ac_silu
                    wait("vector", "AC", ac_silu)
                    wait("vector", "PE", lastu + LAG)
                    dsth = hact[:, hq * 1024 + th * 512: hq * 1024 + (th + 1) * 512]
                    ve_hact[(s, pair)] = emit_i("vector", lambda E, d=dsth, pair=pair, ub=ub: E.tensor_mul(
                        d, gs[pair % 2], ub[:, :]), "VE")

            wait("tensor", "LWD", lw_ds)
            wait("tensor", "VE", ve_hact[(s, 7)])
            for tt in range(NT):
                for e2 in range(2):
                    b0, b1 = D_banks2[e2]
                    if first_down:
                        wait("tensor", "VE", ve_p3_end)
                        first_down = False
                    if tt == 0 and e2 == 1:
                        wait("tensor", "VE", ve_hact[(s, 7)])
                        wait("tensor", "AC", ac_silu_prev[0])
                    wait("tensor", "VE", ve_down_evac[e2])
                    last = None
                    for hq in range(4):
                        for ei, bank in ((0, b0), (1, b1)):
                            es = e2 * 2 + ei
                            def f(E, s=s, tt=tt, hq=hq, es=es, bank=bank):
                                return E.matmul(bank[:, :],
                                                hact[:, hq * 1024 + tt * 128: hq * 1024 + (tt + 1) * 128],
                                                wd_sb[:, hq * 2048 + es * 512: hq * 2048 + (es + 1) * 512],
                                                start=(hq == 0), stop=(hq == 3))
                            if hq == 3 and ei == 1:
                                last = emit_i("tensor", f, "PE")
                            else:
                                emit_i("tensor", f)
                    pe_down_done[s] = last
                    wait("vector", "PE", last + LAG)
                    for ei, bank in ((0, b0), (1, b1)):
                        es = e2 * 2 + ei
                        ysl = y_sb[:, tt * 2048 + es * 512: tt * 2048 + (es + 1) * 512]
                        ve_down_evac[e2] = emit_i(
                            "vector",
                            lambda E, ysl=ysl, bank=bank: E.tensor_add(ysl, ysl, bank[:, :]),
                            "VE")
                        if s == 15:
                            ve_final_tt[tt] = ve_down_evac[e2]

        wait("tensor", "VE", cnt["VE"])
        wait("tensor", "AC", cnt["AC"])
        for _ in range(LAG + 1):
            emit_i("tensor", lambda E: E.matmul(
                psA[:, 0:128], id_sb[:, :], id_sb[:, :], start=True, stop=True), "PE")

        for tt in range(NT):
            wait("sync", "VE", ve_final_tt[tt])
            emit_i("sync", lambda E, tt=tt: E.dma_start(
                out=y_d[tt * 128:(tt + 1) * 128, :],
                in_=y_sb[:, tt * 2048:(tt + 1) * 2048]), "SO", 16)
        wait("sync", "SO", cnt["SO"])

        with nc.Block() as block:
            @block.sync
            def _(E):
                for f in prog["sync"]:
                    f(E)

            @block.gpsimd
            def _(E):
                for f in prog["gpsimd"]:
                    f(E)

            @block.tensor
            def _(E):
                for f in prog["tensor"]:
                    f(E)

            @block.scalar
            def _(E):
                for f in prog["scalar"]:
                    f(E)

            @block.vector
            def _(E):
                for f in prog["vector"]:
                    f(E)

    return nc


def _host_layouts(Wq, Wk, Wv, Wo, Wg, Wu, Wd, bo):
    wq_l = np.ascontiguousarray(
        Wq.reshape(16, 128, 16, 128).transpose(2, 1, 0, 3).reshape(16, 128, 2048))
    wk_l = np.ascontiguousarray(Wk.reshape(16, 128, 128).transpose(1, 0, 2).reshape(128, 2048))
    wv_l = np.ascontiguousarray(Wv.reshape(16, 128, 128).transpose(1, 0, 2).reshape(128, 2048))
    wo_l = np.ascontiguousarray(
        Wo.reshape(16, 128, 4, 512).transpose(2, 1, 0, 3).reshape(4, 128, 8192))
    WgT = np.ascontiguousarray(Wg.T)
    WuT = np.ascontiguousarray(Wu.T)
    WdT = np.ascontiguousarray(Wd.T)
    wg_l = np.ascontiguousarray(
        WgT.reshape(16, 128, 16, 512).transpose(2, 1, 0, 3).reshape(16, 128, 8192))
    wu_l = np.ascontiguousarray(
        WuT.reshape(16, 128, 16, 512).transpose(2, 1, 0, 3).reshape(16, 128, 8192))
    wd_l = np.ascontiguousarray(
        WdT.reshape(16, 4, 128, 4, 512).transpose(0, 2, 1, 3, 4).reshape(16, 128, 8192))
    bo_bc = np.ascontiguousarray(np.broadcast_to(bo.reshape(1, -1), (128, 2048)))
    return dict(wq_l=wq_l, wk_l=wk_l, wv_l=wv_l, wo_l=wo_l,
                wg_l=wg_l, wu_l=wu_l, wd_l=wd_l, bo_bc=bo_bc)


def _masks(parity):
    i = np.arange(128)[:, None]
    jc = np.arange(512)[None, :]
    if parity == 0:
        key_true = jc
    else:
        key_true = np.where(jc % 2 == 0, jc + 1, jc - 1)
    mE = np.where(key_true <= 2 * i + parity, 0.0, NEG).astype(np.float32)
    mL = np.where(key_true <= 256 + 2 * i + parity, 0.0, NEG).astype(np.float32)
    return np.stack([mE, mL]).astype(ml_dtypes.bfloat16)


def kernel(L0_attn_x, L0_attn_Wq, L0_attn_Wk, L0_attn_Wv, L0_attn_Wo, L0_attn_bo,
           L0_ffn_w_gate, L0_ffn_w_up, L0_ffn_w_down):
    x = np.ascontiguousarray(np.asarray(L0_attn_x, dtype=np.float32))
    shared = _host_layouts(
        np.asarray(L0_attn_Wq, np.float32), np.asarray(L0_attn_Wk, np.float32),
        np.asarray(L0_attn_Wv, np.float32), np.asarray(L0_attn_Wo, np.float32),
        np.asarray(L0_ffn_w_gate, np.float32), np.asarray(L0_ffn_w_up, np.float32),
        np.asarray(L0_ffn_w_down, np.float32), np.asarray(L0_attn_bo, np.float32))
    ident = np.eye(128, dtype=ml_dtypes.bfloat16)
    masks = [_masks(0), _masks(1)]

    nc = build_nc()
    in_maps = []
    for core in range(8):
        b, p = core // 2, core % 2
        xb = x[b]
        xperm = np.ascontiguousarray(
            xb.reshape(1024, 2, 2048)[:, [p, 1 - p], :].reshape(2048, 2048))
        in_map = dict(shared)
        in_map["x"] = xperm
        in_map["x_own"] = np.ascontiguousarray(xb[p::2, :])
        in_map["masks"] = masks[p]
        in_map["ident"] = ident
        in_maps.append(in_map)

    res = run_bass_kernel_spmd(nc, in_maps, core_ids=list(range(8)))
    out = np.empty((B, T, D), dtype=np.float32)
    for core in range(8):
        b, p = core // 2, core % 2
        out[b, p::2, :] = res.results[core]["y"]
    return out


# version 23
# speedup vs baseline: 1.0497x; 1.0335x over previous
import math
import contextlib
import numpy as np
import ml_dtypes
import concourse.bass as bass
import concourse.mybir as mybir
from concourse.bass_utils import run_bass_kernel_spmd

F32 = mybir.dt.float32
BF16 = mybir.dt.bfloat16
AF = mybir.ActivationFunctionType
AX = mybir.AxisListType
ALU = mybir.AluOpType

B, T, D = 4, 2048, 2048
H, DH = 16, 128
HID = 8192
TO = 1024
NT = 8
GT = 16
DC = 16
QK_SCALE = 1.0 / math.sqrt(DH)
NEG = -1.0e30

ENGINES = ("sync", "gpsimd", "tensor", "scalar", "vector")


def build_nc():
    nc = bass.Bass("TRN2", target_bir_lowering=False)

    x_d = nc.dram_tensor("x", [T, D], F32, kind="ExternalInput").ap()
    x_own_d = nc.dram_tensor("x_own", [TO, D], F32, kind="ExternalInput").ap()
    wq_d = nc.dram_tensor("wq_l", [16, 128, 2048], F32, kind="ExternalInput").ap()
    wk_d = nc.dram_tensor("wk_l", [128, 2048], F32, kind="ExternalInput").ap()
    wv_d = nc.dram_tensor("wv_l", [128, 2048], F32, kind="ExternalInput").ap()
    wo_d = nc.dram_tensor("wo_l", [4, 128, 8192], F32, kind="ExternalInput").ap()
    wg_d = nc.dram_tensor("wg_l", [16, 128, 8192], F32, kind="ExternalInput").ap()
    wu_d = nc.dram_tensor("wu_l", [16, 128, 8192], F32, kind="ExternalInput").ap()
    wd_d = nc.dram_tensor("wd_l", [16, 128, 8192], F32, kind="ExternalInput").ap()
    bo_d = nc.dram_tensor("bo_bc", [128, 2048], F32, kind="ExternalInput").ap()
    mask_d = nc.dram_tensor("masks", [2, 128, 512], BF16, kind="ExternalInput").ap()
    id_d = nc.dram_tensor("ident", [128, 128], BF16, kind="ExternalInput").ap()
    y_d = nc.dram_tensor("y", [TO, D], F32, kind="ExternalOutput").ap()

    ctx = contextlib.ExitStack()
    with ctx:
        y_sb = ctx.enter_context(nc.sbuf_tensor("y_sb", [128, NT * 2048], F32))
        xtS = ctx.enter_context(nc.sbuf_tensor("xtS", [128, 2 * 2048], F32))
        misc = ctx.enter_context(nc.sbuf_tensor("misc", [128, 2048], F32))
        stats = ctx.enter_context(nc.sbuf_tensor("stats", [128, 80], F32))

        xnT = ctx.enter_context(nc.sbuf_tensor("xnT", [128, DC * TO], BF16))
        qT = ctx.enter_context(nc.sbuf_tensor("qT", [128, 16 * TO], BF16))
        pool = ctx.enter_context(nc.sbuf_tensor("pool", [128, 26624], BF16))
        id_sb = ctx.enter_context(nc.sbuf_tensor("id_sb", [128, 128], BF16))
        masks_sb = ctx.enter_context(nc.sbuf_tensor("masks_sb", [128, 1024], BF16))

        P_XNT_GRP, P_XN_NAT, P_WQ, P_WK, P_WV, P_SQ, P_KT, P_VSB = (
            0, 8192, 12288, 16384, 18432, 20480, 22528, 24576)
        P_PSCALED, P_PT = 0, 16384
        P_WO = 8192
        P_WG, P_WU, P_WD = 0, 8192, 16384

        scores4 = ctx.enter_context(nc.psum_tensor("scores4", [128, 2048], F32))
        psA = scores4[:, 0:512]
        psB = scores4[:, 512:1024]
        psC = scores4[:, 1024:1536]
        psD = scores4[:, 1536:2048]
        psE = ctx.enter_context(nc.psum_tensor("psE", [128, 512], F32))
        psF = ctx.enter_context(nc.psum_tensor("psF", [128, 512], F32))
        psT0 = ctx.enter_context(nc.psum_tensor("psT0", [128, 1024], BF16))
        psT1 = ctx.enter_context(nc.psum_tensor("psT1", [128, 1024], BF16))

        _sem_names = ["LCON", "LYI", "LMK", "LXT0", "LXT1", "LXT2", "LWK", "LWQ0", "LWQ1",
                      "LWO0", "LWO1", "LWG", "LWU", "LWD", "SO", "GPX",
                      "PE", "AC", "VE"]
        SEM = {nm: ctx.enter_context(nc.semaphore(nm)) for nm in _sem_names}

        prog = {e: [] for e in ENGINES}
        cnt = {k: 0 for k in SEM}

        def emit(eng, fn):
            prog[eng].append(fn)

        def wait(eng, sem, val):
            if val and val > 0:
                emit(eng, lambda E, s=SEM[sem], v=val: E.wait_ge(s, v))

        def emit_i(eng, fn, sem=None, by=1):
            if sem is None:
                emit(eng, fn)
                return None
            cnt[sem] += by
            v = cnt[sem]
            emit(eng, lambda E, f=fn, s=SEM[sem], b=by: f(E).then_inc(s, b))
            return v

        LAG = 0

        emit_i("sync", lambda E: E.dma_start(out=id_sb[:, :], in_=id_d), "LCON", 16)
        lx_const = cnt["LCON"]
        emit_i("gpsimd", lambda E: E.memset(stats[:, 72:73], 1.0e-5), "GPX")
        gp_eps = cnt["GPX"]
        eps_ap = stats[:, 72:73]

        wk_sb = pool[:, P_WK:P_WK + 2048]
        wv_sb = pool[:, P_WV:P_WV + 2048]
        emit_i("gpsimd", lambda E: E.dma_start(out=wk_sb, in_=wk_d), "LWK", 16)
        emit_i("gpsimd", lambda E: E.dma_start(out=wv_sb, in_=wv_d), "LWK", 16)
        lw_kv = cnt["LWK"]

        for tt in range(NT):
            emit_i("scalar", lambda E, tt=tt: E.dma_start(
                out=y_sb[:, tt * 2048:(tt + 1) * 2048],
                in_=x_own_d[tt * 128:(tt + 1) * 128, :]), "LYI", 16)

        emit_i("gpsimd", lambda E: E.dma_start(out=masks_sb[:, 0:512], in_=mask_d[0]), "LMK", 16)
        emit_i("gpsimd", lambda E: E.dma_start(out=masks_sb[:, 512:1024], in_=mask_d[1]), "LMK", 16)
        lx_masks = cnt["LMK"]
        mask_E = masks_sb[:, 0:512]
        mask_L = masks_sb[:, 512:1024]

        xt3 = [xtS[:, 0:2048], xtS[:, 2048:4096], misc[:, 0:2048]]
        sq = pool[:, P_SQ:P_SQ + 2048]
        xn_nat = [pool[:, P_XN_NAT + i * 2048: P_XN_NAT + (i + 1) * 2048] for i in range(2)]
        xnT_grp = pool[:, P_XNT_GRP:P_XNT_GRP + 8192]
        kT = pool[:, P_KT:P_KT + 2048]
        v_sb = pool[:, P_VSB:P_VSB + 2048]

        def stat(slot, col):
            return stats[:, slot * 8 + col: slot * 8 + col + 1]

        ac_norm_done = 0
        ac_norm_hist = {}
        ac_norm_half = {}
        ac_sq_hist = {}
        pe_trans_tile = {}
        psT_war = {0: 0, 1: 0}
        ve_xnT_own_last = 0
        ve_kT_evac = 0
        ve_v_evac = {0: 0, 1: 0}
        ve_grp_evac_last = 0

        def ln_s1(gt):
            sl = gt % 3
            xt = xt3[gt % 3]
            wait("sync", "AC", ac_norm_hist.get(gt - 3, 0))
            lxs = "LXT%d" % (gt % 3)
            lxv = emit_i("sync", lambda E, gt=gt, xt=xt: E.dma_start(
                out=xt, in_=x_d[gt * 128:(gt + 1) * 128, :]), lxs, 16)
            wait("vector", lxs, lxv)
            ve_sum = emit_i("vector", lambda E, sl=sl, xt=xt: E.reduce_sum(
                stat(sl, 0), xt, axis=AX.X), "VE")
            wait("scalar", lxs, lxv)
            ac_sq = emit_i("scalar", lambda E, sl=sl, xt=xt: E.activation(
                sq, xt, AF.Square, accum_out=stat(sl, 1)), "AC")
            ac_sq_hist[gt] = ac_sq
            return dict(gt=gt, sl=sl, xt=xt, ve_sum=ve_sum, ac_sq=ac_sq)

        def ln_s2(st1):
            nonlocal ac_norm_done
            gt, sl, xt = st1["gt"], st1["sl"], st1["xt"]
            wait("vector", "VE", st1["ve_sum"])
            ve_m = emit_i("vector", lambda E, sl=sl: E.tensor_scalar_mul(
                stat(sl, 2), stat(sl, 0), 1.0 / D), "VE")
            wait("vector", "VE", ve_m)
            ve_msq = emit_i("vector", lambda E, sl=sl: E.tensor_mul(
                stat(sl, 3), stat(sl, 2), stat(sl, 2)), "VE")
            wait("vector", "AC", st1["ac_sq"])
            wait("vector", "VE", ve_msq)
            ve_var = emit_i("vector", lambda E, sl=sl: E.scalar_tensor_tensor(
                stat(sl, 4), stat(sl, 1), 1.0 / D, stat(sl, 3),
                ALU.mult, ALU.subtract), "VE")
            wait("scalar", "VE", ve_var)
            if gt == 0:
                wait("scalar", "GPX", gp_eps)
            ac_ln = emit_i("scalar", lambda E, sl=sl: E.activation(
                stat(sl, 5), stat(sl, 4), AF.Ln, bias=eps_ap), "AC")
            wait("scalar", "AC", ac_ln)
            ac_rstd = emit_i("scalar", lambda E, sl=sl: E.activation(
                stat(sl, 6), stat(sl, 5), AF.Exp, scale=-0.5), "AC")
            wait("vector", "AC", ac_rstd)
            ve_negmr = emit_i("vector", lambda E, sl=sl: E.tensor_scalar(
                stat(sl, 7), stat(sl, 2), stat(sl, 6), -1.0,
                ALU.mult, ALU.mult), "VE")
            wait("scalar", "PE", pe_trans_tile.get(gt - 2, 0))
            wait("scalar", "VE", ve_negmr)
            ac_norm_h0 = emit_i("scalar", lambda E, sl=sl, gt=gt, xt=xt: E.activation(
                xn_nat[gt % 2][:, 0:1024], xt[:, 0:1024], AF.Identity,
                bias=stat(sl, 7), scale=stat(sl, 6)), "AC")
            ac_norm_done = emit_i("scalar", lambda E, sl=sl, gt=gt, xt=xt: E.activation(
                xn_nat[gt % 2][:, 1024:2048], xt[:, 1024:2048], AF.Identity,
                bias=stat(sl, 7), scale=stat(sl, 6)), "AC")
            ac_norm_hist[gt] = ac_norm_done
            ac_norm_half[gt] = (ac_norm_h0, ac_norm_done)

        def ln_s3(gt):
            nonlocal ve_grp_evac_last, ve_xnT_own_last
            ti = gt % 4
            if gt == 0:
                wait("tensor", "LCON", lx_const)
            for half in range(2):
                wait("tensor", "AC", ac_norm_half[gt][half])
                pst = psT0 if half == 0 else psT1
                wait("tensor", "VE", psT_war[half])
                last = None
                for q8 in range(8):
                    dc = half * 8 + q8
                    def f(E, pst=pst, q8=q8, dc=dc, gt=gt):
                        return E.transpose(
                            pst[:, q8 * 128:(q8 + 1) * 128],
                            xn_nat[gt % 2][:, dc * 128:(dc + 1) * 128],
                            id_sb[:, :])
                    if q8 < 7:
                        emit_i("tensor", f)
                    else:
                        last = emit_i("tensor", f, "PE")
                pe_trans_tile[gt] = last
                wait("vector", "PE", last + LAG)
                dst_full = xnT_grp.rearrange("p (dc t) -> p dc t", dc=16)[
                    :, half * 8:(half + 1) * 8, ti * 128:(ti + 1) * 128]
                src_full = pst.rearrange("p (dc t) -> p dc t", dc=8)
                emit_i("vector", lambda E, d=dst_full, s=src_full: E.tensor_copy(d, s), "VE")
                dst_own = xnT.rearrange("p (dc t) -> p dc t", dc=16)[
                    :, half * 8:(half + 1) * 8, gt * 64:gt * 64 + 64]
                src_own = pst.rearrange("p (dc t two) -> p dc t two", dc=8, two=2)[:, :, :, 0]
                ev = emit_i("vector", lambda E, d=dst_own, s=src_own: E.tensor_copy(d, s), "VE")
                psT_war[half] = ev
                ve_grp_evac_last = ev
                if gt == 7:
                    ve_xnT_th0[0] = ev
                if gt == GT - 1:
                    ve_xnT_own_last = ev

        def ln_kv(g4):
            nonlocal ve_kT_evac
            wait("tensor", "LWK", lw_kv)
            wait("tensor", "VE", ve_grp_evac_last)
            wait("tensor", "VE", ve_kT_evac)
            last = None
            for dc in range(DC):
                def f(E, dc=dc):
                    return E.matmul(psC[:, :],
                                    wk_sb[:, dc * 128:(dc + 1) * 128],
                                    xnT_grp[:, dc * 512:(dc + 1) * 512],
                                    start=(dc == 0), stop=(dc == DC - 1))
                if dc < DC - 1:
                    emit_i("tensor", f)
                else:
                    last = emit_i("tensor", f, "PE")
            wait("vector", "PE", last + LAG)
            ve_kT_evac = emit_i("vector", lambda E, g4=g4: E.tensor_copy(
                kT[:, g4 * 512:(g4 + 1) * 512], psC[:, :]), "VE")
            for ti in range(4):
                jt = g4 * 4 + ti
                vps = psD if jt % 2 == 0 else psE
                wait("tensor", "VE", ve_v_evac[jt % 2])
                last = None
                for dc in range(DC):
                    def f(E, dc=dc, ti=ti, vps=vps):
                        return E.matmul(vps[:, 0:128],
                                        xnT_grp[:, dc * 512 + ti * 128: dc * 512 + (ti + 1) * 128],
                                        wv_sb[:, dc * 128:(dc + 1) * 128],
                                        start=(dc == 0), stop=(dc == DC - 1))
                    if dc < DC - 1:
                        emit_i("tensor", f)
                    else:
                        last = emit_i("tensor", f, "PE")
                wait("vector", "PE", last + LAG)
                ve_v_evac[jt % 2] = emit_i("vector", lambda E, jt=jt, vps=vps: E.tensor_copy(
                    v_sb[:, jt * 128:(jt + 1) * 128], vps[:, 0:128]), "VE")

        wq_sb = [pool[:, P_WQ + i * 2048: P_WQ + (i + 1) * 2048] for i in range(2)]
        pe_wq_done = {}
        pe_wq_buf_last = {0: 0, 1: 0}
        ve_q_evac = {}
        psF_war = [("VE", 0)]
        ve_xnT_th0 = [0]
        q1_first = [True]

        def q_proj_half(ec, th):
            lws = "LWQ0" if ec % 2 == 0 else "LWQ1"
            wait("gpsimd", "PE", pe_wq_buf_last[ec % 2])
            lw = emit_i("gpsimd", lambda E, ec=ec: E.dma_start(
                out=wq_sb[ec % 2], in_=wq_d[ec]), lws, 16)
            wait("tensor", lws, lw)
            if th == 0:
                wait("tensor", "VE", ve_xnT_th0[0])
            elif q1_first[0]:
                wait("tensor", "VE", ve_xnT_own_last)
                q1_first[0] = False
            weng, wv = psF_war[0]
            wait("tensor", weng, wv)
            last = None
            for dc in range(DC):
                def f(E, ec=ec, th=th, dc=dc):
                    return E.matmul(psF[:, :],
                                    wq_sb[ec % 2][:, dc * 128:(dc + 1) * 128],
                                    xnT[:, dc * 1024 + th * 512: dc * 1024 + (th + 1) * 512],
                                    start=(dc == 0), stop=(dc == DC - 1))
                if dc < DC - 1:
                    emit_i("tensor", f)
                else:
                    last = emit_i("tensor", f, "PE")
            pe_wq_done[ec] = last
            pe_wq_buf_last[ec % 2] = last
            wait("vector", "PE", last + LAG)
            ve = emit_i("vector", lambda E, ec=ec, th=th: E.tensor_scalar_mul(
                qT[:, ec * 1024 + th * 512: ec * 1024 + (th + 1) * 512],
                psF[:, :], QK_SCALE), "VE")
            psF_war[0] = ("VE", ve)
            ve_q_evac[ec] = ve

        st1s = {}
        for gt in range(GT + 1):
            if gt >= 1:
                ln_s2(st1s[gt - 1])
            if gt < GT:
                st1s[gt] = ln_s1(gt)
            if gt >= 1:
                ln_s3(gt - 1)
                if (gt - 1) % 4 == 3:
                    ln_kv((gt - 1) // 4)
            if gt >= 9:
                q_proj_half(2 * (gt - 9), 0)
                q_proj_half(2 * (gt - 9) + 1, 0)

        ac_norm_final = ac_norm_done

        wait("scalar", "VE", st1s[14]["ve_sum"])
        emit_i("scalar", lambda E: E.dma_start(out=misc[:, 0:2048], in_=bo_d), "LYI", 16)

        probsT = pool[:, P_PT:P_PT + 4096]

        wait("tensor", "VE", ve_kT_evac)
        wait("tensor", "VE", max(ve_v_evac.values()))
        wait("tensor", "LMK", lx_masks)

        sc_banks = [psA, psB, psC, psD]
        sc_ring = [0, 1, 2, 3]
        sc_war = {0: ("VE", 0), 1: ("VE", 0), 2: ("VE", 0), 3: ("VE", 0)}
        cidx = 0
        pstW = {0: ("VE", psT_war[0]), 1: ("VE", psT_war[1])}
        av_banks = {0: psE, 1: psF}
        av_war0 = [("VE", max(ve_v_evac.values()))]

        pe_trans_slot = {}
        ve_avevac_last = 0
        ac_avevac_last = 0
        ac_exp_done = {}
        ve_partials_read = {}

        def p2_stat_mk(par, u):
            base = 24 + (par * 2 + u % 2) * 12
            def f(col):
                return stats[:, base + col: base + col + 1]
            return f, base

        probs_scaled6 = [pool[:, P_PSCALED + i * 2048: P_PSCALED + (i + 1) * 2048]
                         for i in range(6)]

        def stageA1(u, h, g):
            nonlocal cidx
            E_ext = 512 * (g + 1)
            C = g + 1
            out = dict(u=u, h=h, g=g, E_ext=E_ext, njt=E_ext // 128, deps=[],
                       exps=[])
            wait("tensor", "VE", ve_q_evac[h])
            for par, tk in enumerate((2 * g, 2 * g + 1)):
                slot = (u % 3) * 2 + par
                pslot = probs_scaled6[slot]
                ps, psbase = p2_stat_mk(par, u)
                ac_exp = 0
                slot0 = cidx % 4
                cpes = []
                for c in range(C):
                    bi = sc_ring[cidx % 4]
                    weng, wv = sc_war[bi]
                    wait("tensor", weng, wv)
                    diag = (c == C - 1)
                    def f(E, h=h, tk=tk, c=c, bank=sc_banks[bi], diag=diag):
                        return E.matmul(bank[:, :],
                                        qT[:, h * 1024 + tk * 128: h * 1024 + (tk + 1) * 128],
                                        kT[:, c * 512:(c + 1) * 512],
                                        start=True, stop=not diag)
                    if not diag:
                        cpe = emit_i("tensor", f, "PE")
                    else:
                        emit_i("tensor", f)
                        m = mask_E if (tk % 2 == 0) else mask_L
                        def fm(E, bank=sc_banks[bi], m=m):
                            return E.matmul(bank[:, :], id_sb[:, :], m,
                                            start=False, stop=True)
                        cpe = emit_i("tensor", fm, "PE")
                    cpes.append(cpe)
                    cidx += 1
                    if c == 0:
                        wait("vector", "PE", cpe + LAG)
                        wait("vector", "AC", ac_exp_done.get((par, u % 2), 0))
                        ve_negm = emit_i("vector", lambda E, ps=ps, b0=sc_banks[bi]: E.reduce_max(
                            ps(8), b0[:, 0:128], axis=AX.X, negate=True), "VE")
                runs = [(c, 1) for c in range(C)]
                for ri, (c0, rlen) in enumerate(runs):
                    wait("scalar", "PE", cpes[c0 + rlen - 1] + LAG)
                    if ri == 0:
                        wait("scalar", "VE", ve_negm)
                        wait("scalar", "PE", pe_trans_slot.get(slot, 0))
                        wait("scalar", "VE", ve_partials_read.get((par, u % 2), 0))
                    sb = (slot0 + c0) % 4
                    def fx(E, c0=c0, rlen=rlen, sb=sb, ps=ps, pslot=pslot, ri=ri):
                        return E.activation(
                            pslot[:, c0 * 512:(c0 + rlen) * 512],
                            scores4[:, sb * 512:(sb + rlen) * 512], AF.Exp,
                            bias=ps(8), accum_out=ps(ri))
                    ac_exp = emit_i("scalar", fx, "AC")
                    for k in range(rlen):
                        sc_war[(sb + k)] = ("AC", ac_exp)
                ac_exp_done[(par, u % 2)] = ac_exp
                out["exps"].append((par, slot, pslot, ps, psbase, ac_exp, len(runs)))
            return out

        def stageA2(info):
            u = info["u"]
            E_ext, C = info["E_ext"], info["g"] + 1
            for (par, slot, pslot, ps, psbase, ac_exp, nruns) in info["exps"]:
                wait("vector", "AC", ac_exp)
                if nruns > 1:
                    ve_z = emit_i("vector", lambda E, ps=ps, psbase=psbase, nruns=nruns: E.reduce_sum(
                        ps(9), stats[:, psbase: psbase + nruns], axis=AX.X), "VE")
                    zcol = 9
                    wait("vector", "VE", ve_z)
                    ve_partials_read[(par, u % 2)] = ve_z
                else:
                    zcol = 0
                ve_zrec = emit_i("vector", lambda E, ps=ps, zcol=zcol: E.reciprocal(
                    ps(10), ps(zcol)), "VE")
                if nruns == 1:
                    ve_partials_read[(par, u % 2)] = ve_zrec
                wait("vector", "VE", ve_zrec)
                ve_scale = emit_i("vector", lambda E, E_ext=E_ext, pslot=pslot, ps=ps: E.tensor_scalar_mul(
                    pslot[:, 0:E_ext], pslot[:, 0:E_ext], ps(10)), "VE")
                info["deps"].append((slot, ve_scale))

        def stageB(info):
            nonlocal ve_avevac_last, ac_avevac_last
            u, h, g, njt = info["u"], info["h"], info["g"], info["njt"]
            nbatches = (njt + 7) // 8
            ve_evac_b = []
            for b in range(nbatches):
                n_in = min(8, njt - b * 8)
                evs = 0
                for par in range(2):
                    slot, ve_scale = info["deps"][par]
                    if b == 0:
                        wait("tensor", "VE", ve_scale)
                    bank = (u + par) % 2
                    pst = psT0 if bank == 0 else psT1
                    weng, wv = pstW[bank]
                    wait("tensor", weng, wv)
                    last = None
                    for q8 in range(n_in):
                        jt = b * 8 + q8
                        def f(E, pst=pst, q8=q8, jt=jt, slot=slot):
                            return E.transpose(
                                pst[:, q8 * 128:(q8 + 1) * 128],
                                probs_scaled6[slot][:, jt * 128:(jt + 1) * 128],
                                id_sb[:, :])
                        if q8 < n_in - 1:
                            emit_i("tensor", f)
                        else:
                            last = emit_i("tensor", f, "PE")
                    pe_trans_slot[slot] = last
                    wait("vector", "PE", last + LAG)
                    dst = probsT.rearrange("p (jt i) -> p jt i", jt=16)[
                        :, b * 8: b * 8 + n_in, par * 128:(par + 1) * 128]
                    srcp = pst.rearrange("p (jt i) -> p jt i", jt=8)[:, 0:n_in, :]
                    ev = emit_i("vector", lambda E, d=dst, s=srcp: E.tensor_copy(d, s), "VE")
                    evs = ev
                    pstW[bank] = ("VE", ev)
                ve_evac_b.append(evs)
            bank_i = u % 2
            avp = av_banks[bank_i]
            weng, wv = av_war0[0] if bank_i == 0 else psF_war[0]
            wait("tensor", weng, wv)
            last = None
            for jt in range(njt):
                if jt % 8 == 0:
                    wait("tensor", "VE", ve_evac_b[jt // 8])
                def f(E, jt=jt, avp=avp, njt=njt):
                    return E.matmul(avp[:, 0:256],
                                    v_sb[:, jt * 128:(jt + 1) * 128],
                                    probsT[:, jt * 256:(jt + 1) * 256],
                                    start=(jt == 0), stop=(jt == njt - 1))
                if jt < njt - 1:
                    emit_i("tensor", f)
                else:
                    last = emit_i("tensor", f, "PE")
            wait("scalar", "PE", last + LAG)
            ac_avevac_last = emit_i("scalar", lambda E, h=h, g=g, avp=avp: E.copy(
                qT[:, h * 1024 + g * 256: h * 1024 + (g + 1) * 256], avp[:, 0:256]), "AC")
            if bank_i == 0:
                av_war0[0] = ("AC", ac_avevac_last)
            else:
                psF_war[0] = ("AC", ac_avevac_last)

        PIPE_LAG = 3
        q_proj_half(0, 1)
        q_proj_half(1, 1)
        infos = []
        units = [(h, g) for h in range(H) for g in range(4)]
        for u, (h, g) in enumerate(units):
            if h + 2 < H and g == 0:
                q_proj_half(h + 2, 1)
            if u >= PIPE_LAG:
                stageB(infos[u - PIPE_LAG])
            infos.append(stageA1(u, h, g))
            if u >= 1:
                stageA2(infos[u - 1])
        stageA2(infos[-1])
        for uu in range(len(units) - PIPE_LAG, len(units)):
            stageB(infos[uu])

        pe_p2_end = cnt["PE"]
        ve_p2_end = cnt["VE"]

        wait("vector", "LYI", cnt["LYI"])
        for tt in range(NT):
            ysl = y_sb[:, tt * 2048:(tt + 1) * 2048]
            emit_i("vector", lambda E, ysl=ysl: E.tensor_add(
                ysl, ysl, misc[:, 0:2048]), "VE")
        pe_wo0_ok = max(pe_trans_slot.get(4, 0), pe_trans_slot.get(5, 0),
                        max(pe_wq_done.values()))

        wo_sb = [pool[:, P_WO + i * 8192: P_WO + (i + 1) * 8192] for i in range(2)]
        wo_ps = [psA, psB]
        pe_wo_done = {}
        ve_wo_evac = {0: ve_p2_end, 1: ve_p2_end}
        first_wo = True
        for es in range(4):
            if es == 0:
                wait("gpsimd", "PE", pe_wo0_ok)
            elif es == 1:
                wait("gpsimd", "PE", pe_p2_end)
            else:
                wait("gpsimd", "PE", pe_wo_done[es - 2])
            lw = emit_i("gpsimd", lambda E, es=es: E.dma_start(
                out=wo_sb[es % 2], in_=wo_d[es]), "LWO0" if es % 2 == 0 else "LWO1", 16)
            wait("tensor", "LWO0" if es % 2 == 0 else "LWO1", lw)
            if first_wo:
                wait("tensor", "AC", ac_avevac_last)
                first_wo = False
            for tt in range(NT):
                ps = wo_ps[tt % 2]
                wait("tensor", "VE", ve_wo_evac[tt % 2])
                last = None
                for cc in range(16):
                    def f(E, es=es, tt=tt, cc=cc, ps=ps):
                        return E.matmul(ps[:, :],
                                        qT[:, cc * 1024 + tt * 128: cc * 1024 + (tt + 1) * 128],
                                        wo_sb[es % 2][:, cc * 512:(cc + 1) * 512],
                                        start=(cc == 0), stop=(cc == 15))
                    if cc < 15:
                        emit_i("tensor", f)
                    else:
                        last = emit_i("tensor", f, "PE")
                pe_wo_done[es] = last
                wait("vector", "PE", last + LAG)
                ysl = y_sb[:, tt * 2048 + es * 512: tt * 2048 + (es + 1) * 512]
                ve_wo_evac[tt % 2] = emit_i("vector", lambda E, ysl=ysl, ps=ps: E.tensor_add(
                    ysl, ysl, ps[:, :]), "VE")

        ve_p3_end = cnt["VE"]
        pe_p3_end = cnt["PE"]

        wg_sb = pool[:, P_WG:P_WG + 8192]
        wu_sb = pool[:, P_WU:P_WU + 8192]
        wd_sb = pool[:, P_WD:P_WD + 8192]
        hact = qT[:, 0:4096]
        gs = [qT[:, 4096 + i * 512: 4096 + (i + 1) * 512] for i in range(2)]

        pe_gate_done = {}
        pe_down_done = {}
        ve_down_evac = {0: ve_p3_end, 1: ve_p3_end}
        ve_final_tt = {}
        ac_silu_prev = {0: 0, 1: 0}
        ve_hact = {}
        GU_banks = [(psA, psB), (psC, psD)]
        D_banks2 = {0: (psE, psF), 1: (psA, psB)}
        first_gate = True
        first_down = True

        for s in range(16):
            if s == 0:
                wait("gpsimd", "PE", pe_p2_end)
            else:
                wait("gpsimd", "PE", pe_gate_done[s - 1])
            lw_gs = emit_i("gpsimd", lambda E, s=s: E.dma_start(out=wg_sb, in_=wg_d[s]), "LWG", 16)
            if s == 0:
                wait("gpsimd", "PE", pe_wo_done[2])
            lw_us = emit_i("gpsimd", lambda E, s=s: E.dma_start(out=wu_sb, in_=wu_d[s]), "LWU", 16)
            if s == 0:
                wait("gpsimd", "PE", pe_p3_end)
            else:
                wait("gpsimd", "PE", pe_down_done[s - 1])
            lw_ds = emit_i("gpsimd", lambda E, s=s: E.dma_start(out=wd_sb, in_=wd_d[s]), "LWD", 16)

            wait("tensor", "LWG", lw_gs)
            wait("tensor", "LWU", lw_us)
            for hq in range(4):
                for th in range(2):
                    pair = hq * 2 + th
                    gb, ub = GU_banks[pair % 2]
                    if first_gate:
                        wait("tensor", "VE", ve_p3_end)
                        first_gate = False
                    if pair < 2 and s > 0:
                        wait("tensor", "VE", ve_down_evac[1] if pair == 0
                             else ve_hact.get((s - 1, 7), 0))
                    wait("tensor", "AC", ac_silu_prev[pair % 2])
                    wait("tensor", "VE", ve_hact.get((s, pair - 2),
                                                     ve_hact.get((s - 1, 6 + pair), 0)))
                    lastg = None
                    for dc in range(DC):
                        def f(E, s=s, hq=hq, th=th, dc=dc, gb=gb):
                            return E.matmul(gb[:, :],
                                            wg_sb[:, dc * 512 + hq * 128: dc * 512 + (hq + 1) * 128],
                                            xnT[:, dc * 1024 + th * 512: dc * 1024 + (th + 1) * 512],
                                            start=(dc == 0), stop=(dc == DC - 1))
                        if dc < DC - 1:
                            emit_i("tensor", f)
                        else:
                            lastg = emit_i("tensor", f, "PE")
                    lastu = None
                    for dc in range(DC):
                        def f(E, s=s, hq=hq, th=th, dc=dc, ub=ub):
                            return E.matmul(ub[:, :],
                                            wu_sb[:, dc * 512 + hq * 128: dc * 512 + (hq + 1) * 128],
                                            xnT[:, dc * 1024 + th * 512: dc * 1024 + (th + 1) * 512],
                                            start=(dc == 0), stop=(dc == DC - 1))
                        if dc < DC - 1:
                            emit_i("tensor", f)
                        else:
                            lastu = emit_i("tensor", f, "PE")
                    pe_gate_done[s] = lastu
                    wait("scalar", "PE", lastg + LAG)
                    ac_silu = emit_i("scalar", lambda E, pair=pair, gb=gb: E.activation(
                        gs[pair % 2], gb[:, :], AF.Silu), "AC")
                    ac_silu_prev[pair % 2] = ac_silu
                    wait("vector", "AC", ac_silu)
                    wait("vector", "PE", lastu + LAG)
                    dsth = hact[:, hq * 1024 + th * 512: hq * 1024 + (th + 1) * 512]
                    ve_hact[(s, pair)] = emit_i("vector", lambda E, d=dsth, pair=pair, ub=ub: E.tensor_mul(
                        d, gs[pair % 2], ub[:, :]), "VE")

            wait("tensor", "LWD", lw_ds)
            wait("tensor", "VE", ve_hact[(s, 7)])
            for tt in range(NT):
                for e2 in range(2):
                    b0, b1 = D_banks2[e2]
                    if first_down:
                        wait("tensor", "VE", ve_p3_end)
                        first_down = False
                    if tt == 0 and e2 == 1:
                        wait("tensor", "VE", ve_hact[(s, 7)])
                        wait("tensor", "AC", ac_silu_prev[0])
                    wait("tensor", "VE", ve_down_evac[e2])
                    last = None
                    for hq in range(4):
                        for ei, bank in ((0, b0), (1, b1)):
                            es = e2 * 2 + ei
                            def f(E, s=s, tt=tt, hq=hq, es=es, bank=bank):
                                return E.matmul(bank[:, :],
                                                hact[:, hq * 1024 + tt * 128: hq * 1024 + (tt + 1) * 128],
                                                wd_sb[:, hq * 2048 + es * 512: hq * 2048 + (es + 1) * 512],
                                                start=(hq == 0), stop=(hq == 3))
                            if hq == 3 and ei == 1:
                                last = emit_i("tensor", f, "PE")
                            else:
                                emit_i("tensor", f)
                    pe_down_done[s] = last
                    wait("vector", "PE", last + LAG)
                    for ei, bank in ((0, b0), (1, b1)):
                        es = e2 * 2 + ei
                        ysl = y_sb[:, tt * 2048 + es * 512: tt * 2048 + (es + 1) * 512]
                        ve_down_evac[e2] = emit_i(
                            "vector",
                            lambda E, ysl=ysl, bank=bank: E.tensor_add(ysl, ysl, bank[:, :]),
                            "VE")
                        if s == 15:
                            ve_final_tt[tt] = ve_down_evac[e2]

        wait("tensor", "VE", cnt["VE"])
        wait("tensor", "AC", cnt["AC"])
        for _ in range(LAG + 1):
            emit_i("tensor", lambda E: E.matmul(
                psA[:, 0:128], id_sb[:, :], id_sb[:, :], start=True, stop=True), "PE")

        for tt in range(NT):
            wait("sync", "VE", ve_final_tt[tt])
            emit_i("sync", lambda E, tt=tt: E.dma_start(
                out=y_d[tt * 128:(tt + 1) * 128, :],
                in_=y_sb[:, tt * 2048:(tt + 1) * 2048]), "SO", 16)
        wait("sync", "SO", cnt["SO"])

        with nc.Block() as block:
            @block.sync
            def _(E):
                for f in prog["sync"]:
                    f(E)

            @block.gpsimd
            def _(E):
                for f in prog["gpsimd"]:
                    f(E)

            @block.tensor
            def _(E):
                for f in prog["tensor"]:
                    f(E)

            @block.scalar
            def _(E):
                for f in prog["scalar"]:
                    f(E)

            @block.vector
            def _(E):
                for f in prog["vector"]:
                    f(E)

    return nc


def _host_layouts(Wq, Wk, Wv, Wo, Wg, Wu, Wd, bo):
    wq_l = np.ascontiguousarray(
        Wq.reshape(16, 128, 16, 128).transpose(2, 1, 0, 3).reshape(16, 128, 2048))
    wk_l = np.ascontiguousarray(Wk.reshape(16, 128, 128).transpose(1, 0, 2).reshape(128, 2048))
    wv_l = np.ascontiguousarray(Wv.reshape(16, 128, 128).transpose(1, 0, 2).reshape(128, 2048))
    wo_l = np.ascontiguousarray(
        Wo.reshape(16, 128, 4, 512).transpose(2, 1, 0, 3).reshape(4, 128, 8192))
    WgT = np.ascontiguousarray(Wg.T)
    WuT = np.ascontiguousarray(Wu.T)
    WdT = np.ascontiguousarray(Wd.T)
    wg_l = np.ascontiguousarray(
        WgT.reshape(16, 128, 16, 512).transpose(2, 1, 0, 3).reshape(16, 128, 8192))
    wu_l = np.ascontiguousarray(
        WuT.reshape(16, 128, 16, 512).transpose(2, 1, 0, 3).reshape(16, 128, 8192))
    wd_l = np.ascontiguousarray(
        WdT.reshape(16, 4, 128, 4, 512).transpose(0, 2, 1, 3, 4).reshape(16, 128, 8192))
    bo_bc = np.ascontiguousarray(np.broadcast_to(bo.reshape(1, -1), (128, 2048)))
    return dict(wq_l=wq_l, wk_l=wk_l, wv_l=wv_l, wo_l=wo_l,
                wg_l=wg_l, wu_l=wu_l, wd_l=wd_l, bo_bc=bo_bc)


def _masks(parity):
    i = np.arange(128)[:, None]
    jc = np.arange(512)[None, :]
    if parity == 0:
        key_true = jc
    else:
        key_true = np.where(jc % 2 == 0, jc + 1, jc - 1)
    mE = np.where(key_true <= 2 * i + parity, 0.0, NEG).astype(np.float32)
    mL = np.where(key_true <= 256 + 2 * i + parity, 0.0, NEG).astype(np.float32)
    return np.stack([mE, mL]).astype(ml_dtypes.bfloat16)


def kernel(L0_attn_x, L0_attn_Wq, L0_attn_Wk, L0_attn_Wv, L0_attn_Wo, L0_attn_bo,
           L0_ffn_w_gate, L0_ffn_w_up, L0_ffn_w_down):
    x = np.ascontiguousarray(np.asarray(L0_attn_x, dtype=np.float32))
    shared = _host_layouts(
        np.asarray(L0_attn_Wq, np.float32), np.asarray(L0_attn_Wk, np.float32),
        np.asarray(L0_attn_Wv, np.float32), np.asarray(L0_attn_Wo, np.float32),
        np.asarray(L0_ffn_w_gate, np.float32), np.asarray(L0_ffn_w_up, np.float32),
        np.asarray(L0_ffn_w_down, np.float32), np.asarray(L0_attn_bo, np.float32))
    ident = np.eye(128, dtype=ml_dtypes.bfloat16)
    masks = [_masks(0), _masks(1)]

    nc = build_nc()
    in_maps = []
    for core in range(8):
        b, p = core // 2, core % 2
        xb = x[b]
        xperm = np.ascontiguousarray(
            xb.reshape(1024, 2, 2048)[:, [p, 1 - p], :].reshape(2048, 2048))
        in_map = dict(shared)
        in_map["x"] = xperm
        in_map["x_own"] = np.ascontiguousarray(xb[p::2, :])
        in_map["masks"] = masks[p]
        in_map["ident"] = ident
        in_maps.append(in_map)

    res = run_bass_kernel_spmd(nc, in_maps, core_ids=list(range(8)))
    out = np.empty((B, T, D), dtype=np.float32)
    for core in range(8):
        b, p = core // 2, core % 2
        out[b, p::2, :] = res.results[core]["y"]
    return out


# version 24
# speedup vs baseline: 1.0513x; 1.0015x over previous
import math
import contextlib
import numpy as np
import ml_dtypes
import concourse.bass as bass
import concourse.mybir as mybir
from concourse.bass_utils import run_bass_kernel_spmd

F32 = mybir.dt.float32
BF16 = mybir.dt.bfloat16
AF = mybir.ActivationFunctionType
AX = mybir.AxisListType
ALU = mybir.AluOpType

B, T, D = 4, 2048, 2048
H, DH = 16, 128
HID = 8192
TO = 1024
NT = 8
GT = 16
DC = 16
QK_SCALE = 1.0 / math.sqrt(DH)
NEG = -1.0e30

ENGINES = ("sync", "gpsimd", "tensor", "scalar", "vector")


def build_nc():
    nc = bass.Bass("TRN2", target_bir_lowering=False)

    x_d = nc.dram_tensor("x", [T, D], F32, kind="ExternalInput").ap()
    x_own_d = nc.dram_tensor("x_own", [TO, D], F32, kind="ExternalInput").ap()
    wq_d = nc.dram_tensor("wq_l", [16, 128, 2048], F32, kind="ExternalInput").ap()
    wk_d = nc.dram_tensor("wk_l", [128, 2048], F32, kind="ExternalInput").ap()
    wv_d = nc.dram_tensor("wv_l", [128, 2048], F32, kind="ExternalInput").ap()
    wo_d = nc.dram_tensor("wo_l", [4, 128, 8192], F32, kind="ExternalInput").ap()
    wg_d = nc.dram_tensor("wg_l", [16, 128, 8192], F32, kind="ExternalInput").ap()
    wu_d = nc.dram_tensor("wu_l", [16, 128, 8192], F32, kind="ExternalInput").ap()
    wd_d = nc.dram_tensor("wd_l", [16, 128, 8192], F32, kind="ExternalInput").ap()
    bo_d = nc.dram_tensor("bo_bc", [128, 2048], F32, kind="ExternalInput").ap()
    mask_d = nc.dram_tensor("masks", [2, 128, 512], BF16, kind="ExternalInput").ap()
    id_d = nc.dram_tensor("ident", [128, 128], BF16, kind="ExternalInput").ap()
    y_d = nc.dram_tensor("y", [TO, D], F32, kind="ExternalOutput").ap()

    ctx = contextlib.ExitStack()
    with ctx:
        y_sb = ctx.enter_context(nc.sbuf_tensor("y_sb", [128, NT * 2048], F32))
        xtS = ctx.enter_context(nc.sbuf_tensor("xtS", [128, 2 * 2048], F32))
        misc = ctx.enter_context(nc.sbuf_tensor("misc", [128, 2048], F32))
        stats = ctx.enter_context(nc.sbuf_tensor("stats", [128, 80], F32))

        xnT = ctx.enter_context(nc.sbuf_tensor("xnT", [128, DC * TO], BF16))
        qT = ctx.enter_context(nc.sbuf_tensor("qT", [128, 16 * TO], BF16))
        pool = ctx.enter_context(nc.sbuf_tensor("pool", [128, 26624], BF16))
        id_sb = ctx.enter_context(nc.sbuf_tensor("id_sb", [128, 128], BF16))
        masks_sb = ctx.enter_context(nc.sbuf_tensor("masks_sb", [128, 1024], BF16))

        P_XNT_GRP, P_XN_NAT, P_WQ, P_WK, P_WV, P_SQ, P_KT, P_VSB = (
            0, 8192, 12288, 16384, 18432, 20480, 22528, 24576)
        P_PSCALED, P_PT = 0, 16384
        P_WO = 8192
        P_WG, P_WU, P_WD = 0, 8192, 16384

        scores4 = ctx.enter_context(nc.psum_tensor("scores4", [128, 2048], F32))
        psA = scores4[:, 0:512]
        psB = scores4[:, 512:1024]
        psC = scores4[:, 1024:1536]
        psD = scores4[:, 1536:2048]
        psE = ctx.enter_context(nc.psum_tensor("psE", [128, 512], F32))
        psF = ctx.enter_context(nc.psum_tensor("psF", [128, 512], F32))
        psT0 = ctx.enter_context(nc.psum_tensor("psT0", [128, 1024], BF16))
        psT1 = ctx.enter_context(nc.psum_tensor("psT1", [128, 1024], BF16))

        _sem_names = ["LCON", "LYI", "LMK", "LXT0", "LXT1", "LXT2", "LWK", "LWQ0", "LWQ1",
                      "LWO0", "LWO1", "LWG", "LWU", "LWD", "SO", "GPX",
                      "PE", "AC", "VE"]
        SEM = {nm: ctx.enter_context(nc.semaphore(nm)) for nm in _sem_names}

        prog = {e: [] for e in ENGINES}
        cnt = {k: 0 for k in SEM}

        def emit(eng, fn):
            prog[eng].append(fn)

        def wait(eng, sem, val):
            if val and val > 0:
                emit(eng, lambda E, s=SEM[sem], v=val: E.wait_ge(s, v))

        def emit_i(eng, fn, sem=None, by=1):
            if sem is None:
                emit(eng, fn)
                return None
            cnt[sem] += by
            v = cnt[sem]
            emit(eng, lambda E, f=fn, s=SEM[sem], b=by: f(E).then_inc(s, b))
            return v

        LAG = 0

        emit_i("sync", lambda E: E.dma_start(out=id_sb[:, :], in_=id_d), "LCON", 16)
        lx_const = cnt["LCON"]
        emit_i("gpsimd", lambda E: E.memset(stats[:, 72:73], 1.0e-5), "GPX")
        gp_eps = cnt["GPX"]
        eps_ap = stats[:, 72:73]

        wk_sb = pool[:, P_WK:P_WK + 2048]
        wv_sb = pool[:, P_WV:P_WV + 2048]
        emit_i("gpsimd", lambda E: E.dma_start(out=wk_sb, in_=wk_d), "LWK", 16)
        emit_i("gpsimd", lambda E: E.dma_start(out=wv_sb, in_=wv_d), "LWK", 16)
        lw_kv = cnt["LWK"]

        for tt in range(NT):
            emit_i("scalar", lambda E, tt=tt: E.dma_start(
                out=y_sb[:, tt * 2048:(tt + 1) * 2048],
                in_=x_own_d[tt * 128:(tt + 1) * 128, :]), "LYI", 16)

        emit_i("gpsimd", lambda E: E.dma_start(out=masks_sb[:, 0:512], in_=mask_d[0]), "LMK", 16)
        emit_i("gpsimd", lambda E: E.dma_start(out=masks_sb[:, 512:1024], in_=mask_d[1]), "LMK", 16)
        lx_masks = cnt["LMK"]
        mask_E = masks_sb[:, 0:512]
        mask_L = masks_sb[:, 512:1024]

        xt3 = [xtS[:, 0:2048], xtS[:, 2048:4096], misc[:, 0:2048]]
        sq = pool[:, P_SQ:P_SQ + 2048]
        xn_nat = [pool[:, P_XN_NAT + i * 2048: P_XN_NAT + (i + 1) * 2048] for i in range(2)]
        xnT_grp = pool[:, P_XNT_GRP:P_XNT_GRP + 8192]
        kT = pool[:, P_KT:P_KT + 2048]
        v_sb = pool[:, P_VSB:P_VSB + 2048]

        def stat(slot, col):
            return stats[:, slot * 8 + col: slot * 8 + col + 1]

        ac_norm_done = 0
        ac_norm_hist = {}
        ac_norm_half = {}
        ac_sq_hist = {}
        pe_trans_tile = {}
        psT_war = {0: 0, 1: 0}
        ve_xnT_own_last = 0
        ve_kT_evac = 0
        ve_v_evac = {0: 0, 1: 0}
        ve_grp_evac_last = 0

        def ln_s1(gt):
            sl = gt % 3
            xt = xt3[gt % 3]
            wait("sync", "AC", ac_norm_hist.get(gt - 3, 0))
            lxs = "LXT%d" % (gt % 3)
            lxv = emit_i("sync", lambda E, gt=gt, xt=xt: E.dma_start(
                out=xt, in_=x_d[gt * 128:(gt + 1) * 128, :]), lxs, 16)
            wait("vector", lxs, lxv)
            ve_sum = emit_i("vector", lambda E, sl=sl, xt=xt: E.reduce_sum(
                stat(sl, 0), xt, axis=AX.X), "VE")
            wait("scalar", lxs, lxv)
            ac_sq = emit_i("scalar", lambda E, sl=sl, xt=xt: E.activation(
                sq, xt, AF.Square, accum_out=stat(sl, 1)), "AC")
            ac_sq_hist[gt] = ac_sq
            return dict(gt=gt, sl=sl, xt=xt, ve_sum=ve_sum, ac_sq=ac_sq)

        def ln_s2(st1):
            nonlocal ac_norm_done
            gt, sl, xt = st1["gt"], st1["sl"], st1["xt"]
            wait("vector", "VE", st1["ve_sum"])
            ve_m = emit_i("vector", lambda E, sl=sl: E.tensor_scalar_mul(
                stat(sl, 2), stat(sl, 0), 1.0 / D), "VE")
            wait("vector", "VE", ve_m)
            ve_msq = emit_i("vector", lambda E, sl=sl: E.tensor_mul(
                stat(sl, 3), stat(sl, 2), stat(sl, 2)), "VE")
            wait("vector", "AC", st1["ac_sq"])
            wait("vector", "VE", ve_msq)
            ve_var = emit_i("vector", lambda E, sl=sl: E.scalar_tensor_tensor(
                stat(sl, 4), stat(sl, 1), 1.0 / D, stat(sl, 3),
                ALU.mult, ALU.subtract), "VE")
            wait("scalar", "VE", ve_var)
            if gt == 0:
                wait("scalar", "GPX", gp_eps)
            ac_ln = emit_i("scalar", lambda E, sl=sl: E.activation(
                stat(sl, 5), stat(sl, 4), AF.Ln, bias=eps_ap), "AC")
            wait("scalar", "AC", ac_ln)
            ac_rstd = emit_i("scalar", lambda E, sl=sl: E.activation(
                stat(sl, 6), stat(sl, 5), AF.Exp, scale=-0.5), "AC")
            wait("vector", "AC", ac_rstd)
            ve_negmr = emit_i("vector", lambda E, sl=sl: E.tensor_scalar(
                stat(sl, 7), stat(sl, 2), stat(sl, 6), -1.0,
                ALU.mult, ALU.mult), "VE")
            wait("scalar", "PE", pe_trans_tile.get(gt - 2, 0))
            wait("scalar", "VE", ve_negmr)
            ac_norm_h0 = emit_i("scalar", lambda E, sl=sl, gt=gt, xt=xt: E.activation(
                xn_nat[gt % 2][:, 0:1024], xt[:, 0:1024], AF.Identity,
                bias=stat(sl, 7), scale=stat(sl, 6)), "AC")
            ac_norm_done = emit_i("scalar", lambda E, sl=sl, gt=gt, xt=xt: E.activation(
                xn_nat[gt % 2][:, 1024:2048], xt[:, 1024:2048], AF.Identity,
                bias=stat(sl, 7), scale=stat(sl, 6)), "AC")
            ac_norm_hist[gt] = ac_norm_done
            ac_norm_half[gt] = (ac_norm_h0, ac_norm_done)

        def ln_s3(gt):
            nonlocal ve_grp_evac_last, ve_xnT_own_last
            ti = gt % 4
            if gt == 0:
                wait("tensor", "LCON", lx_const)
            for half in range(2):
                wait("tensor", "AC", ac_norm_half[gt][half])
                pst = psT0 if half == 0 else psT1
                wait("tensor", "VE", psT_war[half])
                last = None
                for q8 in range(8):
                    dc = half * 8 + q8
                    def f(E, pst=pst, q8=q8, dc=dc, gt=gt):
                        return E.transpose(
                            pst[:, q8 * 128:(q8 + 1) * 128],
                            xn_nat[gt % 2][:, dc * 128:(dc + 1) * 128],
                            id_sb[:, :])
                    if q8 < 7:
                        emit_i("tensor", f)
                    else:
                        last = emit_i("tensor", f, "PE")
                pe_trans_tile[gt] = last
                wait("vector", "PE", last + LAG)
                dst_full = xnT_grp.rearrange("p (dc t) -> p dc t", dc=16)[
                    :, half * 8:(half + 1) * 8, ti * 128:(ti + 1) * 128]
                src_full = pst.rearrange("p (dc t) -> p dc t", dc=8)
                emit_i("vector", lambda E, d=dst_full, s=src_full: E.tensor_copy(d, s), "VE")
                dst_own = xnT.rearrange("p (dc t) -> p dc t", dc=16)[
                    :, half * 8:(half + 1) * 8, gt * 64:gt * 64 + 64]
                src_own = pst.rearrange("p (dc t two) -> p dc t two", dc=8, two=2)[:, :, :, 0]
                ev = emit_i("vector", lambda E, d=dst_own, s=src_own: E.tensor_copy(d, s), "VE")
                psT_war[half] = ev
                ve_grp_evac_last = ev
                if gt == 7:
                    ve_xnT_th0[0] = ev
                if gt == GT - 1:
                    ve_xnT_own_last = ev

        def ln_kv(g4):
            nonlocal ve_kT_evac
            wait("tensor", "LWK", lw_kv)
            wait("tensor", "VE", ve_grp_evac_last)
            wait("tensor", "VE", ve_kT_evac)
            last = None
            for dc in range(DC):
                def f(E, dc=dc):
                    return E.matmul(psC[:, :],
                                    wk_sb[:, dc * 128:(dc + 1) * 128],
                                    xnT_grp[:, dc * 512:(dc + 1) * 512],
                                    start=(dc == 0), stop=(dc == DC - 1))
                if dc < DC - 1:
                    emit_i("tensor", f)
                else:
                    last = emit_i("tensor", f, "PE")
            wait("vector", "PE", last + LAG)
            ve_kT_evac = emit_i("vector", lambda E, g4=g4: E.tensor_copy(
                kT[:, g4 * 512:(g4 + 1) * 512], psC[:, :]), "VE")
            for ti in range(4):
                jt = g4 * 4 + ti
                vps = psD if jt % 2 == 0 else psE
                wait("tensor", "VE", ve_v_evac[jt % 2])
                last = None
                for dc in range(DC):
                    def f(E, dc=dc, ti=ti, vps=vps):
                        return E.matmul(vps[:, 0:128],
                                        xnT_grp[:, dc * 512 + ti * 128: dc * 512 + (ti + 1) * 128],
                                        wv_sb[:, dc * 128:(dc + 1) * 128],
                                        start=(dc == 0), stop=(dc == DC - 1))
                    if dc < DC - 1:
                        emit_i("tensor", f)
                    else:
                        last = emit_i("tensor", f, "PE")
                wait("vector", "PE", last + LAG)
                ve_v_evac[jt % 2] = emit_i("vector", lambda E, jt=jt, vps=vps: E.tensor_copy(
                    v_sb[:, jt * 128:(jt + 1) * 128], vps[:, 0:128]), "VE")

        wq_sb = [pool[:, P_WQ + i * 2048: P_WQ + (i + 1) * 2048] for i in range(2)]
        pe_wq_done = {}
        pe_wq_buf_last = {0: 0, 1: 0}
        ve_q_evac = {}
        psF_war = [("VE", 0)]
        ve_xnT_th0 = [0]
        q1_first = [True]

        def q_proj_half(ec, th):
            lws = "LWQ0" if ec % 2 == 0 else "LWQ1"
            wait("gpsimd", "PE", pe_wq_buf_last[ec % 2])
            lw = emit_i("gpsimd", lambda E, ec=ec: E.dma_start(
                out=wq_sb[ec % 2], in_=wq_d[ec]), lws, 16)
            wait("tensor", lws, lw)
            if th == 0:
                wait("tensor", "VE", ve_xnT_th0[0])
            elif q1_first[0]:
                wait("tensor", "VE", ve_xnT_own_last)
                q1_first[0] = False
            weng, wv = psF_war[0]
            wait("tensor", weng, wv)
            last = None
            for dc in range(DC):
                def f(E, ec=ec, th=th, dc=dc):
                    return E.matmul(psF[:, :],
                                    wq_sb[ec % 2][:, dc * 128:(dc + 1) * 128],
                                    xnT[:, dc * 1024 + th * 512: dc * 1024 + (th + 1) * 512],
                                    start=(dc == 0), stop=(dc == DC - 1))
                if dc < DC - 1:
                    emit_i("tensor", f)
                else:
                    last = emit_i("tensor", f, "PE")
            pe_wq_done[ec] = last
            pe_wq_buf_last[ec % 2] = last
            wait("vector", "PE", last + LAG)
            ve = emit_i("vector", lambda E, ec=ec, th=th: E.tensor_scalar_mul(
                qT[:, ec * 1024 + th * 512: ec * 1024 + (th + 1) * 512],
                psF[:, :], QK_SCALE), "VE")
            psF_war[0] = ("VE", ve)
            ve_q_evac[ec] = ve

        st1s = {}
        for gt in range(GT + 1):
            if gt >= 1:
                ln_s2(st1s[gt - 1])
            if gt < GT:
                st1s[gt] = ln_s1(gt)
            if gt >= 1:
                ln_s3(gt - 1)
                if (gt - 1) % 4 == 3:
                    ln_kv((gt - 1) // 4)
            if gt >= 9:
                q_proj_half(2 * (gt - 9), 0)
                q_proj_half(2 * (gt - 9) + 1, 0)

        ac_norm_final = ac_norm_done

        wait("scalar", "VE", st1s[14]["ve_sum"])
        emit_i("scalar", lambda E: E.dma_start(out=misc[:, 0:2048], in_=bo_d), "LYI", 16)

        probsT = pool[:, P_PT:P_PT + 4096]

        wait("tensor", "VE", ve_kT_evac)
        wait("tensor", "VE", max(ve_v_evac.values()))
        wait("tensor", "LMK", lx_masks)

        sc_banks = [psA, psB, psC, psD]
        sc_ring = [0, 1, 2, 3]
        sc_war = {0: ("VE", 0), 1: ("VE", 0), 2: ("VE", 0), 3: ("VE", 0)}
        cidx = 0
        pstW = {0: ("VE", psT_war[0]), 1: ("VE", psT_war[1])}
        av_banks = {0: psE, 1: psF}
        av_war0 = [("VE", max(ve_v_evac.values()))]

        pe_trans_slot = {}
        ve_avevac_last = 0
        ac_avevac_last = 0
        ac_exp_done = {}
        ve_partials_read = {}

        def p2_stat_mk(par, u):
            base = 24 + (par * 2 + u % 2) * 12
            def f(col):
                return stats[:, base + col: base + col + 1]
            return f, base

        probs_scaled6 = [pool[:, P_PSCALED + i * 2048: P_PSCALED + (i + 1) * 2048]
                         for i in range(6)]

        def stageA1(u, h, g):
            nonlocal cidx
            E_ext = 512 * (g + 1)
            C = g + 1
            out = dict(u=u, h=h, g=g, E_ext=E_ext, njt=E_ext // 128, deps=[],
                       exps=[])
            wait("tensor", "VE", ve_q_evac[h])
            for par, tk in enumerate((2 * g, 2 * g + 1)):
                slot = (u % 3) * 2 + par
                pslot = probs_scaled6[slot]
                ps, psbase = p2_stat_mk(par, u)
                ac_exp = 0
                slot0 = cidx % 4
                cpes = []
                for c in range(C):
                    bi = sc_ring[cidx % 4]
                    weng, wv = sc_war[bi]
                    wait("tensor", weng, wv)
                    diag = (c == C - 1)
                    def f(E, h=h, tk=tk, c=c, bank=sc_banks[bi], diag=diag):
                        return E.matmul(bank[:, :],
                                        qT[:, h * 1024 + tk * 128: h * 1024 + (tk + 1) * 128],
                                        kT[:, c * 512:(c + 1) * 512],
                                        start=True, stop=not diag)
                    if not diag:
                        cpe = emit_i("tensor", f, "PE")
                    else:
                        emit_i("tensor", f)
                        m = mask_E if (tk % 2 == 0) else mask_L
                        def fm(E, bank=sc_banks[bi], m=m):
                            return E.matmul(bank[:, :], id_sb[:, :], m,
                                            start=False, stop=True)
                        cpe = emit_i("tensor", fm, "PE")
                    cpes.append(cpe)
                    cidx += 1
                    if c == 0:
                        wait("vector", "PE", cpe + LAG)
                        wait("vector", "AC", ac_exp_done.get((par, u % 2), 0))
                        ve_negm = emit_i("vector", lambda E, ps=ps, b0=sc_banks[bi]: E.reduce_max(
                            ps(8), b0[:, 0:128], axis=AX.X, negate=True), "VE")
                runs = [(c, 1) for c in range(C)]
                for ri, (c0, rlen) in enumerate(runs):
                    wait("scalar", "PE", cpes[c0 + rlen - 1] + LAG)
                    if ri == 0:
                        wait("scalar", "VE", ve_negm)
                        wait("scalar", "PE", pe_trans_slot.get(slot, 0))
                        wait("scalar", "VE", ve_partials_read.get((par, u % 2), 0))
                    sb = (slot0 + c0) % 4
                    def fx(E, c0=c0, rlen=rlen, sb=sb, ps=ps, pslot=pslot, ri=ri):
                        return E.activation(
                            pslot[:, c0 * 512:(c0 + rlen) * 512],
                            scores4[:, sb * 512:(sb + rlen) * 512], AF.Exp,
                            bias=ps(8), accum_out=ps(ri))
                    ac_exp = emit_i("scalar", fx, "AC")
                    for k in range(rlen):
                        sc_war[(sb + k)] = ("AC", ac_exp)
                ac_exp_done[(par, u % 2)] = ac_exp
                out["exps"].append((par, slot, pslot, ps, psbase, ac_exp, len(runs)))
            return out

        def stageA2(info):
            u = info["u"]
            E_ext, C = info["E_ext"], info["g"] + 1
            for (par, slot, pslot, ps, psbase, ac_exp, nruns) in info["exps"]:
                wait("vector", "AC", ac_exp)
                if nruns > 1:
                    ve_z = emit_i("vector", lambda E, ps=ps, psbase=psbase, nruns=nruns: E.reduce_sum(
                        ps(9), stats[:, psbase: psbase + nruns], axis=AX.X), "VE")
                    zcol = 9
                    wait("vector", "VE", ve_z)
                    ve_partials_read[(par, u % 2)] = ve_z
                else:
                    zcol = 0
                ve_zrec = emit_i("vector", lambda E, ps=ps, zcol=zcol: E.reciprocal(
                    ps(10), ps(zcol)), "VE")
                if nruns == 1:
                    ve_partials_read[(par, u % 2)] = ve_zrec
                wait("vector", "VE", ve_zrec)
                ve_scale = emit_i("vector", lambda E, E_ext=E_ext, pslot=pslot, ps=ps: E.tensor_scalar_mul(
                    pslot[:, 0:E_ext], pslot[:, 0:E_ext], ps(10)), "VE")
                info["deps"].append((slot, ve_scale))

        def stageB(info):
            nonlocal ve_avevac_last, ac_avevac_last
            u, h, g, njt = info["u"], info["h"], info["g"], info["njt"]
            nbatches = (njt + 7) // 8
            ve_evac_b = []
            for b in range(nbatches):
                n_in = min(8, njt - b * 8)
                evs = 0
                for par in range(2):
                    slot, ve_scale = info["deps"][par]
                    if b == 0:
                        wait("tensor", "VE", ve_scale)
                    bank = (u + par) % 2
                    pst = psT0 if bank == 0 else psT1
                    weng, wv = pstW[bank]
                    wait("tensor", weng, wv)
                    last = None
                    for q8 in range(n_in):
                        jt = b * 8 + q8
                        def f(E, pst=pst, q8=q8, jt=jt, slot=slot):
                            return E.transpose(
                                pst[:, q8 * 128:(q8 + 1) * 128],
                                probs_scaled6[slot][:, jt * 128:(jt + 1) * 128],
                                id_sb[:, :])
                        if q8 < n_in - 1:
                            emit_i("tensor", f)
                        else:
                            last = emit_i("tensor", f, "PE")
                    pe_trans_slot[slot] = last
                    wait("vector", "PE", last + LAG)
                    dst = probsT[:, par * 2048 + b * 1024:
                                 par * 2048 + b * 1024 + n_in * 128]
                    srcp = pst[:, 0:n_in * 128]
                    ev = emit_i("vector", lambda E, d=dst, s=srcp: E.tensor_copy(d, s), "VE")
                    evs = ev
                    pstW[bank] = ("VE", ev)
                ve_evac_b.append(evs)
            bank_i = u % 2
            avp = av_banks[bank_i]
            weng, wv = av_war0[0] if bank_i == 0 else psF_war[0]
            wait("tensor", weng, wv)
            last = None
            for jt in range(njt):
                if jt % 8 == 0:
                    wait("tensor", "VE", ve_evac_b[jt // 8])
                def f(E, jt=jt, avp=avp, njt=njt):
                    rhs = probsT.rearrange("p (par c) -> p par c", par=2)[
                        :, :, jt * 128:(jt + 1) * 128]
                    return E.matmul(avp[:, 0:256],
                                    v_sb[:, jt * 128:(jt + 1) * 128],
                                    rhs,
                                    start=(jt == 0), stop=(jt == njt - 1))
                if jt < njt - 1:
                    emit_i("tensor", f)
                else:
                    last = emit_i("tensor", f, "PE")
            wait("scalar", "PE", last + LAG)
            ac_avevac_last = emit_i("scalar", lambda E, h=h, g=g, avp=avp: E.copy(
                qT[:, h * 1024 + g * 256: h * 1024 + (g + 1) * 256], avp[:, 0:256]), "AC")
            if bank_i == 0:
                av_war0[0] = ("AC", ac_avevac_last)
            else:
                psF_war[0] = ("AC", ac_avevac_last)

        PIPE_LAG = 3
        q_proj_half(0, 1)
        q_proj_half(1, 1)
        infos = []
        units = [(h, g) for h in range(H) for g in range(4)]
        for u, (h, g) in enumerate(units):
            if h + 2 < H and g == 0:
                q_proj_half(h + 2, 1)
            if u >= PIPE_LAG:
                stageB(infos[u - PIPE_LAG])
            if u >= 1:
                stageA2(infos[u - 1])
            infos.append(stageA1(u, h, g))
        stageA2(infos[-1])
        for uu in range(len(units) - PIPE_LAG, len(units)):
            stageB(infos[uu])

        pe_p2_end = cnt["PE"]
        ve_p2_end = cnt["VE"]

        wait("vector", "LYI", cnt["LYI"])
        for tt in range(NT):
            ysl = y_sb[:, tt * 2048:(tt + 1) * 2048]
            emit_i("vector", lambda E, ysl=ysl: E.tensor_add(
                ysl, ysl, misc[:, 0:2048]), "VE")
        pe_wo0_ok = max(pe_trans_slot.get(4, 0), pe_trans_slot.get(5, 0),
                        max(pe_wq_done.values()))

        wo_sb = [pool[:, P_WO + i * 8192: P_WO + (i + 1) * 8192] for i in range(2)]
        wo_ps = [psA, psB]
        pe_wo_done = {}
        ve_wo_evac = {0: ve_p2_end, 1: ve_p2_end}
        first_wo = True
        for es in range(4):
            if es == 0:
                wait("gpsimd", "PE", pe_wo0_ok)
            elif es == 1:
                wait("gpsimd", "PE", pe_p2_end)
            else:
                wait("gpsimd", "PE", pe_wo_done[es - 2])
            lw = emit_i("gpsimd", lambda E, es=es: E.dma_start(
                out=wo_sb[es % 2], in_=wo_d[es]), "LWO0" if es % 2 == 0 else "LWO1", 16)
            wait("tensor", "LWO0" if es % 2 == 0 else "LWO1", lw)
            if first_wo:
                wait("tensor", "AC", ac_avevac_last)
                first_wo = False
            for tt in range(NT):
                ps = wo_ps[tt % 2]
                wait("tensor", "VE", ve_wo_evac[tt % 2])
                last = None
                for cc in range(16):
                    def f(E, es=es, tt=tt, cc=cc, ps=ps):
                        return E.matmul(ps[:, :],
                                        qT[:, cc * 1024 + tt * 128: cc * 1024 + (tt + 1) * 128],
                                        wo_sb[es % 2][:, cc * 512:(cc + 1) * 512],
                                        start=(cc == 0), stop=(cc == 15))
                    if cc < 15:
                        emit_i("tensor", f)
                    else:
                        last = emit_i("tensor", f, "PE")
                pe_wo_done[es] = last
                wait("vector", "PE", last + LAG)
                ysl = y_sb[:, tt * 2048 + es * 512: tt * 2048 + (es + 1) * 512]
                ve_wo_evac[tt % 2] = emit_i("vector", lambda E, ysl=ysl, ps=ps: E.tensor_add(
                    ysl, ysl, ps[:, :]), "VE")

        ve_p3_end = cnt["VE"]
        pe_p3_end = cnt["PE"]

        wg_sb = pool[:, P_WG:P_WG + 8192]
        wu_sb = pool[:, P_WU:P_WU + 8192]
        wd_sb = pool[:, P_WD:P_WD + 8192]
        hact = qT[:, 0:4096]
        gs = [qT[:, 4096 + i * 512: 4096 + (i + 1) * 512] for i in range(2)]

        pe_gate_done = {}
        pe_down_done = {}
        ve_down_evac = {0: ve_p3_end, 1: ve_p3_end}
        ve_final_tt = {}
        ac_silu_prev = {0: 0, 1: 0}
        ve_hact = {}
        GU_banks = [(psA, psB), (psC, psD)]
        D_banks2 = {0: (psE, psF), 1: (psA, psB)}
        first_gate = True
        first_down = True

        for s in range(16):
            if s == 0:
                wait("gpsimd", "PE", pe_p2_end)
            else:
                wait("gpsimd", "PE", pe_gate_done[s - 1])
            lw_gs = emit_i("gpsimd", lambda E, s=s: E.dma_start(out=wg_sb, in_=wg_d[s]), "LWG", 16)
            if s == 0:
                wait("gpsimd", "PE", pe_wo_done[2])
            lw_us = emit_i("gpsimd", lambda E, s=s: E.dma_start(out=wu_sb, in_=wu_d[s]), "LWU", 16)
            if s == 0:
                wait("gpsimd", "PE", pe_p3_end)
            else:
                wait("gpsimd", "PE", pe_down_done[s - 1])
            lw_ds = emit_i("gpsimd", lambda E, s=s: E.dma_start(out=wd_sb, in_=wd_d[s]), "LWD", 16)

            wait("tensor", "LWG", lw_gs)
            wait("tensor", "LWU", lw_us)
            for hq in range(4):
                for th in range(2):
                    pair = hq * 2 + th
                    gb, ub = GU_banks[pair % 2]
                    if first_gate:
                        wait("tensor", "VE", ve_p3_end)
                        first_gate = False
                    if pair < 2 and s > 0:
                        wait("tensor", "VE", ve_down_evac[1] if pair == 0
                             else ve_hact.get((s - 1, 7), 0))
                    wait("tensor", "AC", ac_silu_prev[pair % 2])
                    wait("tensor", "VE", ve_hact.get((s, pair - 2),
                                                     ve_hact.get((s - 1, 6 + pair), 0)))
                    lastg = None
                    for dc in range(DC):
                        def f(E, s=s, hq=hq, th=th, dc=dc, gb=gb):
                            return E.matmul(gb[:, :],
                                            wg_sb[:, dc * 512 + hq * 128: dc * 512 + (hq + 1) * 128],
                                            xnT[:, dc * 1024 + th * 512: dc * 1024 + (th + 1) * 512],
                                            start=(dc == 0), stop=(dc == DC - 1))
                        if dc < DC - 1:
                            emit_i("tensor", f)
                        else:
                            lastg = emit_i("tensor", f, "PE")
                    lastu = None
                    for dc in range(DC):
                        def f(E, s=s, hq=hq, th=th, dc=dc, ub=ub):
                            return E.matmul(ub[:, :],
                                            wu_sb[:, dc * 512 + hq * 128: dc * 512 + (hq + 1) * 128],
                                            xnT[:, dc * 1024 + th * 512: dc * 1024 + (th + 1) * 512],
                                            start=(dc == 0), stop=(dc == DC - 1))
                        if dc < DC - 1:
                            emit_i("tensor", f)
                        else:
                            lastu = emit_i("tensor", f, "PE")
                    pe_gate_done[s] = lastu
                    wait("scalar", "PE", lastg + LAG)
                    ac_silu = emit_i("scalar", lambda E, pair=pair, gb=gb: E.activation(
                        gs[pair % 2], gb[:, :], AF.Silu), "AC")
                    ac_silu_prev[pair % 2] = ac_silu
                    wait("vector", "AC", ac_silu)
                    wait("vector", "PE", lastu + LAG)
                    dsth = hact[:, hq * 1024 + th * 512: hq * 1024 + (th + 1) * 512]
                    ve_hact[(s, pair)] = emit_i("vector", lambda E, d=dsth, pair=pair, ub=ub: E.tensor_mul(
                        d, gs[pair % 2], ub[:, :]), "VE")

            wait("tensor", "LWD", lw_ds)
            wait("tensor", "VE", ve_hact[(s, 7)])
            for tt in range(NT):
                for e2 in range(2):
                    b0, b1 = D_banks2[e2]
                    if first_down:
                        wait("tensor", "VE", ve_p3_end)
                        first_down = False
                    if tt == 0 and e2 == 1:
                        wait("tensor", "VE", ve_hact[(s, 7)])
                        wait("tensor", "AC", ac_silu_prev[0])
                    wait("tensor", "VE", ve_down_evac[e2])
                    last = None
                    for hq in range(4):
                        for ei, bank in ((0, b0), (1, b1)):
                            es = e2 * 2 + ei
                            def f(E, s=s, tt=tt, hq=hq, es=es, bank=bank):
                                return E.matmul(bank[:, :],
                                                hact[:, hq * 1024 + tt * 128: hq * 1024 + (tt + 1) * 128],
                                                wd_sb[:, hq * 2048 + es * 512: hq * 2048 + (es + 1) * 512],
                                                start=(hq == 0), stop=(hq == 3))
                            if hq == 3 and ei == 1:
                                last = emit_i("tensor", f, "PE")
                            else:
                                emit_i("tensor", f)
                    pe_down_done[s] = last
                    wait("vector", "PE", last + LAG)
                    for ei, bank in ((0, b0), (1, b1)):
                        es = e2 * 2 + ei
                        ysl = y_sb[:, tt * 2048 + es * 512: tt * 2048 + (es + 1) * 512]
                        ve_down_evac[e2] = emit_i(
                            "vector",
                            lambda E, ysl=ysl, bank=bank: E.tensor_add(ysl, ysl, bank[:, :]),
                            "VE")
                        if s == 15:
                            ve_final_tt[tt] = ve_down_evac[e2]

        wait("tensor", "VE", cnt["VE"])
        wait("tensor", "AC", cnt["AC"])
        for _ in range(LAG + 1):
            emit_i("tensor", lambda E: E.matmul(
                psA[:, 0:128], id_sb[:, :], id_sb[:, :], start=True, stop=True), "PE")

        for tt in range(NT):
            wait("sync", "VE", ve_final_tt[tt])
            emit_i("sync", lambda E, tt=tt: E.dma_start(
                out=y_d[tt * 128:(tt + 1) * 128, :],
                in_=y_sb[:, tt * 2048:(tt + 1) * 2048]), "SO", 16)
        wait("sync", "SO", cnt["SO"])

        with nc.Block() as block:
            @block.sync
            def _(E):
                for f in prog["sync"]:
                    f(E)

            @block.gpsimd
            def _(E):
                for f in prog["gpsimd"]:
                    f(E)

            @block.tensor
            def _(E):
                for f in prog["tensor"]:
                    f(E)

            @block.scalar
            def _(E):
                for f in prog["scalar"]:
                    f(E)

            @block.vector
            def _(E):
                for f in prog["vector"]:
                    f(E)

    return nc


def _host_layouts(Wq, Wk, Wv, Wo, Wg, Wu, Wd, bo):
    wq_l = np.ascontiguousarray(
        Wq.reshape(16, 128, 16, 128).transpose(2, 1, 0, 3).reshape(16, 128, 2048))
    wk_l = np.ascontiguousarray(Wk.reshape(16, 128, 128).transpose(1, 0, 2).reshape(128, 2048))
    wv_l = np.ascontiguousarray(Wv.reshape(16, 128, 128).transpose(1, 0, 2).reshape(128, 2048))
    wo_l = np.ascontiguousarray(
        Wo.reshape(16, 128, 4, 512).transpose(2, 1, 0, 3).reshape(4, 128, 8192))
    WgT = np.ascontiguousarray(Wg.T)
    WuT = np.ascontiguousarray(Wu.T)
    WdT = np.ascontiguousarray(Wd.T)
    wg_l = np.ascontiguousarray(
        WgT.reshape(16, 128, 16, 512).transpose(2, 1, 0, 3).reshape(16, 128, 8192))
    wu_l = np.ascontiguousarray(
        WuT.reshape(16, 128, 16, 512).transpose(2, 1, 0, 3).reshape(16, 128, 8192))
    wd_l = np.ascontiguousarray(
        WdT.reshape(16, 4, 128, 4, 512).transpose(0, 2, 1, 3, 4).reshape(16, 128, 8192))
    bo_bc = np.ascontiguousarray(np.broadcast_to(bo.reshape(1, -1), (128, 2048)))
    return dict(wq_l=wq_l, wk_l=wk_l, wv_l=wv_l, wo_l=wo_l,
                wg_l=wg_l, wu_l=wu_l, wd_l=wd_l, bo_bc=bo_bc)


def _masks(parity):
    i = np.arange(128)[:, None]
    jc = np.arange(512)[None, :]
    if parity == 0:
        key_true = jc
    else:
        key_true = np.where(jc % 2 == 0, jc + 1, jc - 1)
    mE = np.where(key_true <= 2 * i + parity, 0.0, NEG).astype(np.float32)
    mL = np.where(key_true <= 256 + 2 * i + parity, 0.0, NEG).astype(np.float32)
    return np.stack([mE, mL]).astype(ml_dtypes.bfloat16)


def kernel(L0_attn_x, L0_attn_Wq, L0_attn_Wk, L0_attn_Wv, L0_attn_Wo, L0_attn_bo,
           L0_ffn_w_gate, L0_ffn_w_up, L0_ffn_w_down):
    x = np.ascontiguousarray(np.asarray(L0_attn_x, dtype=np.float32))
    shared = _host_layouts(
        np.asarray(L0_attn_Wq, np.float32), np.asarray(L0_attn_Wk, np.float32),
        np.asarray(L0_attn_Wv, np.float32), np.asarray(L0_attn_Wo, np.float32),
        np.asarray(L0_ffn_w_gate, np.float32), np.asarray(L0_ffn_w_up, np.float32),
        np.asarray(L0_ffn_w_down, np.float32), np.asarray(L0_attn_bo, np.float32))
    ident = np.eye(128, dtype=ml_dtypes.bfloat16)
    masks = [_masks(0), _masks(1)]

    nc = build_nc()
    in_maps = []
    for core in range(8):
        b, p = core // 2, core % 2
        xb = x[b]
        xperm = np.ascontiguousarray(
            xb.reshape(1024, 2, 2048)[:, [p, 1 - p], :].reshape(2048, 2048))
        in_map = dict(shared)
        in_map["x"] = xperm
        in_map["x_own"] = np.ascontiguousarray(xb[p::2, :])
        in_map["masks"] = masks[p]
        in_map["ident"] = ident
        in_maps.append(in_map)

    res = run_bass_kernel_spmd(nc, in_maps, core_ids=list(range(8)))
    out = np.empty((B, T, D), dtype=np.float32)
    for core in range(8):
        b, p = core // 2, core % 2
        out[b, p::2, :] = res.results[core]["y"]
    return out


# version 25
# speedup vs baseline: 1.0521x; 1.0008x over previous
import math
import contextlib
import numpy as np
import ml_dtypes
import concourse.bass as bass
import concourse.mybir as mybir
from concourse.bass_utils import run_bass_kernel_spmd

F32 = mybir.dt.float32
BF16 = mybir.dt.bfloat16
AF = mybir.ActivationFunctionType
AX = mybir.AxisListType
ALU = mybir.AluOpType

B, T, D = 4, 2048, 2048
H, DH = 16, 128
HID = 8192
TO = 1024
NT = 8
GT = 16
DC = 16
QK_SCALE = 1.0 / math.sqrt(DH)
NEG = -1.0e30

ENGINES = ("sync", "gpsimd", "tensor", "scalar", "vector")


def build_nc():
    nc = bass.Bass("TRN2", target_bir_lowering=False)

    x_d = nc.dram_tensor("x", [T, D], F32, kind="ExternalInput").ap()
    x_own_d = nc.dram_tensor("x_own", [TO, D], F32, kind="ExternalInput").ap()
    wq_d = nc.dram_tensor("wq_l", [16, 128, 2048], F32, kind="ExternalInput").ap()
    wk_d = nc.dram_tensor("wk_l", [128, 2048], F32, kind="ExternalInput").ap()
    wv_d = nc.dram_tensor("wv_l", [128, 2048], F32, kind="ExternalInput").ap()
    wo_d = nc.dram_tensor("wo_l", [4, 128, 8192], F32, kind="ExternalInput").ap()
    wg_d = nc.dram_tensor("wg_l", [16, 128, 8192], F32, kind="ExternalInput").ap()
    wu_d = nc.dram_tensor("wu_l", [16, 128, 8192], F32, kind="ExternalInput").ap()
    wd_d = nc.dram_tensor("wd_l", [16, 128, 8192], F32, kind="ExternalInput").ap()
    bo_d = nc.dram_tensor("bo_bc", [128, 2048], F32, kind="ExternalInput").ap()
    mask_d = nc.dram_tensor("masks", [2, 128, 512], BF16, kind="ExternalInput").ap()
    id_d = nc.dram_tensor("ident", [128, 128], BF16, kind="ExternalInput").ap()
    y_d = nc.dram_tensor("y", [TO, D], F32, kind="ExternalOutput").ap()

    ctx = contextlib.ExitStack()
    with ctx:
        y_sb = ctx.enter_context(nc.sbuf_tensor("y_sb", [128, NT * 2048], F32))
        xtS = ctx.enter_context(nc.sbuf_tensor("xtS", [128, 2 * 2048], F32))
        misc = ctx.enter_context(nc.sbuf_tensor("misc", [128, 2048], F32))
        stats = ctx.enter_context(nc.sbuf_tensor("stats", [128, 80], F32))

        xnT = ctx.enter_context(nc.sbuf_tensor("xnT", [128, DC * TO], BF16))
        qT = ctx.enter_context(nc.sbuf_tensor("qT", [128, 16 * TO], BF16))
        pool = ctx.enter_context(nc.sbuf_tensor("pool", [128, 26624], BF16))
        id_sb = ctx.enter_context(nc.sbuf_tensor("id_sb", [128, 128], BF16))
        masks_sb = ctx.enter_context(nc.sbuf_tensor("masks_sb", [128, 1024], BF16))

        P_XNT_GRP, P_XN_NAT, P_WQ, P_WK, P_WV, P_SQ, P_KT, P_VSB = (
            0, 8192, 12288, 16384, 18432, 20480, 22528, 24576)
        P_PSCALED, P_PT = 0, 16384
        P_WO = 8192
        P_WG, P_WU, P_WD = 0, 8192, 16384

        scores4 = ctx.enter_context(nc.psum_tensor("scores4", [128, 2048], F32))
        psA = scores4[:, 0:512]
        psB = scores4[:, 512:1024]
        psC = scores4[:, 1024:1536]
        psD = scores4[:, 1536:2048]
        psE = ctx.enter_context(nc.psum_tensor("psE", [128, 512], F32))
        psF = ctx.enter_context(nc.psum_tensor("psF", [128, 512], F32))
        psT0 = ctx.enter_context(nc.psum_tensor("psT0", [128, 1024], BF16))
        psT1 = ctx.enter_context(nc.psum_tensor("psT1", [128, 1024], BF16))

        _sem_names = ["LCON", "LYI", "LMK", "LXT0", "LXT1", "LXT2", "LWK", "LWQ0", "LWQ1",
                      "LWO0", "LWO1", "LWG", "LWU", "LWD", "SO", "GPX",
                      "PE", "AC", "VE"]
        SEM = {nm: ctx.enter_context(nc.semaphore(nm)) for nm in _sem_names}

        prog = {e: [] for e in ENGINES}
        cnt = {k: 0 for k in SEM}

        def emit(eng, fn):
            prog[eng].append(fn)

        def wait(eng, sem, val):
            if val and val > 0:
                emit(eng, lambda E, s=SEM[sem], v=val: E.wait_ge(s, v))

        def emit_i(eng, fn, sem=None, by=1):
            if sem is None:
                emit(eng, fn)
                return None
            cnt[sem] += by
            v = cnt[sem]
            emit(eng, lambda E, f=fn, s=SEM[sem], b=by: f(E).then_inc(s, b))
            return v

        LAG = 0

        emit_i("sync", lambda E: E.dma_start(out=id_sb[:, :], in_=id_d), "LCON", 16)
        lx_const = cnt["LCON"]
        emit_i("gpsimd", lambda E: E.memset(stats[:, 72:73], 1.0e-5), "GPX")
        gp_eps = cnt["GPX"]
        eps_ap = stats[:, 72:73]

        wk_sb = pool[:, P_WK:P_WK + 2048]
        wv_sb = pool[:, P_WV:P_WV + 2048]
        emit_i("gpsimd", lambda E: E.dma_start(out=wk_sb, in_=wk_d), "LWK", 16)
        emit_i("gpsimd", lambda E: E.dma_start(out=wv_sb, in_=wv_d), "LWK", 16)
        lw_kv = cnt["LWK"]

        for tt in range(NT):
            emit_i("scalar", lambda E, tt=tt: E.dma_start(
                out=y_sb[:, tt * 2048:(tt + 1) * 2048],
                in_=x_own_d[tt * 128:(tt + 1) * 128, :]), "LYI", 16)

        emit_i("gpsimd", lambda E: E.dma_start(out=masks_sb[:, 0:512], in_=mask_d[0]), "LMK", 16)
        emit_i("gpsimd", lambda E: E.dma_start(out=masks_sb[:, 512:1024], in_=mask_d[1]), "LMK", 16)
        lx_masks = cnt["LMK"]
        mask_E = masks_sb[:, 0:512]
        mask_L = masks_sb[:, 512:1024]

        xt3 = [xtS[:, 0:2048], xtS[:, 2048:4096], misc[:, 0:2048]]
        sq = pool[:, P_SQ:P_SQ + 2048]
        xn_nat = [pool[:, P_XN_NAT + i * 2048: P_XN_NAT + (i + 1) * 2048] for i in range(2)]
        xnT_grp = pool[:, P_XNT_GRP:P_XNT_GRP + 8192]
        kT = pool[:, P_KT:P_KT + 2048]
        v_sb = pool[:, P_VSB:P_VSB + 2048]

        def stat(slot, col):
            return stats[:, slot * 8 + col: slot * 8 + col + 1]

        ac_norm_done = 0
        ac_norm_hist = {}
        ac_norm_half = {}
        ac_sq_hist = {}
        pe_trans_tile = {}
        psT_war = {0: 0, 1: 0}
        ve_xnT_own_last = 0
        ve_kT_evac = 0
        ve_v_evac = {0: 0, 1: 0}
        ve_grp_evac_last = 0

        lx_hist = {}

        def ln_dma(gt):
            sl = gt % 3
            wait("sync", "AC", ac_norm_hist.get(gt - 3, 0))
            lx_hist[gt] = emit_i("sync", lambda E, gt=gt, xt=xt3[gt % 3]: E.dma_start(
                out=xt, in_=x_d[gt * 128:(gt + 1) * 128, :]), "LXT%d" % sl, 16)

        def ln_s1(gt):
            sl = gt % 3
            xt = xt3[gt % 3]
            lxs, lxv = "LXT%d" % sl, lx_hist[gt]
            wait("vector", lxs, lxv)
            ve_sum = emit_i("vector", lambda E, sl=sl, xt=xt: E.reduce_sum(
                stat(sl, 0), xt, axis=AX.X), "VE")
            wait("scalar", lxs, lxv)
            ac_sq = emit_i("scalar", lambda E, sl=sl, xt=xt: E.activation(
                sq, xt, AF.Square, accum_out=stat(sl, 1)), "AC")
            ac_sq_hist[gt] = ac_sq
            return dict(gt=gt, sl=sl, xt=xt, ve_sum=ve_sum, ac_sq=ac_sq)

        def ln_s2(st1):
            nonlocal ac_norm_done
            gt, sl, xt = st1["gt"], st1["sl"], st1["xt"]
            wait("vector", "VE", st1["ve_sum"])
            ve_m = emit_i("vector", lambda E, sl=sl: E.tensor_scalar_mul(
                stat(sl, 2), stat(sl, 0), 1.0 / D), "VE")
            wait("vector", "VE", ve_m)
            ve_msq = emit_i("vector", lambda E, sl=sl: E.tensor_mul(
                stat(sl, 3), stat(sl, 2), stat(sl, 2)), "VE")
            wait("vector", "AC", st1["ac_sq"])
            wait("vector", "VE", ve_msq)
            ve_var = emit_i("vector", lambda E, sl=sl: E.scalar_tensor_tensor(
                stat(sl, 4), stat(sl, 1), 1.0 / D, stat(sl, 3),
                ALU.mult, ALU.subtract), "VE")
            wait("scalar", "VE", ve_var)
            if gt == 0:
                wait("scalar", "GPX", gp_eps)
            ac_ln = emit_i("scalar", lambda E, sl=sl: E.activation(
                stat(sl, 5), stat(sl, 4), AF.Ln, bias=eps_ap), "AC")
            wait("scalar", "AC", ac_ln)
            ac_rstd = emit_i("scalar", lambda E, sl=sl: E.activation(
                stat(sl, 6), stat(sl, 5), AF.Exp, scale=-0.5), "AC")
            wait("vector", "AC", ac_rstd)
            ve_negmr = emit_i("vector", lambda E, sl=sl: E.tensor_scalar(
                stat(sl, 7), stat(sl, 2), stat(sl, 6), -1.0,
                ALU.mult, ALU.mult), "VE")
            wait("scalar", "PE", pe_trans_tile.get(gt - 2, 0))
            wait("scalar", "VE", ve_negmr)
            ac_norm_h0 = emit_i("scalar", lambda E, sl=sl, gt=gt, xt=xt: E.activation(
                xn_nat[gt % 2][:, 0:1024], xt[:, 0:1024], AF.Identity,
                bias=stat(sl, 7), scale=stat(sl, 6)), "AC")
            ac_norm_done = emit_i("scalar", lambda E, sl=sl, gt=gt, xt=xt: E.activation(
                xn_nat[gt % 2][:, 1024:2048], xt[:, 1024:2048], AF.Identity,
                bias=stat(sl, 7), scale=stat(sl, 6)), "AC")
            ac_norm_hist[gt] = ac_norm_done
            ac_norm_half[gt] = (ac_norm_h0, ac_norm_done)

        def ln_s3(gt):
            nonlocal ve_grp_evac_last, ve_xnT_own_last
            ti = gt % 4
            if gt == 0:
                wait("tensor", "LCON", lx_const)
            for half in range(2):
                wait("tensor", "AC", ac_norm_half[gt][half])
                pst = psT0 if half == 0 else psT1
                wait("tensor", "VE", psT_war[half])
                last = None
                for q8 in range(8):
                    dc = half * 8 + q8
                    def f(E, pst=pst, q8=q8, dc=dc, gt=gt):
                        return E.transpose(
                            pst[:, q8 * 128:(q8 + 1) * 128],
                            xn_nat[gt % 2][:, dc * 128:(dc + 1) * 128],
                            id_sb[:, :])
                    if q8 < 7:
                        emit_i("tensor", f)
                    else:
                        last = emit_i("tensor", f, "PE")
                pe_trans_tile[gt] = last
                wait("vector", "PE", last + LAG)
                dst_full = xnT_grp.rearrange("p (dc t) -> p dc t", dc=16)[
                    :, half * 8:(half + 1) * 8, ti * 128:(ti + 1) * 128]
                src_full = pst.rearrange("p (dc t) -> p dc t", dc=8)
                emit_i("vector", lambda E, d=dst_full, s=src_full: E.tensor_copy(d, s), "VE")
                dst_own = xnT.rearrange("p (dc t) -> p dc t", dc=16)[
                    :, half * 8:(half + 1) * 8, gt * 64:gt * 64 + 64]
                src_own = pst.rearrange("p (dc t two) -> p dc t two", dc=8, two=2)[:, :, :, 0]
                ev = emit_i("vector", lambda E, d=dst_own, s=src_own: E.tensor_copy(d, s), "VE")
                psT_war[half] = ev
                ve_grp_evac_last = ev
                if gt == 7:
                    ve_xnT_th0[0] = ev
                if gt == GT - 1:
                    ve_xnT_own_last = ev

        def ln_kv(g4):
            nonlocal ve_kT_evac
            wait("tensor", "LWK", lw_kv)
            wait("tensor", "VE", ve_grp_evac_last)
            wait("tensor", "VE", ve_kT_evac)
            last = None
            for dc in range(DC):
                def f(E, dc=dc):
                    return E.matmul(psC[:, :],
                                    wk_sb[:, dc * 128:(dc + 1) * 128],
                                    xnT_grp[:, dc * 512:(dc + 1) * 512],
                                    start=(dc == 0), stop=(dc == DC - 1))
                if dc < DC - 1:
                    emit_i("tensor", f)
                else:
                    last = emit_i("tensor", f, "PE")
            wait("vector", "PE", last + LAG)
            ve_kT_evac = emit_i("vector", lambda E, g4=g4: E.tensor_copy(
                kT[:, g4 * 512:(g4 + 1) * 512], psC[:, :]), "VE")
            for ti in range(4):
                jt = g4 * 4 + ti
                vps = psD if jt % 2 == 0 else psE
                wait("tensor", "VE", ve_v_evac[jt % 2])
                last = None
                for dc in range(DC):
                    def f(E, dc=dc, ti=ti, vps=vps):
                        return E.matmul(vps[:, 0:128],
                                        xnT_grp[:, dc * 512 + ti * 128: dc * 512 + (ti + 1) * 128],
                                        wv_sb[:, dc * 128:(dc + 1) * 128],
                                        start=(dc == 0), stop=(dc == DC - 1))
                    if dc < DC - 1:
                        emit_i("tensor", f)
                    else:
                        last = emit_i("tensor", f, "PE")
                wait("vector", "PE", last + LAG)
                ve_v_evac[jt % 2] = emit_i("vector", lambda E, jt=jt, vps=vps: E.tensor_copy(
                    v_sb[:, jt * 128:(jt + 1) * 128], vps[:, 0:128]), "VE")

        wq_sb = [pool[:, P_WQ + i * 2048: P_WQ + (i + 1) * 2048] for i in range(2)]
        pe_wq_done = {}
        pe_wq_buf_last = {0: 0, 1: 0}
        ve_q_evac = {}
        psF_war = [("VE", 0)]
        ve_xnT_th0 = [0]
        q1_first = [True]

        def q_proj_half(ec, th):
            lws = "LWQ0" if ec % 2 == 0 else "LWQ1"
            wait("gpsimd", "PE", pe_wq_buf_last[ec % 2])
            lw = emit_i("gpsimd", lambda E, ec=ec: E.dma_start(
                out=wq_sb[ec % 2], in_=wq_d[ec]), lws, 16)
            wait("tensor", lws, lw)
            if th == 0:
                wait("tensor", "VE", ve_xnT_th0[0])
            elif q1_first[0]:
                wait("tensor", "VE", ve_xnT_own_last)
                q1_first[0] = False
            weng, wv = psF_war[0]
            wait("tensor", weng, wv)
            last = None
            for dc in range(DC):
                def f(E, ec=ec, th=th, dc=dc):
                    return E.matmul(psF[:, :],
                                    wq_sb[ec % 2][:, dc * 128:(dc + 1) * 128],
                                    xnT[:, dc * 1024 + th * 512: dc * 1024 + (th + 1) * 512],
                                    start=(dc == 0), stop=(dc == DC - 1))
                if dc < DC - 1:
                    emit_i("tensor", f)
                else:
                    last = emit_i("tensor", f, "PE")
            pe_wq_done[ec] = last
            pe_wq_buf_last[ec % 2] = last
            wait("vector", "PE", last + LAG)
            ve = emit_i("vector", lambda E, ec=ec, th=th: E.tensor_scalar_mul(
                qT[:, ec * 1024 + th * 512: ec * 1024 + (th + 1) * 512],
                psF[:, :], QK_SCALE), "VE")
            psF_war[0] = ("VE", ve)
            ve_q_evac[ec] = ve

        st1s = {}
        ln_dma(0)
        ln_dma(1)
        ln_dma(2)
        for gt in range(GT + 1):
            if gt >= 1:
                ln_s2(st1s[gt - 1])
            if 3 <= gt + 2 < GT:
                ln_dma(gt + 2)
            if gt < GT:
                st1s[gt] = ln_s1(gt)
            if gt >= 1:
                ln_s3(gt - 1)
                if (gt - 1) % 4 == 3:
                    ln_kv((gt - 1) // 4)
            if gt >= 9:
                q_proj_half(2 * (gt - 9), 0)
                q_proj_half(2 * (gt - 9) + 1, 0)

        ac_norm_final = ac_norm_done

        wait("scalar", "VE", st1s[14]["ve_sum"])
        emit_i("scalar", lambda E: E.dma_start(out=misc[:, 0:2048], in_=bo_d), "LYI", 16)

        probsT = pool[:, P_PT:P_PT + 4096]

        wait("tensor", "VE", ve_kT_evac)
        wait("tensor", "VE", max(ve_v_evac.values()))
        wait("tensor", "LMK", lx_masks)

        sc_banks = [psA, psB, psC, psD]
        sc_ring = [0, 1, 2, 3]
        sc_war = {0: ("VE", 0), 1: ("VE", 0), 2: ("VE", 0), 3: ("VE", 0)}
        cidx = 0
        pstW = {0: ("VE", psT_war[0]), 1: ("VE", psT_war[1])}
        av_banks = {0: psE, 1: psF}
        av_war0 = [("VE", max(ve_v_evac.values()))]

        pe_trans_slot = {}
        ve_avevac_last = 0
        ac_avevac_last = 0
        ac_exp_done = {}
        ve_partials_read = {}

        def p2_stat_mk(par, u):
            base = 24 + (par * 2 + u % 2) * 12
            def f(col):
                return stats[:, base + col: base + col + 1]
            return f, base

        probs_scaled6 = [pool[:, P_PSCALED + i * 2048: P_PSCALED + (i + 1) * 2048]
                         for i in range(6)]

        def stageA1(u, h, g):
            nonlocal cidx
            E_ext = 512 * (g + 1)
            C = g + 1
            out = dict(u=u, h=h, g=g, E_ext=E_ext, njt=E_ext // 128, deps=[],
                       exps=[])
            wait("tensor", "VE", ve_q_evac[h])
            for par, tk in enumerate((2 * g, 2 * g + 1)):
                slot = (u % 3) * 2 + par
                pslot = probs_scaled6[slot]
                ps, psbase = p2_stat_mk(par, u)
                ac_exp = 0
                slot0 = cidx % 4
                cpes = []
                for c in range(C):
                    bi = sc_ring[cidx % 4]
                    weng, wv = sc_war[bi]
                    wait("tensor", weng, wv)
                    diag = (c == C - 1)
                    def f(E, h=h, tk=tk, c=c, bank=sc_banks[bi], diag=diag):
                        return E.matmul(bank[:, :],
                                        qT[:, h * 1024 + tk * 128: h * 1024 + (tk + 1) * 128],
                                        kT[:, c * 512:(c + 1) * 512],
                                        start=True, stop=not diag)
                    if not diag:
                        cpe = emit_i("tensor", f, "PE")
                    else:
                        emit_i("tensor", f)
                        m = mask_E if (tk % 2 == 0) else mask_L
                        def fm(E, bank=sc_banks[bi], m=m):
                            return E.matmul(bank[:, :], id_sb[:, :], m,
                                            start=False, stop=True)
                        cpe = emit_i("tensor", fm, "PE")
                    cpes.append(cpe)
                    cidx += 1
                    if c == 0:
                        wait("vector", "PE", cpe + LAG)
                        wait("vector", "AC", ac_exp_done.get((par, u % 2), 0))
                        ve_negm = emit_i("vector", lambda E, ps=ps, b0=sc_banks[bi]: E.reduce_max(
                            ps(8), b0[:, 0:128], axis=AX.X, negate=True), "VE")
                runs = [(c, 1) for c in range(C)]
                for ri, (c0, rlen) in enumerate(runs):
                    wait("scalar", "PE", cpes[c0 + rlen - 1] + LAG)
                    if ri == 0:
                        wait("scalar", "VE", ve_negm)
                        wait("scalar", "PE", pe_trans_slot.get(slot, 0))
                        wait("scalar", "VE", ve_partials_read.get((par, u % 2), 0))
                    sb = (slot0 + c0) % 4
                    def fx(E, c0=c0, rlen=rlen, sb=sb, ps=ps, pslot=pslot, ri=ri):
                        return E.activation(
                            pslot[:, c0 * 512:(c0 + rlen) * 512],
                            scores4[:, sb * 512:(sb + rlen) * 512], AF.Exp,
                            bias=ps(8), accum_out=ps(ri))
                    ac_exp = emit_i("scalar", fx, "AC")
                    for k in range(rlen):
                        sc_war[(sb + k)] = ("AC", ac_exp)
                ac_exp_done[(par, u % 2)] = ac_exp
                out["exps"].append((par, slot, pslot, ps, psbase, ac_exp, len(runs)))
            return out

        def stageA2(info):
            u = info["u"]
            E_ext, C = info["E_ext"], info["g"] + 1
            for (par, slot, pslot, ps, psbase, ac_exp, nruns) in info["exps"]:
                wait("vector", "AC", ac_exp)
                if nruns > 1:
                    ve_z = emit_i("vector", lambda E, ps=ps, psbase=psbase, nruns=nruns: E.reduce_sum(
                        ps(9), stats[:, psbase: psbase + nruns], axis=AX.X), "VE")
                    zcol = 9
                    wait("vector", "VE", ve_z)
                    ve_partials_read[(par, u % 2)] = ve_z
                else:
                    zcol = 0
                ve_zrec = emit_i("vector", lambda E, ps=ps, zcol=zcol: E.reciprocal(
                    ps(10), ps(zcol)), "VE")
                if nruns == 1:
                    ve_partials_read[(par, u % 2)] = ve_zrec
                wait("vector", "VE", ve_zrec)
                ve_scale = emit_i("vector", lambda E, E_ext=E_ext, pslot=pslot, ps=ps: E.tensor_scalar_mul(
                    pslot[:, 0:E_ext], pslot[:, 0:E_ext], ps(10)), "VE")
                info["deps"].append((slot, ve_scale))

        def stageB(info):
            nonlocal ve_avevac_last, ac_avevac_last
            u, h, g, njt = info["u"], info["h"], info["g"], info["njt"]
            nbatches = (njt + 7) // 8
            ve_evac_b = []
            for b in range(nbatches):
                n_in = min(8, njt - b * 8)
                evs = 0
                for par in range(2):
                    slot, ve_scale = info["deps"][par]
                    if b == 0:
                        wait("tensor", "VE", ve_scale)
                    bank = (u + par) % 2
                    pst = psT0 if bank == 0 else psT1
                    weng, wv = pstW[bank]
                    wait("tensor", weng, wv)
                    last = None
                    for q8 in range(n_in):
                        jt = b * 8 + q8
                        def f(E, pst=pst, q8=q8, jt=jt, slot=slot):
                            return E.transpose(
                                pst[:, q8 * 128:(q8 + 1) * 128],
                                probs_scaled6[slot][:, jt * 128:(jt + 1) * 128],
                                id_sb[:, :])
                        if q8 < n_in - 1:
                            emit_i("tensor", f)
                        else:
                            last = emit_i("tensor", f, "PE")
                    pe_trans_slot[slot] = last
                    wait("vector", "PE", last + LAG)
                    dst = probsT[:, par * 2048 + b * 1024:
                                 par * 2048 + b * 1024 + n_in * 128]
                    srcp = pst[:, 0:n_in * 128]
                    ev = emit_i("vector", lambda E, d=dst, s=srcp: E.tensor_copy(d, s), "VE")
                    evs = ev
                    pstW[bank] = ("VE", ev)
                ve_evac_b.append(evs)
            bank_i = u % 2
            avp = av_banks[bank_i]
            weng, wv = av_war0[0] if bank_i == 0 else psF_war[0]
            wait("tensor", weng, wv)
            last = None
            for jt in range(njt):
                if jt % 8 == 0:
                    wait("tensor", "VE", ve_evac_b[jt // 8])
                def f(E, jt=jt, avp=avp, njt=njt):
                    rhs = probsT.rearrange("p (par c) -> p par c", par=2)[
                        :, :, jt * 128:(jt + 1) * 128]
                    return E.matmul(avp[:, 0:256],
                                    v_sb[:, jt * 128:(jt + 1) * 128],
                                    rhs,
                                    start=(jt == 0), stop=(jt == njt - 1))
                if jt < njt - 1:
                    emit_i("tensor", f)
                else:
                    last = emit_i("tensor", f, "PE")
            wait("scalar", "PE", last + LAG)
            ac_avevac_last = emit_i("scalar", lambda E, h=h, g=g, avp=avp: E.copy(
                qT[:, h * 1024 + g * 256: h * 1024 + (g + 1) * 256], avp[:, 0:256]), "AC")
            if bank_i == 0:
                av_war0[0] = ("AC", ac_avevac_last)
            else:
                psF_war[0] = ("AC", ac_avevac_last)

        PIPE_LAG = 3
        q_proj_half(0, 1)
        q_proj_half(1, 1)
        infos = []
        units = [(h, g) for h in range(H) for g in range(4)]
        for u, (h, g) in enumerate(units):
            if h + 2 < H and g == 0:
                q_proj_half(h + 2, 1)
            if u >= PIPE_LAG:
                stageB(infos[u - PIPE_LAG])
            if u >= 1:
                stageA2(infos[u - 1])
            infos.append(stageA1(u, h, g))
        stageA2(infos[-1])
        for uu in range(len(units) - PIPE_LAG, len(units)):
            stageB(infos[uu])

        pe_p2_end = cnt["PE"]
        ve_p2_end = cnt["VE"]

        wait("vector", "LYI", cnt["LYI"])
        for tt in range(NT):
            ysl = y_sb[:, tt * 2048:(tt + 1) * 2048]
            emit_i("vector", lambda E, ysl=ysl: E.tensor_add(
                ysl, ysl, misc[:, 0:2048]), "VE")
        pe_wo0_ok = max(pe_trans_slot.get(4, 0), pe_trans_slot.get(5, 0),
                        max(pe_wq_done.values()))

        wo_sb = [pool[:, P_WO + i * 8192: P_WO + (i + 1) * 8192] for i in range(2)]
        wo_ps = [psA, psB]
        pe_wo_done = {}
        ve_wo_evac = {0: ve_p2_end, 1: ve_p2_end}
        first_wo = True
        for es in range(4):
            if es == 0:
                wait("gpsimd", "PE", pe_wo0_ok)
            elif es == 1:
                wait("gpsimd", "PE", pe_p2_end)
            else:
                wait("gpsimd", "PE", pe_wo_done[es - 2])
            lw = emit_i("gpsimd", lambda E, es=es: E.dma_start(
                out=wo_sb[es % 2], in_=wo_d[es]), "LWO0" if es % 2 == 0 else "LWO1", 16)
            wait("tensor", "LWO0" if es % 2 == 0 else "LWO1", lw)
            if first_wo:
                wait("tensor", "AC", ac_avevac_last)
                first_wo = False
            for tt in range(NT):
                ps = wo_ps[tt % 2]
                wait("tensor", "VE", ve_wo_evac[tt % 2])
                last = None
                for cc in range(16):
                    def f(E, es=es, tt=tt, cc=cc, ps=ps):
                        return E.matmul(ps[:, :],
                                        qT[:, cc * 1024 + tt * 128: cc * 1024 + (tt + 1) * 128],
                                        wo_sb[es % 2][:, cc * 512:(cc + 1) * 512],
                                        start=(cc == 0), stop=(cc == 15))
                    if cc < 15:
                        emit_i("tensor", f)
                    else:
                        last = emit_i("tensor", f, "PE")
                pe_wo_done[es] = last
                wait("vector", "PE", last + LAG)
                ysl = y_sb[:, tt * 2048 + es * 512: tt * 2048 + (es + 1) * 512]
                ve_wo_evac[tt % 2] = emit_i("vector", lambda E, ysl=ysl, ps=ps: E.tensor_add(
                    ysl, ysl, ps[:, :]), "VE")

        ve_p3_end = cnt["VE"]
        pe_p3_end = cnt["PE"]

        wg_sb = pool[:, P_WG:P_WG + 8192]
        wu_sb = pool[:, P_WU:P_WU + 8192]
        wd_sb = pool[:, P_WD:P_WD + 8192]
        hact = qT[:, 0:4096]
        gs = [qT[:, 4096 + i * 512: 4096 + (i + 1) * 512] for i in range(2)]

        pe_gate_done = {}
        pe_down_done = {}
        ve_down_evac = {0: ve_p3_end, 1: ve_p3_end}
        ve_final_tt = {}
        ac_silu_prev = {0: 0, 1: 0}
        ve_hact = {}
        GU_banks = [(psA, psB), (psC, psD)]
        D_banks2 = {0: (psE, psF), 1: (psA, psB)}
        first_gate = True
        first_down = True

        for s in range(16):
            if s == 0:
                wait("gpsimd", "PE", pe_p2_end)
            else:
                wait("gpsimd", "PE", pe_gate_done[s - 1])
            lw_gs = emit_i("gpsimd", lambda E, s=s: E.dma_start(out=wg_sb, in_=wg_d[s]), "LWG", 16)
            if s == 0:
                wait("gpsimd", "PE", pe_wo_done[2])
            lw_us = emit_i("gpsimd", lambda E, s=s: E.dma_start(out=wu_sb, in_=wu_d[s]), "LWU", 16)
            if s == 0:
                wait("gpsimd", "PE", pe_p3_end)
            else:
                wait("gpsimd", "PE", pe_down_done[s - 1])
            lw_ds = emit_i("gpsimd", lambda E, s=s: E.dma_start(out=wd_sb, in_=wd_d[s]), "LWD", 16)

            wait("tensor", "LWG", lw_gs)
            wait("tensor", "LWU", lw_us)
            for hq in range(4):
                for th in range(2):
                    pair = hq * 2 + th
                    gb, ub = GU_banks[pair % 2]
                    if first_gate:
                        wait("tensor", "VE", ve_p3_end)
                        first_gate = False
                    if pair < 2 and s > 0:
                        wait("tensor", "VE", ve_down_evac[1] if pair == 0
                             else ve_hact.get((s - 1, 7), 0))
                    wait("tensor", "AC", ac_silu_prev[pair % 2])
                    wait("tensor", "VE", ve_hact.get((s, pair - 2),
                                                     ve_hact.get((s - 1, 6 + pair), 0)))
                    lastg = None
                    for dc in range(DC):
                        def f(E, s=s, hq=hq, th=th, dc=dc, gb=gb):
                            return E.matmul(gb[:, :],
                                            wg_sb[:, dc * 512 + hq * 128: dc * 512 + (hq + 1) * 128],
                                            xnT[:, dc * 1024 + th * 512: dc * 1024 + (th + 1) * 512],
                                            start=(dc == 0), stop=(dc == DC - 1))
                        if dc < DC - 1:
                            emit_i("tensor", f)
                        else:
                            lastg = emit_i("tensor", f, "PE")
                    lastu = None
                    for dc in range(DC):
                        def f(E, s=s, hq=hq, th=th, dc=dc, ub=ub):
                            return E.matmul(ub[:, :],
                                            wu_sb[:, dc * 512 + hq * 128: dc * 512 + (hq + 1) * 128],
                                            xnT[:, dc * 1024 + th * 512: dc * 1024 + (th + 1) * 512],
                                            start=(dc == 0), stop=(dc == DC - 1))
                        if dc < DC - 1:
                            emit_i("tensor", f)
                        else:
                            lastu = emit_i("tensor", f, "PE")
                    pe_gate_done[s] = lastu
                    wait("scalar", "PE", lastg + LAG)
                    ac_silu = emit_i("scalar", lambda E, pair=pair, gb=gb: E.activation(
                        gs[pair % 2], gb[:, :], AF.Silu), "AC")
                    ac_silu_prev[pair % 2] = ac_silu
                    wait("vector", "AC", ac_silu)
                    wait("vector", "PE", lastu + LAG)
                    dsth = hact[:, hq * 1024 + th * 512: hq * 1024 + (th + 1) * 512]
                    ve_hact[(s, pair)] = emit_i("vector", lambda E, d=dsth, pair=pair, ub=ub: E.tensor_mul(
                        d, gs[pair % 2], ub[:, :]), "VE")

            wait("tensor", "LWD", lw_ds)
            wait("tensor", "VE", ve_hact[(s, 7)])
            for tt in range(NT):
                for e2 in range(2):
                    b0, b1 = D_banks2[e2]
                    if first_down:
                        wait("tensor", "VE", ve_p3_end)
                        first_down = False
                    if tt == 0 and e2 == 1:
                        wait("tensor", "VE", ve_hact[(s, 7)])
                        wait("tensor", "AC", ac_silu_prev[0])
                    wait("tensor", "VE", ve_down_evac[e2])
                    last = None
                    for hq in range(4):
                        for ei, bank in ((0, b0), (1, b1)):
                            es = e2 * 2 + ei
                            def f(E, s=s, tt=tt, hq=hq, es=es, bank=bank):
                                return E.matmul(bank[:, :],
                                                hact[:, hq * 1024 + tt * 128: hq * 1024 + (tt + 1) * 128],
                                                wd_sb[:, hq * 2048 + es * 512: hq * 2048 + (es + 1) * 512],
                                                start=(hq == 0), stop=(hq == 3))
                            if hq == 3 and ei == 1:
                                last = emit_i("tensor", f, "PE")
                            else:
                                emit_i("tensor", f)
                    pe_down_done[s] = last
                    wait("vector", "PE", last + LAG)
                    for ei, bank in ((0, b0), (1, b1)):
                        es = e2 * 2 + ei
                        ysl = y_sb[:, tt * 2048 + es * 512: tt * 2048 + (es + 1) * 512]
                        ve_down_evac[e2] = emit_i(
                            "vector",
                            lambda E, ysl=ysl, bank=bank: E.tensor_add(ysl, ysl, bank[:, :]),
                            "VE")
                        if s == 15:
                            ve_final_tt[tt] = ve_down_evac[e2]

        wait("tensor", "VE", cnt["VE"])
        wait("tensor", "AC", cnt["AC"])
        for _ in range(LAG + 1):
            emit_i("tensor", lambda E: E.matmul(
                psA[:, 0:128], id_sb[:, :], id_sb[:, :], start=True, stop=True), "PE")

        for tt in range(NT):
            wait("sync", "VE", ve_final_tt[tt])
            emit_i("sync", lambda E, tt=tt: E.dma_start(
                out=y_d[tt * 128:(tt + 1) * 128, :],
                in_=y_sb[:, tt * 2048:(tt + 1) * 2048]), "SO", 16)
        wait("sync", "SO", cnt["SO"])

        with nc.Block() as block:
            @block.sync
            def _(E):
                for f in prog["sync"]:
                    f(E)

            @block.gpsimd
            def _(E):
                for f in prog["gpsimd"]:
                    f(E)

            @block.tensor
            def _(E):
                for f in prog["tensor"]:
                    f(E)

            @block.scalar
            def _(E):
                for f in prog["scalar"]:
                    f(E)

            @block.vector
            def _(E):
                for f in prog["vector"]:
                    f(E)

    return nc


def _host_layouts(Wq, Wk, Wv, Wo, Wg, Wu, Wd, bo):
    wq_l = np.ascontiguousarray(
        Wq.reshape(16, 128, 16, 128).transpose(2, 1, 0, 3).reshape(16, 128, 2048))
    wk_l = np.ascontiguousarray(Wk.reshape(16, 128, 128).transpose(1, 0, 2).reshape(128, 2048))
    wv_l = np.ascontiguousarray(Wv.reshape(16, 128, 128).transpose(1, 0, 2).reshape(128, 2048))
    wo_l = np.ascontiguousarray(
        Wo.reshape(16, 128, 4, 512).transpose(2, 1, 0, 3).reshape(4, 128, 8192))
    WgT = np.ascontiguousarray(Wg.T)
    WuT = np.ascontiguousarray(Wu.T)
    WdT = np.ascontiguousarray(Wd.T)
    wg_l = np.ascontiguousarray(
        WgT.reshape(16, 128, 16, 512).transpose(2, 1, 0, 3).reshape(16, 128, 8192))
    wu_l = np.ascontiguousarray(
        WuT.reshape(16, 128, 16, 512).transpose(2, 1, 0, 3).reshape(16, 128, 8192))
    wd_l = np.ascontiguousarray(
        WdT.reshape(16, 4, 128, 4, 512).transpose(0, 2, 1, 3, 4).reshape(16, 128, 8192))
    bo_bc = np.ascontiguousarray(np.broadcast_to(bo.reshape(1, -1), (128, 2048)))
    return dict(wq_l=wq_l, wk_l=wk_l, wv_l=wv_l, wo_l=wo_l,
                wg_l=wg_l, wu_l=wu_l, wd_l=wd_l, bo_bc=bo_bc)


def _masks(parity):
    i = np.arange(128)[:, None]
    jc = np.arange(512)[None, :]
    if parity == 0:
        key_true = jc
    else:
        key_true = np.where(jc % 2 == 0, jc + 1, jc - 1)
    mE = np.where(key_true <= 2 * i + parity, 0.0, NEG).astype(np.float32)
    mL = np.where(key_true <= 256 + 2 * i + parity, 0.0, NEG).astype(np.float32)
    return np.stack([mE, mL]).astype(ml_dtypes.bfloat16)


def kernel(L0_attn_x, L0_attn_Wq, L0_attn_Wk, L0_attn_Wv, L0_attn_Wo, L0_attn_bo,
           L0_ffn_w_gate, L0_ffn_w_up, L0_ffn_w_down):
    x = np.ascontiguousarray(np.asarray(L0_attn_x, dtype=np.float32))
    shared = _host_layouts(
        np.asarray(L0_attn_Wq, np.float32), np.asarray(L0_attn_Wk, np.float32),
        np.asarray(L0_attn_Wv, np.float32), np.asarray(L0_attn_Wo, np.float32),
        np.asarray(L0_ffn_w_gate, np.float32), np.asarray(L0_ffn_w_up, np.float32),
        np.asarray(L0_ffn_w_down, np.float32), np.asarray(L0_attn_bo, np.float32))
    ident = np.eye(128, dtype=ml_dtypes.bfloat16)
    masks = [_masks(0), _masks(1)]

    nc = build_nc()
    in_maps = []
    for core in range(8):
        b, p = core // 2, core % 2
        xb = x[b]
        xperm = np.ascontiguousarray(
            xb.reshape(1024, 2, 2048)[:, [p, 1 - p], :].reshape(2048, 2048))
        in_map = dict(shared)
        in_map["x"] = xperm
        in_map["x_own"] = np.ascontiguousarray(xb[p::2, :])
        in_map["masks"] = masks[p]
        in_map["ident"] = ident
        in_maps.append(in_map)

    res = run_bass_kernel_spmd(nc, in_maps, core_ids=list(range(8)))
    out = np.empty((B, T, D), dtype=np.float32)
    for core in range(8):
        b, p = core // 2, core % 2
        out[b, p::2, :] = res.results[core]["y"]
    return out


# version 26 (final)
# speedup vs baseline: 1.0541x; 1.0019x over previous
import math
import contextlib
import numpy as np
import ml_dtypes
import concourse.bass as bass
import concourse.mybir as mybir
from concourse.bass_utils import run_bass_kernel_spmd

F32 = mybir.dt.float32
BF16 = mybir.dt.bfloat16
AF = mybir.ActivationFunctionType
AX = mybir.AxisListType
ALU = mybir.AluOpType

B, T, D = 4, 2048, 2048
H, DH = 16, 128
HID = 8192
TO = 1024
NT = 8
GT = 16
DC = 16
QK_SCALE = 1.0 / math.sqrt(DH)
NEG = -1.0e30

ENGINES = ("sync", "gpsimd", "tensor", "scalar", "vector")


def build_nc():
    nc = bass.Bass("TRN2", target_bir_lowering=False)

    x_d = nc.dram_tensor("x", [T, D], F32, kind="ExternalInput").ap()
    x_own_d = nc.dram_tensor("x_own", [TO, D], F32, kind="ExternalInput").ap()
    wq_d = nc.dram_tensor("wq_l", [16, 128, 2048], F32, kind="ExternalInput").ap()
    wk_d = nc.dram_tensor("wk_l", [128, 2048], F32, kind="ExternalInput").ap()
    wv_d = nc.dram_tensor("wv_l", [128, 2048], F32, kind="ExternalInput").ap()
    wo_d = nc.dram_tensor("wo_l", [4, 128, 8192], F32, kind="ExternalInput").ap()
    wg_d = nc.dram_tensor("wg_l", [16, 128, 8192], F32, kind="ExternalInput").ap()
    wu_d = nc.dram_tensor("wu_l", [16, 128, 8192], F32, kind="ExternalInput").ap()
    wd_d = nc.dram_tensor("wd_l", [16, 128, 8192], F32, kind="ExternalInput").ap()
    bo_d = nc.dram_tensor("bo_bc", [128, 2048], F32, kind="ExternalInput").ap()
    mask_d = nc.dram_tensor("masks", [2, 128, 512], BF16, kind="ExternalInput").ap()
    id_d = nc.dram_tensor("ident", [128, 128], BF16, kind="ExternalInput").ap()
    y_d = nc.dram_tensor("y", [TO, D], F32, kind="ExternalOutput").ap()

    ctx = contextlib.ExitStack()
    with ctx:
        y_sb = ctx.enter_context(nc.sbuf_tensor("y_sb", [128, NT * 2048], F32))
        xtS = ctx.enter_context(nc.sbuf_tensor("xtS", [128, 2 * 2048], F32))
        misc = ctx.enter_context(nc.sbuf_tensor("misc", [128, 2048], F32))
        stats = ctx.enter_context(nc.sbuf_tensor("stats", [128, 80], F32))

        xnT = ctx.enter_context(nc.sbuf_tensor("xnT", [128, DC * TO], BF16))
        qT = ctx.enter_context(nc.sbuf_tensor("qT", [128, 16 * TO], BF16))
        pool = ctx.enter_context(nc.sbuf_tensor("pool", [128, 26624], BF16))
        id_sb = ctx.enter_context(nc.sbuf_tensor("id_sb", [128, 128], BF16))
        masks_sb = ctx.enter_context(nc.sbuf_tensor("masks_sb", [128, 1024], BF16))

        P_XNT_GRP, P_XN_NAT, P_WQ, P_WK, P_WV, P_SQ, P_KT, P_VSB = (
            0, 8192, 12288, 16384, 18432, 20480, 22528, 24576)
        P_PSCALED, P_PT = 0, 16384
        P_WO = 8192
        P_WG, P_WU, P_WD = 0, 8192, 16384

        scores4 = ctx.enter_context(nc.psum_tensor("scores4", [128, 2048], F32))
        psA = scores4[:, 0:512]
        psB = scores4[:, 512:1024]
        psC = scores4[:, 1024:1536]
        psD = scores4[:, 1536:2048]
        psE = ctx.enter_context(nc.psum_tensor("psE", [128, 512], F32))
        psF = ctx.enter_context(nc.psum_tensor("psF", [128, 512], F32))
        psT0 = ctx.enter_context(nc.psum_tensor("psT0", [128, 1024], BF16))
        psT1 = ctx.enter_context(nc.psum_tensor("psT1", [128, 1024], BF16))

        _sem_names = ["LCON", "LYI", "LMK", "LXT0", "LXT1", "LXT2", "LWK", "LWQ0", "LWQ1",
                      "LWO0", "LWO1", "LWG", "LWU", "LWD", "SO", "GPX",
                      "PE", "AC", "VE"]
        SEM = {nm: ctx.enter_context(nc.semaphore(nm)) for nm in _sem_names}

        prog = {e: [] for e in ENGINES}
        cnt = {k: 0 for k in SEM}

        def emit(eng, fn):
            prog[eng].append(fn)

        def wait(eng, sem, val):
            if val and val > 0:
                emit(eng, lambda E, s=SEM[sem], v=val: E.wait_ge(s, v))

        def emit_i(eng, fn, sem=None, by=1):
            if sem is None:
                emit(eng, fn)
                return None
            cnt[sem] += by
            v = cnt[sem]
            emit(eng, lambda E, f=fn, s=SEM[sem], b=by: f(E).then_inc(s, b))
            return v

        LAG = 0

        emit_i("sync", lambda E: E.dma_start(out=id_sb[:, :], in_=id_d), "LCON", 16)
        lx_const = cnt["LCON"]
        emit_i("gpsimd", lambda E: E.memset(stats[:, 72:73], 1.0e-5), "GPX")
        gp_eps = cnt["GPX"]
        eps_ap = stats[:, 72:73]

        wk_sb = pool[:, P_WK:P_WK + 2048]
        wv_sb = pool[:, P_WV:P_WV + 2048]
        emit_i("gpsimd", lambda E: E.dma_start(out=wk_sb, in_=wk_d), "LWK", 16)
        emit_i("gpsimd", lambda E: E.dma_start(out=wv_sb, in_=wv_d), "LWK", 16)
        lw_kv = cnt["LWK"]

        for tt in range(NT):
            emit_i("scalar", lambda E, tt=tt: E.dma_start(
                out=y_sb[:, tt * 2048:(tt + 1) * 2048],
                in_=x_own_d[tt * 128:(tt + 1) * 128, :]), "LYI", 16)

        emit_i("gpsimd", lambda E: E.dma_start(out=masks_sb[:, 0:512], in_=mask_d[0]), "LMK", 16)
        emit_i("gpsimd", lambda E: E.dma_start(out=masks_sb[:, 512:1024], in_=mask_d[1]), "LMK", 16)
        lx_masks = cnt["LMK"]
        mask_E = masks_sb[:, 0:512]
        mask_L = masks_sb[:, 512:1024]

        xt3 = [xtS[:, 0:2048], xtS[:, 2048:4096], misc[:, 0:2048]]
        sq = pool[:, P_SQ:P_SQ + 2048]
        xn_nat = [pool[:, P_XN_NAT + i * 2048: P_XN_NAT + (i + 1) * 2048] for i in range(2)]
        xnT_grp = pool[:, P_XNT_GRP:P_XNT_GRP + 8192]
        kT = pool[:, P_KT:P_KT + 2048]
        v_sb = pool[:, P_VSB:P_VSB + 2048]

        def stat(slot, col):
            return stats[:, slot * 8 + col: slot * 8 + col + 1]

        ac_norm_done = 0
        ac_norm_hist = {}
        ac_norm_half = {}
        ac_sq_hist = {}
        pe_trans_tile = {}
        psT_war = {0: 0, 1: 0}
        ve_xnT_own_last = 0
        ve_kT_evac = 0
        ve_v_evac = {0: 0, 1: 0}
        ve_grp_evac_last = 0

        lx_hist = {}

        def ln_dma(gt):
            sl = gt % 3
            wait("sync", "AC", ac_norm_hist.get(gt - 3, 0))
            lx_hist[gt] = emit_i("sync", lambda E, gt=gt, xt=xt3[gt % 3]: E.dma_start(
                out=xt, in_=x_d[gt * 128:(gt + 1) * 128, :]), "LXT%d" % sl, 16)

        def ln_s1(gt):
            sl = gt % 3
            xt = xt3[gt % 3]
            lxs, lxv = "LXT%d" % sl, lx_hist[gt]
            wait("vector", lxs, lxv)
            ve_sum = emit_i("vector", lambda E, sl=sl, xt=xt: E.reduce_sum(
                stat(sl, 0), xt, axis=AX.X), "VE")
            wait("scalar", lxs, lxv)
            ac_sq = emit_i("scalar", lambda E, sl=sl, xt=xt: E.activation(
                sq, xt, AF.Square, accum_out=stat(sl, 1)), "AC")
            ac_sq_hist[gt] = ac_sq
            return dict(gt=gt, sl=sl, xt=xt, ve_sum=ve_sum, ac_sq=ac_sq)

        def ln_s2(st1):
            nonlocal ac_norm_done
            gt, sl, xt = st1["gt"], st1["sl"], st1["xt"]
            wait("vector", "VE", st1["ve_sum"])
            ve_m = emit_i("vector", lambda E, sl=sl: E.tensor_scalar_mul(
                stat(sl, 2), stat(sl, 0), 1.0 / D), "VE")
            wait("vector", "VE", ve_m)
            ve_msq = emit_i("vector", lambda E, sl=sl: E.tensor_mul(
                stat(sl, 3), stat(sl, 2), stat(sl, 2)), "VE")
            wait("vector", "AC", st1["ac_sq"])
            wait("vector", "VE", ve_msq)
            ve_var = emit_i("vector", lambda E, sl=sl: E.scalar_tensor_tensor(
                stat(sl, 4), stat(sl, 1), 1.0 / D, stat(sl, 3),
                ALU.mult, ALU.subtract), "VE")
            wait("scalar", "VE", ve_var)
            if gt == 0:
                wait("scalar", "GPX", gp_eps)
            ac_ln = emit_i("scalar", lambda E, sl=sl: E.activation(
                stat(sl, 5), stat(sl, 4), AF.Ln, bias=eps_ap), "AC")
            wait("scalar", "AC", ac_ln)
            ac_rstd = emit_i("scalar", lambda E, sl=sl: E.activation(
                stat(sl, 6), stat(sl, 5), AF.Exp, scale=-0.5), "AC")
            wait("vector", "AC", ac_rstd)
            ve_negmr = emit_i("vector", lambda E, sl=sl: E.tensor_scalar(
                stat(sl, 7), stat(sl, 2), stat(sl, 6), -1.0,
                ALU.mult, ALU.mult), "VE")
            wait("scalar", "PE", pe_trans_tile.get(gt - 2, 0))
            wait("scalar", "VE", ve_negmr)
            ac_norm_h0 = emit_i("scalar", lambda E, sl=sl, gt=gt, xt=xt: E.activation(
                xn_nat[gt % 2][:, 0:1024], xt[:, 0:1024], AF.Identity,
                bias=stat(sl, 7), scale=stat(sl, 6)), "AC")
            ac_norm_done = emit_i("scalar", lambda E, sl=sl, gt=gt, xt=xt: E.activation(
                xn_nat[gt % 2][:, 1024:2048], xt[:, 1024:2048], AF.Identity,
                bias=stat(sl, 7), scale=stat(sl, 6)), "AC")
            ac_norm_hist[gt] = ac_norm_done
            ac_norm_half[gt] = (ac_norm_h0, ac_norm_done)

        def ln_s3(gt):
            nonlocal ve_grp_evac_last, ve_xnT_own_last
            ti = gt % 4
            if gt == 0:
                wait("tensor", "LCON", lx_const)
            for half in range(2):
                wait("tensor", "AC", ac_norm_half[gt][half])
                pst = psT0 if half == 0 else psT1
                wait("tensor", "VE", psT_war[half])
                last = None
                for q8 in range(8):
                    dc = half * 8 + q8
                    def f(E, pst=pst, q8=q8, dc=dc, gt=gt):
                        return E.transpose(
                            pst[:, q8 * 128:(q8 + 1) * 128],
                            xn_nat[gt % 2][:, dc * 128:(dc + 1) * 128],
                            id_sb[:, :])
                    if q8 < 7:
                        emit_i("tensor", f)
                    else:
                        last = emit_i("tensor", f, "PE")
                pe_trans_tile[gt] = last
                wait("vector", "PE", last + LAG)
                dst_full = xnT_grp.rearrange("p (dc t) -> p dc t", dc=16)[
                    :, half * 8:(half + 1) * 8, ti * 128:(ti + 1) * 128]
                src_full = pst.rearrange("p (dc t) -> p dc t", dc=8)
                emit_i("vector", lambda E, d=dst_full, s=src_full: E.tensor_copy(d, s), "VE")
                dst_own = xnT.rearrange("p (dc t) -> p dc t", dc=16)[
                    :, half * 8:(half + 1) * 8, gt * 64:gt * 64 + 64]
                src_own = pst.rearrange("p (dc t two) -> p dc t two", dc=8, two=2)[:, :, :, 0]
                ev = emit_i("vector", lambda E, d=dst_own, s=src_own: E.tensor_copy(d, s), "VE")
                psT_war[half] = ev
                ve_grp_evac_last = ev
                if gt == 7:
                    ve_xnT_th0[0] = ev
                if gt == GT - 1:
                    ve_xnT_own_last = ev

        def ln_kv(g4):
            nonlocal ve_kT_evac
            wait("tensor", "LWK", lw_kv)
            wait("tensor", "VE", ve_grp_evac_last)
            wait("tensor", "VE", ve_kT_evac)
            last = None
            for dc in range(DC):
                def f(E, dc=dc):
                    return E.matmul(psC[:, :],
                                    wk_sb[:, dc * 128:(dc + 1) * 128],
                                    xnT_grp[:, dc * 512:(dc + 1) * 512],
                                    start=(dc == 0), stop=(dc == DC - 1))
                if dc < DC - 1:
                    emit_i("tensor", f)
                else:
                    last = emit_i("tensor", f, "PE")
            wait("vector", "PE", last + LAG)
            ve_kT_evac = emit_i("vector", lambda E, g4=g4: E.tensor_copy(
                kT[:, g4 * 512:(g4 + 1) * 512], psC[:, :]), "VE")
            for ti in range(4):
                jt = g4 * 4 + ti
                vps = psD if jt % 2 == 0 else psE
                wait("tensor", "VE", ve_v_evac[jt % 2])
                last = None
                for dc in range(DC):
                    def f(E, dc=dc, ti=ti, vps=vps):
                        return E.matmul(vps[:, 0:128],
                                        xnT_grp[:, dc * 512 + ti * 128: dc * 512 + (ti + 1) * 128],
                                        wv_sb[:, dc * 128:(dc + 1) * 128],
                                        start=(dc == 0), stop=(dc == DC - 1))
                    if dc < DC - 1:
                        emit_i("tensor", f)
                    else:
                        last = emit_i("tensor", f, "PE")
                wait("vector", "PE", last + LAG)
                ve_v_evac[jt % 2] = emit_i("vector", lambda E, jt=jt, vps=vps: E.tensor_copy(
                    v_sb[:, jt * 128:(jt + 1) * 128], vps[:, 0:128]), "VE")

        wq_sb = [pool[:, P_WQ + i * 2048: P_WQ + (i + 1) * 2048] for i in range(2)]
        pe_wq_done = {}
        pe_wq_buf_last = {0: 0, 1: 0}
        ve_q_evac = {}
        psF_war = [("VE", 0)]
        ve_xnT_th0 = [0]
        q1_first = [True]

        def q_proj_half(ec, th):
            lws = "LWQ0" if ec % 2 == 0 else "LWQ1"
            wait("gpsimd", "PE", pe_wq_buf_last[ec % 2])
            lw = emit_i("gpsimd", lambda E, ec=ec: E.dma_start(
                out=wq_sb[ec % 2], in_=wq_d[ec]), lws, 16)
            wait("tensor", lws, lw)
            if th == 0:
                wait("tensor", "VE", ve_xnT_th0[0])
            elif q1_first[0]:
                wait("tensor", "VE", ve_xnT_own_last)
                q1_first[0] = False
            weng, wv = psF_war[0]
            wait("tensor", weng, wv)
            last = None
            for dc in range(DC):
                def f(E, ec=ec, th=th, dc=dc):
                    return E.matmul(psF[:, :],
                                    wq_sb[ec % 2][:, dc * 128:(dc + 1) * 128],
                                    xnT[:, dc * 1024 + th * 512: dc * 1024 + (th + 1) * 512],
                                    start=(dc == 0), stop=(dc == DC - 1))
                if dc < DC - 1:
                    emit_i("tensor", f)
                else:
                    last = emit_i("tensor", f, "PE")
            pe_wq_done[ec] = last
            pe_wq_buf_last[ec % 2] = last
            wait("vector", "PE", last + LAG)
            ve = emit_i("vector", lambda E, ec=ec, th=th: E.tensor_scalar_mul(
                qT[:, ec * 1024 + th * 512: ec * 1024 + (th + 1) * 512],
                psF[:, :], QK_SCALE), "VE")
            psF_war[0] = ("VE", ve)
            ve_q_evac[ec] = ve

        st1s = {}
        ln_dma(0)
        ln_dma(1)
        ln_dma(2)
        for gt in range(GT + 1):
            if gt >= 1:
                ln_s2(st1s[gt - 1])
            if 3 <= gt + 2 < GT:
                ln_dma(gt + 2)
            if gt < GT:
                st1s[gt] = ln_s1(gt)
            if gt >= 1:
                ln_s3(gt - 1)
                if (gt - 1) % 4 == 3:
                    ln_kv((gt - 1) // 4)
            if gt >= 9:
                q_proj_half(2 * (gt - 9), 0)
                q_proj_half(2 * (gt - 9) + 1, 0)

        ac_norm_final = ac_norm_done

        wait("scalar", "VE", st1s[14]["ve_sum"])
        emit_i("scalar", lambda E: E.dma_start(out=misc[:, 0:2048], in_=bo_d), "LYI", 16)

        probsT = pool[:, P_PT:P_PT + 4096]

        wait("tensor", "VE", ve_kT_evac)
        wait("tensor", "VE", max(ve_v_evac.values()))
        wait("tensor", "LMK", lx_masks)

        sc_banks = [psA, psB, psC, psD]
        sc_ring = [0, 1, 2, 3]
        sc_war = {0: ("VE", 0), 1: ("VE", 0), 2: ("VE", 0), 3: ("VE", 0)}
        cidx = 0
        pstW = {0: ("VE", psT_war[0]), 1: ("VE", psT_war[1])}
        av_banks = {0: psE, 1: psF}
        av_war0 = [("VE", max(ve_v_evac.values()))]

        pe_trans_slot = {}
        ve_avevac_last = 0
        ac_avevac_last = 0
        ac_exp_done = {}
        ve_partials_read = {}

        def p2_stat_mk(par, u):
            base = 24 + (par * 2 + u % 2) * 12
            def f(col):
                return stats[:, base + col: base + col + 1]
            return f, base

        probs_scaled6 = [pool[:, P_PSCALED + i * 2048: P_PSCALED + (i + 1) * 2048]
                         for i in range(6)]

        def stageA1(u, h, g):
            nonlocal cidx
            E_ext = 512 * (g + 1)
            C = g + 1
            out = dict(u=u, h=h, g=g, E_ext=E_ext, njt=E_ext // 128, deps=[],
                       exps=[])
            wait("tensor", "VE", ve_q_evac[h])
            for par, tk in enumerate((2 * g, 2 * g + 1)):
                slot = (u % 3) * 2 + par
                pslot = probs_scaled6[slot]
                ps, psbase = p2_stat_mk(par, u)
                ac_exp = 0
                slot0 = cidx % 4
                cpes = []
                for c in range(C):
                    bi = sc_ring[cidx % 4]
                    weng, wv = sc_war[bi]
                    wait("tensor", weng, wv)
                    diag = (c == C - 1)
                    def f(E, h=h, tk=tk, c=c, bank=sc_banks[bi], diag=diag):
                        return E.matmul(bank[:, :],
                                        qT[:, h * 1024 + tk * 128: h * 1024 + (tk + 1) * 128],
                                        kT[:, c * 512:(c + 1) * 512],
                                        start=True, stop=not diag)
                    if not diag:
                        cpe = emit_i("tensor", f, "PE")
                    else:
                        emit_i("tensor", f)
                        m = mask_E if (tk % 2 == 0) else mask_L
                        def fm(E, bank=sc_banks[bi], m=m):
                            return E.matmul(bank[:, :], id_sb[:, :], m,
                                            start=False, stop=True)
                        cpe = emit_i("tensor", fm, "PE")
                    cpes.append(cpe)
                    cidx += 1
                    if c == 0:
                        wait("vector", "PE", cpe + LAG)
                        wait("vector", "AC", ac_exp_done.get((par, u % 2), 0))
                        ve_negm = emit_i("vector", lambda E, ps=ps, b0=sc_banks[bi]: E.reduce_max(
                            ps(8), b0[:, 0:128], axis=AX.X, negate=True), "VE")
                runs = [(c, 1) for c in range(C)]
                for ri, (c0, rlen) in enumerate(runs):
                    wait("scalar", "PE", cpes[c0 + rlen - 1] + LAG)
                    if ri == 0:
                        wait("scalar", "VE", ve_negm)
                        wait("scalar", "PE", pe_trans_slot.get(slot, 0))
                        wait("scalar", "VE", ve_partials_read.get((par, u % 2), 0))
                    sb = (slot0 + c0) % 4
                    def fx(E, c0=c0, rlen=rlen, sb=sb, ps=ps, pslot=pslot, ri=ri):
                        return E.activation(
                            pslot[:, c0 * 512:(c0 + rlen) * 512],
                            scores4[:, sb * 512:(sb + rlen) * 512], AF.Exp,
                            bias=ps(8), accum_out=ps(ri))
                    ac_exp = emit_i("scalar", fx, "AC")
                    for k in range(rlen):
                        sc_war[(sb + k)] = ("AC", ac_exp)
                ac_exp_done[(par, u % 2)] = ac_exp
                out["exps"].append((par, slot, pslot, ps, psbase, ac_exp, len(runs)))
            return out

        def stageA2(info):
            u = info["u"]
            E_ext, C = info["E_ext"], info["g"] + 1
            for (par, slot, pslot, ps, psbase, ac_exp, nruns) in info["exps"]:
                wait("vector", "AC", ac_exp)
                if nruns > 1:
                    ve_z = emit_i("vector", lambda E, ps=ps, psbase=psbase, nruns=nruns: E.reduce_sum(
                        ps(9), stats[:, psbase: psbase + nruns], axis=AX.X), "VE")
                    zcol = 9
                    wait("vector", "VE", ve_z)
                    ve_partials_read[(par, u % 2)] = ve_z
                else:
                    zcol = 0
                ve_zrec = emit_i("vector", lambda E, ps=ps, zcol=zcol: E.reciprocal(
                    ps(10), ps(zcol)), "VE")
                if nruns == 1:
                    ve_partials_read[(par, u % 2)] = ve_zrec
                wait("vector", "VE", ve_zrec)
                ve_scale = emit_i("vector", lambda E, E_ext=E_ext, pslot=pslot, ps=ps: E.tensor_scalar_mul(
                    pslot[:, 0:E_ext], pslot[:, 0:E_ext], ps(10)), "VE")
                info["deps"].append((slot, ve_scale))

        def stageB(info):
            nonlocal ve_avevac_last, ac_avevac_last
            u, h, g, njt = info["u"], info["h"], info["g"], info["njt"]
            nbatches = (njt + 7) // 8
            ve_evac_b = []
            for b in range(nbatches):
                n_in = min(8, njt - b * 8)
                evs = 0
                for par in range(2):
                    slot, ve_scale = info["deps"][par]
                    if b == 0:
                        wait("tensor", "VE", ve_scale)
                    bank = (u + par) % 2
                    pst = psT0 if bank == 0 else psT1
                    weng, wv = pstW[bank]
                    wait("tensor", weng, wv)
                    last = None
                    for q8 in range(n_in):
                        jt = b * 8 + q8
                        def f(E, pst=pst, q8=q8, jt=jt, slot=slot):
                            return E.transpose(
                                pst[:, q8 * 128:(q8 + 1) * 128],
                                probs_scaled6[slot][:, jt * 128:(jt + 1) * 128],
                                id_sb[:, :])
                        if q8 < n_in - 1:
                            emit_i("tensor", f)
                        else:
                            last = emit_i("tensor", f, "PE")
                    pe_trans_slot[slot] = last
                    wait("vector", "PE", last + LAG)
                    dst = probsT[:, par * 2048 + b * 1024:
                                 par * 2048 + b * 1024 + n_in * 128]
                    srcp = pst[:, 0:n_in * 128]
                    ev = emit_i("vector", lambda E, d=dst, s=srcp: E.tensor_copy(d, s), "VE")
                    evs = ev
                    pstW[bank] = ("VE", ev)
                ve_evac_b.append(evs)
            bank_i = u % 2
            avp = av_banks[bank_i]
            weng, wv = av_war0[0] if bank_i == 0 else psF_war[0]
            wait("tensor", weng, wv)
            last = None
            for jt in range(njt):
                if jt % 8 == 0:
                    wait("tensor", "VE", ve_evac_b[jt // 8])
                def f(E, jt=jt, avp=avp, njt=njt):
                    rhs = probsT.rearrange("p (par c) -> p par c", par=2)[
                        :, :, jt * 128:(jt + 1) * 128]
                    return E.matmul(avp[:, 0:256],
                                    v_sb[:, jt * 128:(jt + 1) * 128],
                                    rhs,
                                    start=(jt == 0), stop=(jt == njt - 1))
                if jt < njt - 1:
                    emit_i("tensor", f)
                else:
                    last = emit_i("tensor", f, "PE")
            wait("scalar", "PE", last + LAG)
            ac_avevac_last = emit_i("scalar", lambda E, h=h, g=g, avp=avp: E.copy(
                qT[:, h * 1024 + g * 256: h * 1024 + (g + 1) * 256], avp[:, 0:256]), "AC")
            if bank_i == 0:
                av_war0[0] = ("AC", ac_avevac_last)
            else:
                psF_war[0] = ("AC", ac_avevac_last)

        PIPE_LAG = 3
        q_proj_half(0, 1)
        q_proj_half(1, 1)
        infos = []
        units = [(h, g) for h in range(H) for g in range(4)]
        for u, (h, g) in enumerate(units):
            if h + 2 < H and g == 0:
                q_proj_half(h + 2, 1)
            if u >= PIPE_LAG:
                stageB(infos[u - PIPE_LAG])
            if u >= 1:
                stageA2(infos[u - 1])
            infos.append(stageA1(u, h, g))
        stageA2(infos[-1])
        for uu in range(len(units) - PIPE_LAG, len(units)):
            stageB(infos[uu])

        pe_p2_end = cnt["PE"]
        ve_p2_end = cnt["VE"]

        wait("vector", "LYI", cnt["LYI"])
        for tt in range(NT):
            ysl = y_sb[:, tt * 2048:(tt + 1) * 2048]
            emit_i("vector", lambda E, ysl=ysl: E.tensor_add(
                ysl, ysl, misc[:, 0:2048]), "VE")
        ve_yinit_end = cnt["VE"]
        pe_wo0_ok = max(pe_trans_slot.get(4, 0), pe_trans_slot.get(5, 0),
                        max(pe_wq_done.values()))

        wo_sb = [pool[:, P_WO + i * 8192: P_WO + (i + 1) * 8192] for i in range(2)]
        wo_ps = [psA, psB]
        pe_wo_done = {}
        ve_wo_evac = {0: ve_yinit_end, 1: ve_yinit_end}
        first_wo = True
        for es in range(4):
            if es == 0:
                wait("gpsimd", "PE", pe_wo0_ok)
            elif es == 1:
                wait("gpsimd", "PE", pe_p2_end)
            else:
                wait("gpsimd", "PE", pe_wo_done[es - 2])
            lw = emit_i("gpsimd", lambda E, es=es: E.dma_start(
                out=wo_sb[es % 2], in_=wo_d[es]), "LWO0" if es % 2 == 0 else "LWO1", 16)
            wait("tensor", "LWO0" if es % 2 == 0 else "LWO1", lw)
            if first_wo:
                wait("tensor", "AC", ac_avevac_last)
                first_wo = False
            for tt in range(NT):
                ps = wo_ps[tt % 2]
                wait("tensor", "VE", ve_wo_evac[tt % 2])
                last = None
                for cc in range(16):
                    def f(E, es=es, tt=tt, cc=cc, ps=ps):
                        return E.matmul(ps[:, :],
                                        qT[:, cc * 1024 + tt * 128: cc * 1024 + (tt + 1) * 128],
                                        wo_sb[es % 2][:, cc * 512:(cc + 1) * 512],
                                        start=(cc == 0), stop=(cc == 15))
                    if cc < 15:
                        emit_i("tensor", f)
                    else:
                        last = emit_i("tensor", f, "PE")
                pe_wo_done[es] = last
                wait("vector", "PE", last + LAG)
                ysl = y_sb[:, tt * 2048 + es * 512: tt * 2048 + (es + 1) * 512]
                ve_wo_evac[tt % 2] = emit_i("vector", lambda E, ysl=ysl, ps=ps: E.tensor_add(
                    ysl, ysl, ps[:, :]), "VE")

        ve_p3_end = cnt["VE"]
        pe_p3_end = cnt["PE"]

        wg_sb = pool[:, P_WG:P_WG + 8192]
        wu_sb = pool[:, P_WU:P_WU + 8192]
        wd_sb = pool[:, P_WD:P_WD + 8192]
        hact = qT[:, 0:4096]
        gs = [qT[:, 4096 + i * 512: 4096 + (i + 1) * 512] for i in range(2)]

        pe_gate_done = {}
        pe_down_done = {}
        ve_down_evac = {0: ve_p3_end, 1: ve_p3_end}
        ve_final_tt = {}
        ac_silu_prev = {0: 0, 1: 0}
        ve_hact = {}
        GU_banks = [(psA, psB), (psC, psD)]
        D_banks2 = {0: (psE, psF), 1: (psA, psB)}
        first_gate = True
        first_down = True

        for s in range(16):
            if s == 0:
                wait("gpsimd", "PE", pe_p2_end)
            else:
                wait("gpsimd", "PE", pe_gate_done[s - 1])
            lw_gs = emit_i("gpsimd", lambda E, s=s: E.dma_start(out=wg_sb, in_=wg_d[s]), "LWG", 16)
            if s == 0:
                wait("gpsimd", "PE", pe_wo_done[2])
            lw_us = emit_i("gpsimd", lambda E, s=s: E.dma_start(out=wu_sb, in_=wu_d[s]), "LWU", 16)
            if s == 0:
                wait("gpsimd", "PE", pe_p3_end)
            else:
                wait("gpsimd", "PE", pe_down_done[s - 1])
            lw_ds = emit_i("gpsimd", lambda E, s=s: E.dma_start(out=wd_sb, in_=wd_d[s]), "LWD", 16)

            wait("tensor", "LWG", lw_gs)
            wait("tensor", "LWU", lw_us)
            for hq in range(4):
                for th in range(2):
                    pair = hq * 2 + th
                    gb, ub = GU_banks[pair % 2]
                    if first_gate:
                        wait("tensor", "VE", ve_p3_end)
                        first_gate = False
                    if pair < 2 and s > 0:
                        wait("tensor", "VE", ve_down_evac[1] if pair == 0
                             else ve_hact.get((s - 1, 7), 0))
                    wait("tensor", "AC", ac_silu_prev[pair % 2])
                    wait("tensor", "VE", ve_hact.get((s, pair - 2),
                                                     ve_hact.get((s - 1, 6 + pair), 0)))
                    lastg = None
                    for dc in range(DC):
                        def f(E, s=s, hq=hq, th=th, dc=dc, gb=gb):
                            return E.matmul(gb[:, :],
                                            wg_sb[:, dc * 512 + hq * 128: dc * 512 + (hq + 1) * 128],
                                            xnT[:, dc * 1024 + th * 512: dc * 1024 + (th + 1) * 512],
                                            start=(dc == 0), stop=(dc == DC - 1))
                        if dc < DC - 1:
                            emit_i("tensor", f)
                        else:
                            lastg = emit_i("tensor", f, "PE")
                    lastu = None
                    for dc in range(DC):
                        def f(E, s=s, hq=hq, th=th, dc=dc, ub=ub):
                            return E.matmul(ub[:, :],
                                            wu_sb[:, dc * 512 + hq * 128: dc * 512 + (hq + 1) * 128],
                                            xnT[:, dc * 1024 + th * 512: dc * 1024 + (th + 1) * 512],
                                            start=(dc == 0), stop=(dc == DC - 1))
                        if dc < DC - 1:
                            emit_i("tensor", f)
                        else:
                            lastu = emit_i("tensor", f, "PE")
                    pe_gate_done[s] = lastu
                    wait("scalar", "PE", lastg + LAG)
                    ac_silu = emit_i("scalar", lambda E, pair=pair, gb=gb: E.activation(
                        gs[pair % 2], gb[:, :], AF.Silu), "AC")
                    ac_silu_prev[pair % 2] = ac_silu
                    wait("vector", "AC", ac_silu)
                    wait("vector", "PE", lastu + LAG)
                    dsth = hact[:, hq * 1024 + th * 512: hq * 1024 + (th + 1) * 512]
                    ve_hact[(s, pair)] = emit_i("vector", lambda E, d=dsth, pair=pair, ub=ub: E.tensor_mul(
                        d, gs[pair % 2], ub[:, :]), "VE")

            wait("tensor", "LWD", lw_ds)
            wait("tensor", "VE", ve_hact[(s, 7)])
            for tt in range(NT):
                for e2 in range(2):
                    b0, b1 = D_banks2[e2]
                    if first_down:
                        wait("tensor", "VE", ve_p3_end)
                        first_down = False
                    if tt == 0 and e2 == 1:
                        wait("tensor", "VE", ve_hact[(s, 7)])
                        wait("tensor", "AC", ac_silu_prev[0])
                    wait("tensor", "VE", ve_down_evac[e2])
                    last = None
                    for hq in range(4):
                        for ei, bank in ((0, b0), (1, b1)):
                            es = e2 * 2 + ei
                            def f(E, s=s, tt=tt, hq=hq, es=es, bank=bank):
                                return E.matmul(bank[:, :],
                                                hact[:, hq * 1024 + tt * 128: hq * 1024 + (tt + 1) * 128],
                                                wd_sb[:, hq * 2048 + es * 512: hq * 2048 + (es + 1) * 512],
                                                start=(hq == 0), stop=(hq == 3))
                            if hq == 3 and ei == 1:
                                last = emit_i("tensor", f, "PE")
                            else:
                                emit_i("tensor", f)
                    pe_down_done[s] = last
                    wait("vector", "PE", last + LAG)
                    for ei, bank in ((0, b0), (1, b1)):
                        es = e2 * 2 + ei
                        ysl = y_sb[:, tt * 2048 + es * 512: tt * 2048 + (es + 1) * 512]
                        ve_down_evac[e2] = emit_i(
                            "vector",
                            lambda E, ysl=ysl, bank=bank: E.tensor_add(ysl, ysl, bank[:, :]),
                            "VE")
                        if s == 15:
                            ve_final_tt[tt] = ve_down_evac[e2]

        wait("tensor", "VE", cnt["VE"])
        wait("tensor", "AC", cnt["AC"])
        for _ in range(LAG + 1):
            emit_i("tensor", lambda E: E.matmul(
                psA[:, 0:128], id_sb[:, :], id_sb[:, :], start=True, stop=True), "PE")

        for tt in range(NT):
            wait("sync", "VE", ve_final_tt[tt])
            emit_i("sync", lambda E, tt=tt: E.dma_start(
                out=y_d[tt * 128:(tt + 1) * 128, :],
                in_=y_sb[:, tt * 2048:(tt + 1) * 2048]), "SO", 16)
        wait("sync", "SO", cnt["SO"])

        with nc.Block() as block:
            @block.sync
            def _(E):
                for f in prog["sync"]:
                    f(E)

            @block.gpsimd
            def _(E):
                for f in prog["gpsimd"]:
                    f(E)

            @block.tensor
            def _(E):
                for f in prog["tensor"]:
                    f(E)

            @block.scalar
            def _(E):
                for f in prog["scalar"]:
                    f(E)

            @block.vector
            def _(E):
                for f in prog["vector"]:
                    f(E)

    return nc


def _host_layouts(Wq, Wk, Wv, Wo, Wg, Wu, Wd, bo):
    wq_l = np.ascontiguousarray(
        Wq.reshape(16, 128, 16, 128).transpose(2, 1, 0, 3).reshape(16, 128, 2048))
    wk_l = np.ascontiguousarray(Wk.reshape(16, 128, 128).transpose(1, 0, 2).reshape(128, 2048))
    wv_l = np.ascontiguousarray(Wv.reshape(16, 128, 128).transpose(1, 0, 2).reshape(128, 2048))
    wo_l = np.ascontiguousarray(
        Wo.reshape(16, 128, 4, 512).transpose(2, 1, 0, 3).reshape(4, 128, 8192))
    WgT = np.ascontiguousarray(Wg.T)
    WuT = np.ascontiguousarray(Wu.T)
    WdT = np.ascontiguousarray(Wd.T)
    wg_l = np.ascontiguousarray(
        WgT.reshape(16, 128, 16, 512).transpose(2, 1, 0, 3).reshape(16, 128, 8192))
    wu_l = np.ascontiguousarray(
        WuT.reshape(16, 128, 16, 512).transpose(2, 1, 0, 3).reshape(16, 128, 8192))
    wd_l = np.ascontiguousarray(
        WdT.reshape(16, 4, 128, 4, 512).transpose(0, 2, 1, 3, 4).reshape(16, 128, 8192))
    bo_bc = np.ascontiguousarray(np.broadcast_to(bo.reshape(1, -1), (128, 2048)))
    return dict(wq_l=wq_l, wk_l=wk_l, wv_l=wv_l, wo_l=wo_l,
                wg_l=wg_l, wu_l=wu_l, wd_l=wd_l, bo_bc=bo_bc)


def _masks(parity):
    i = np.arange(128)[:, None]
    jc = np.arange(512)[None, :]
    if parity == 0:
        key_true = jc
    else:
        key_true = np.where(jc % 2 == 0, jc + 1, jc - 1)
    mE = np.where(key_true <= 2 * i + parity, 0.0, NEG).astype(np.float32)
    mL = np.where(key_true <= 256 + 2 * i + parity, 0.0, NEG).astype(np.float32)
    return np.stack([mE, mL]).astype(ml_dtypes.bfloat16)


def kernel(L0_attn_x, L0_attn_Wq, L0_attn_Wk, L0_attn_Wv, L0_attn_Wo, L0_attn_bo,
           L0_ffn_w_gate, L0_ffn_w_up, L0_ffn_w_down):
    x = np.ascontiguousarray(np.asarray(L0_attn_x, dtype=np.float32))
    shared = _host_layouts(
        np.asarray(L0_attn_Wq, np.float32), np.asarray(L0_attn_Wk, np.float32),
        np.asarray(L0_attn_Wv, np.float32), np.asarray(L0_attn_Wo, np.float32),
        np.asarray(L0_ffn_w_gate, np.float32), np.asarray(L0_ffn_w_up, np.float32),
        np.asarray(L0_ffn_w_down, np.float32), np.asarray(L0_attn_bo, np.float32))
    ident = np.eye(128, dtype=ml_dtypes.bfloat16)
    masks = [_masks(0), _masks(1)]

    nc = build_nc()
    in_maps = []
    for core in range(8):
        b, p = core // 2, core % 2
        xb = x[b]
        xperm = np.ascontiguousarray(
            xb.reshape(1024, 2, 2048)[:, [p, 1 - p], :].reshape(2048, 2048))
        in_map = dict(shared)
        in_map["x"] = xperm
        in_map["x_own"] = np.ascontiguousarray(xb[p::2, :])
        in_map["masks"] = masks[p]
        in_map["ident"] = ident
        in_maps.append(in_map)

    res = run_bass_kernel_spmd(nc, in_maps, core_ids=list(range(8)))
    out = np.empty((B, T, D), dtype=np.float32)
    for core in range(8):
        b, p = core // 2, core % 2
        out[b, p::2, :] = res.results[core]["y"]
    return out
